# version 10
# baseline (speedup 1.0000x reference)
import math
from contextlib import ExitStack

import numpy as np
import ml_dtypes

import concourse.bass as bass
import concourse.mybir as mybir
from concourse.bass_utils import run_bass_kernel_spmd
from concourse.alu_op_type import AluOpType

F32 = mybir.dt.float32
BF16 = mybir.dt.bfloat16
I32 = mybir.dt.int32
AF = mybir.ActivationFunctionType
AX = mybir.AxisListType
Op = AluOpType

D = 4096
T = 4096
HL = 16
KVL = 4
HD = 128
DQ = HL * HD
DKV = KVL * HD
TC = 1024
NBLK = T // TC
NJ = T // 512
SCALE = 1.0 / math.sqrt(HD)
EPS = 1e-5
TEMP_INV = 10.0
THRESH = 0.01
TOWN = 2048

DEBUG_DUMPS = []
UPTO = 99


class KB:
    ENGS = ("tensor", "vector", "scalar", "gpsimd", "sync")

    def __init__(self, nc, stack):
        self.nc = nc
        self.stack = stack
        self.streams = {e: [] for e in self.ENGS}
        self.sems = {}

    def sem(self, name):
        if name not in self.sems:
            h = self.stack.enter_context(self.nc.semaphore(name))
            self.sems[name] = [h, 0]
        return name

    def val(self, name):
        return self.sems[name][1]

    def em(self, eng, fn):
        self.streams[eng].append(fn)

    def w(self, eng, sname, v=None):
        if v is None:
            v = self.sems[sname][1]
        if v <= 0:
            return
        h = self.sems[sname][0]
        self.em(eng, lambda e, h=h, v=v: e.wait_ge(h, v))

    def inc(self, sname, n=1):
        s = self.sems[sname]
        s[1] += n
        return s[1]

    def run(self, eng, fn, inc=None, inc_n=1):
        if inc is not None:
            h = self.sems[inc][0]
            self.inc(inc, inc_n)
            self.em(eng, lambda e, fn=fn, h=h, n=inc_n: fn(e).then_inc(h, n))
        else:
            self.em(eng, lambda e, fn=fn: fn(e))

    def drain_inc(self, eng, sname, n=1):
        h = self.sems[sname][0]
        self.inc(sname, n)
        self.em(eng, lambda e, h=h, n=n: e.drain().then_inc(h, n))

    def drain(self, eng):
        self.em(eng, lambda e: e.drain())

    def dma(self, qeng, out_ap, in_ap, sname, slow=False):
        h = self.sems[sname][0]
        self.inc(sname, 16)
        if slow:
            self.em(qeng, lambda e, o=out_ap, i=in_ap, h=h: e.dma_start(
                out=o, in_=i, allow_slow_non_contiguous=True).then_inc(h, 16))
        else:
            self.em(qeng, lambda e, o=out_ap, i=in_ap, h=h: e.dma_start(
                out=o, in_=i).then_inc(h, 16))

    def barrier(self, tag):
        for eng in ("tensor", "vector", "scalar", "gpsimd"):
            self.drain_inc(eng, self.sem("bar_" + eng))
        self.run("sync", lambda e: e.nop(), inc=self.sem("bar_sync"))
        snapshot = [(n, s[1]) for n, s in self.sems.items()]
        for eng in self.ENGS:
            for n, v in snapshot:
                if v > 0:
                    h = self.sems[n][0]
                    self.em(eng, lambda e, h=h, v=v: e.wait_ge(h, v))

    def finalize(self, block):
        for eng in self.ENGS:
            fns = self.streams[eng]
            dec = getattr(block, eng)

            def section(e, fns=fns):
                for f in fns:
                    f(e)
            dec(section)


def _rope_tables():
    base = 500000.0
    inv_freq = 1.0 / base ** (np.arange(0, HD, 2, dtype=np.float32) / HD)
    factor, low_f, high_f, old_ctx = 32.0, 1.0, 4.0, 8192
    low_wl, high_wl = old_ctx / low_f, old_ctx / high_f
    wavelen = 2 * math.pi / inv_freq
    inv_llama = np.where(wavelen > low_wl, inv_freq / factor, inv_freq)
    smooth = (old_ctx / wavelen - low_f) / (high_f - low_f)
    smoothed = (1 - smooth) * inv_llama / factor + smooth * inv_llama
    is_med = ~(wavelen < high_wl) & ~(wavelen > low_wl)
    inv = np.where(is_med, smoothed, inv_llama).astype(np.float32)
    pos = np.arange(T, dtype=np.float32)
    freqs = pos[:, None] * inv[None, :]
    emb = np.concatenate([freqs, freqs], axis=-1)
    cosT = np.ascontiguousarray(np.cos(emb).astype(np.float32).T)
    sinT = np.ascontiguousarray(np.sin(emb).astype(np.float32).T)
    return cosT, sinT



def _cdma(k, qeng, sbuf_ap, dram2d, C, sem):
    in3 = dram2d.rearrange("(c p) n -> p c n", p=128)
    n = in3.shape[2]
    out3 = sbuf_ap.rearrange("p (c n) -> p c n", c=C)
    k.dma(qeng, out3, in3, sem)


def _cdma_out(k, qeng, dram2d, sbuf_ap, C, sem):
    out3 = dram2d.rearrange("(c p) n -> p c n", p=128)
    in3 = sbuf_ap.rearrange("p (c n) -> p c n", c=C)
    k.dma(qeng, out3, in3, sem)


def build_nc(dumps=(), upto=99):
    nc = bass.Bass("TRN2", target_bir_lowering=False, debug=False)

    def din(name, shape, dt):
        return nc.dram_tensor(name, shape, dt, kind="ExternalInput").ap()

    dump_list = []

    def dint(name, shape, dt):
        if name in ("wo_part", "rs_out", "cnt_in", "cnt_out"):
            t = nc.dram_tensor(name, shape, dt, kind="Internal").ap()
            if name in dumps:
                dump_list.append((name, t, shape, dt))
            return t
        kind = "ExternalOutput" if name in dumps else "Internal"
        return nc.dram_tensor(name, shape, dt, kind=kind).ap()

    x_seq = din("x_seq", [T, D], F32)
    x_own = din("x_own", [TOWN, D], F32)
    wq = din("wq", [D, DQ], F32)
    wk = din("wk", [D, DKV], F32)
    wv = din("wv", [D, DKV], F32)
    wo = din("wo", [DQ, D], F32)
    wf = din("wf", [D, D], F32)
    bias_rep = din("bias_rep", [128, D], F32)
    cent = din("cent", [D, 64], F32)
    assign = din("assign", [1, D], I32)
    cosT = din("cosT", [128, T], F32)
    sinT = din("sinT", [128, T], F32)
    iden = din("iden", [128, 128], BF16)
    ones = din("ones", [128, 128], BF16)
    onescol = din("onescol", [128, 1], BF16)
    onesk1 = din("onesk1", [1, 128], BF16)
    iota64 = din("iota64", [64, 1], F32)
    masks = din("masks", [128, 4 * 512], BF16)
    epsv = din("epsv", [128, 1], F32)

    cc_names = ("wo_part", "rs_out", "cnt_in", "cnt_out")
    q_sp = dint("q_sp", [DQ, T], BF16)
    k_sp = dint("k_sp", [DKV, T], BF16)
    v_sp = dint("v_sp", [T, DKV], BF16)
    wo_part = dint("wo_part", [8, 2, 2, 1024, 512], F32)
    rs_out = dint("rs_out", [8, 2, 1024, 512], F32)
    wf_bf = dint("wf_bf", [D, D], BF16)
    wo_bf = dint("wo_bf", [DQ, D], BF16)
    xres_sp = dint("xres_sp", [TOWN, D], F32)
    h2_sp = dint("h2_sp", [D, TOWN], BF16)
    cnt_in = dint("cnt_in", [1, 64], F32)
    cnt_out = dint("cnt_out", [1, 64], F32)
    out = nc.dram_tensor("out", [TOWN, D], F32, kind="ExternalOutput").ap()

    with ExitStack() as stack:
        E = stack.enter_context
        k = KB(nc, stack)
        c_iden = E(nc.sbuf_tensor("c_iden", [128, 128], BF16))
        c_ones = E(nc.sbuf_tensor("c_ones", [128, 128], BF16))
        c_onescol = E(nc.sbuf_tensor("c_onescol", [128, 1], BF16))
        c_onesk1 = E(nc.sbuf_tensor("c_onesk1", [1, 128], BF16))
        c_iota = E(nc.sbuf_tensor("c_iota", [64, 1], F32))
        c_masks = E(nc.sbuf_tensor("c_masks", [128, 4 * 512], BF16))
        c_cent = E(nc.sbuf_tensor("c_cent", [128, 32 * 64], BF16))
        c_qsel = E(nc.sbuf_tensor("c_qsel", [128, 16], F32))
        c_eps = E(nc.sbuf_tensor("c_eps", [128, 1], F32))

        ld_init = k.sem("ld_init")
        k.dma("sync", c_iden[:, :], iden, ld_init)
        k.dma("sync", c_ones[:, :], ones, ld_init)
        k.dma("sync", c_onescol[:, :], onescol, ld_init)
        k.dma("sync", c_onesk1[:, :], onesk1, ld_init)
        k.dma("sync", c_iota[:, :], iota64, ld_init)
        k.dma("sync", c_masks[:, :], masks, ld_init)
        k.dma("sync", c_eps[:, :], epsv, ld_init)
        _cdma(k, "gpsimd", c_cent[:, :], cent, 32, ld_init)
        for e in ("tensor", "vector", "scalar"):
            k.w(e, ld_init)

        dv = k.sem("dv")
        ac = k.sem("ac")
        pe = k.sem("pe")
        cc_s = k.sem("cc_s")

        def zero_out_and_finish():
            zbuf = E(nc.sbuf_tensor("zbuf", [128, D], F32))
            zs = k.sem("z_w")
            k.run("vector", lambda e: e.memset(zbuf[:, :], 0.0))
            k.drain_inc("vector", dv)
            k.w("sync", dv)
            for ti in range(16):
                k.dma("sync", out[ti * 128:(ti + 1) * 128, :], zbuf[:, :], zs)
            k.barrier("zend")
            block = E(nc.Block())
            k.finalize(block)
            return nc

        with ExitStack() as p1:
            E1 = p1.enter_context
            h_fm = E1(nc.sbuf_tensor("h_fm", [128, 32 * TC], BF16))
            x_in = [E1(nc.sbuf_tensor(f"x_in{i}", [128, D], F32)) for i in range(2)]
            h_tm = [E1(nc.sbuf_tensor(f"h_tm{i}", [128, D], BF16)) for i in range(2)]
            wq_sl = [E1(nc.sbuf_tensor(f"wq_sl{i}", [128, 32 * 256], BF16)) for i in range(2)]
            wv_sl = E1(nc.sbuf_tensor("wv_sl", [128, 32 * 512], BF16))
            cs_sl = [E1(nc.sbuf_tensor(f"cs_sl{i}", [128, 2 * 512], F32)) for i in range(2)]
            r1 = E1(nc.sbuf_tensor("r1", [128, 512], F32))
            rAB = E1(nc.sbuf_tensor("rAB", [128, 512], F32))
            qko = [E1(nc.sbuf_tensor(f"qko{i}", [128, 512], BF16)) for i in range(2)]
            vst = [E1(nc.sbuf_tensor(f"vst{i}", [128, 512], BF16)) for i in range(2)]
            statb = [E1(nc.sbuf_tensor(f"statb{i}", [128, 8], F32)) for i in range(2)]
            p_a = E1(nc.psum_tensor("p_a", [128, 512], F32))
            p_b = E1(nc.psum_tensor("p_b", [128, 512], F32))
            p_x = E1(nc.psum_tensor("p_x", [128, 512], BF16))
            p_mm = [p_a, p_b]

            s_x = [k.sem(f"s_x{i}") for i in range(2)]
            s_wq = [k.sem(f"s_wq{i}") for i in range(2)]
            s_wv = k.sem("s_wv")
            s_cs = [k.sem(f"s_cs{i}") for i in range(2)]
            stqk = [k.sem(f"stqk{i}") for i in range(2)]
            v_wsem = [k.sem(f"v_w{i}") for i in range(2)]

            xin_free = [(0, 0), (0, 0)]
            htm_free = [0, 0]
            pmm_free = [0, 0]
            wq_free = [0, 0]
            evac_cnt = [0]

            def ln_tile(blk, ti):
                t = blk * 8 + ti
                sl = t % 2
                dvw, acw = xin_free[sl]
                k.w("sync", dv, dvw)
                k.w("sync", ac, acw)
                k.dma("sync", x_in[sl][:, :], x_seq[t * 128:(t + 1) * 128, :], s_x[sl])
                k.w("vector", s_x[sl])
                k.w("scalar", s_x[sl])
                k.w("scalar", pe, htm_free[sl])
                k.run("vector", lambda e, sl=sl: e.reduce_sum(
                    out=statb[sl][:, 0:1], in_=x_in[sl][:, :], axis=AX.X))
                k.run("scalar", lambda e, sl=sl: e.activation(
                    out=h_tm[sl][:, :], in_=x_in[sl][:, :], func=AF.Square,
                    accum_out=statb[sl][:, 2:3]))
                k.drain_inc("scalar", ac)
                k.w("vector", ac)
                k.drain("vector")
                k.run("vector", lambda e: e.tensor_scalar(
                    out=statb[sl][:, 1:2], in0=statb[sl][:, 0:1], scalar1=1.0 / D, scalar2=None,
                    op0=Op.mult, op1=Op.bypass))
                k.drain("vector")
                k.run("vector", lambda e: e.tensor_tensor(
                    out=statb[sl][:, 3:4], in0=statb[sl][:, 1:2], in1=statb[sl][:, 1:2], op=Op.mult))
                k.drain("vector")
                k.run("vector", lambda e: e.tensor_scalar(
                    out=statb[sl][:, 4:5], in0=statb[sl][:, 2:3], scalar1=1.0 / D, scalar2=statb[sl][:, 3:4],
                    op0=Op.mult, op1=Op.subtract))
                k.drain_inc("vector", dv)
                k.w("scalar", dv)
                k.run("scalar", lambda e: e.activation(
                    out=statb[sl][:, 5:6], in_=statb[sl][:, 4:5], func=AF.Ln, bias=c_eps[:, :]))
                k.run("scalar", lambda e: e.activation(
                    out=statb[sl][:, 6:7], in_=statb[sl][:, 5:6], func=AF.Exp, scale=-0.5))
                k.drain_inc("scalar", ac)
                k.w("vector", ac)
                k.run("vector", lambda e, sl=sl: e.tensor_scalar(
                    out=h_tm[sl][:, :], in0=x_in[sl][:, :],
                    scalar1=statb[sl][:, 1:2], scalar2=statb[sl][:, 6:7],
                    op0=Op.subtract, op1=Op.mult))
                k.drain_inc("vector", dv)
                xin_free[sl] = (k.val(dv), k.val(ac))
                k.w("tensor", dv)
                for grp in range(8):
                    for u in range(4):
                        ch = grp * 4 + u
                        k.run("tensor", lambda e, ch=ch, u=u, sl=sl: e.transpose(
                            p_x[:, u * 128:(u + 1) * 128],
                            h_tm[sl][:, ch * 128:(ch + 1) * 128], c_iden[:, :]),
                            inc=pe if u == 3 else None)
                    k.w("vector", pe)
                    for u in range(4):
                        ch = grp * 4 + u
                        k.run("vector", lambda e, ch=ch, u=u, ti=ti: e.tensor_copy(
                            h_fm[:, ch * TC + ti * 128:ch * TC + (ti + 1) * 128],
                            p_x[:, u * 128:(u + 1) * 128]))
                    k.drain_inc("vector", dv)
                    if grp < 7:
                        k.w("tensor", dv)
                htm_free[sl] = k.val(pe)

            def proj_and_rope(slab, slab_cols, fout_off, t512_abs, pp_i, dst, row_off, st_sl):
                pp = p_mm[pp_i]
                blk_t512 = t512_abs % 2
                k.w("tensor", dv, pmm_free[pp_i])
                for fi in range(32):
                    k.run("tensor", lambda e, fi=fi, pp=pp, slab=slab, slab_cols=slab_cols,
                          fout_off=fout_off, blk_t512=blk_t512: e.matmul(
                        pp[:, :],
                        slab[:, fi * slab_cols + fout_off:fi * slab_cols + fout_off + 128],
                        h_fm[:, fi * TC + blk_t512 * 512:fi * TC + blk_t512 * 512 + 512],
                        start=(fi == 0), stop=(fi == 31)),
                        inc=pe if fi == 31 else None)
                csl = cs_sl[t512_abs % 2]
                ob = qko[st_sl]
                k.w("vector", pe)
                k.w("vector", s_cs[t512_abs % 2])
                k.w("vector", stqk[st_sl])
                k.run("vector", lambda e, pp=pp, csl=csl: e.tensor_tensor(
                    out=r1[:, :], in0=pp[:, :], in1=csl[:, 0:512], op=Op.mult))
                k.run("vector", lambda e, pp=pp, csl=csl: e.tensor_tensor(
                    out=rAB[0:64, :], in0=pp[64:128, :], in1=csl[0:64, 512:1024], op=Op.mult))
                k.run("vector", lambda e, pp=pp, csl=csl: e.tensor_tensor(
                    out=rAB[64:128, :], in0=pp[0:64, :], in1=csl[64:128, 512:1024], op=Op.mult))
                k.drain("vector")
                k.run("vector", lambda e, ob=ob: e.tensor_tensor(
                    out=ob[0:64, :], in0=r1[0:64, :], in1=rAB[0:64, :], op=Op.subtract))
                k.run("vector", lambda e, ob=ob: e.tensor_tensor(
                    out=ob[64:128, :], in0=r1[64:128, :], in1=rAB[64:128, :], op=Op.add))
                k.drain_inc("vector", dv)
                pmm_free[pp_i] = k.val(dv)
                k.w("scalar", dv)
                k.dma("scalar", dst[row_off:row_off + 128,
                                    t512_abs * 512:(t512_abs + 1) * 512], ob[:, :], stqk[st_sl])

            if upto >= 1:
                _cdma(k, "gpsimd", wv_sl[:, :], wv, 32, s_wv)
                for blk in range(NBLK):
                    for ti in range(8):
                        ln_tile(blk, ti)
                    for t5 in range(2):
                        t512_abs = blk * 2 + t5
                        sl = t512_abs % 2
                        if t512_abs >= 2:
                            k.w("sync", dv)
                        k.dma("sync", cs_sl[sl][:, 0:512],
                              cosT[:, t512_abs * 512:(t512_abs + 1) * 512], s_cs[sl])
                        k.dma("sync", cs_sl[sl][:, 512:1024],
                              sinT[:, t512_abs * 512:(t512_abs + 1) * 512], s_cs[sl])
                    for sq in range(10):
                        sl = sq % 2
                        wsrc = wq if sq < 8 else wk
                        woff = sq * 256 if sq < 8 else (sq - 8) * 256
                        k.w("gpsimd", pe, wq_free[sl])
                        _cdma(k, "gpsimd", wq_sl[sl][:, :],
                              wsrc[:, woff:woff + 256], 32, s_wq[sl])
                        k.w("tensor", s_wq[sl])
                        for hh in range(2):
                            if sq < 8:
                                dst, row = q_sp, (sq * 2 + hh) * 128
                            else:
                                dst, row = k_sp, ((sq - 8) * 2 + hh) * 128
                            for t5 in range(2):
                                t512_abs = blk * 2 + t5
                                i = evac_cnt[0]
                                proj_and_rope(wq_sl[sl], 256, hh * 128, t512_abs,
                                              i % 2, dst, row, i % 2)
                                evac_cnt[0] += 1
                        wq_free[sl] = k.val(pe)
                    k.w("tensor", s_wv)
                    for ti in range(8):
                        t = blk * 8 + ti
                        ppi = ti % 2
                        pp = p_mm[ppi]
                        k.w("tensor", dv, pmm_free[ppi])
                        for fi in range(32):
                            k.run("tensor", lambda e, fi=fi, ti=ti, pp=pp: e.matmul(
                                pp[:, :],
                                h_fm[:, fi * TC + ti * 128:fi * TC + (ti + 1) * 128],
                                wv_sl[:, fi * 512:(fi + 1) * 512],
                                start=(fi == 0), stop=(fi == 31)),
                                inc=pe if fi == 31 else None)
                        k.w("vector", pe)
                        k.w("vector", v_wsem[ti % 2])
                        k.run("vector", lambda e, ti=ti, pp=pp: e.tensor_copy(
                            vst[ti % 2][:, :], pp[:, :]))
                        k.drain_inc("vector", dv)
                        pmm_free[ppi] = k.val(dv)
                        k.w("scalar", dv)
                        k.dma("scalar", v_sp[t * 128:(t + 1) * 128, :], vst[ti % 2][:, :],
                              v_wsem[ti % 2])
            k.barrier("p1_end")

        if upto < 2:
            return zero_out_and_finish()

        with ExitStack() as p2:
            E2 = p2.enter_context
            K_res = E2(nc.sbuf_tensor("K_res", [128, 4 * T], BF16))
            V_res = E2(nc.sbuf_tensor("V_res", [128, 32 * DKV], BF16))
            O_sb = E2(nc.sbuf_tensor("O_sb", [128, 64 * 512], BF16))
            q_t = [E2(nc.sbuf_tensor(f"q_t{i}", [128, 4 * 512], BF16)) for i in range(2)]
            att2 = [E2(nc.sbuf_tensor(f"att2_{i}", [128, 1024], BF16)) for i in range(3)]
            s_lnb = [E2(nc.sbuf_tensor(f"s_lnb{i}", [128, 512], F32)) for i in range(2)]
            s_rec = [E2(nc.sbuf_tensor(f"s_rec{i}", [128, 512], F32)) for i in range(2)]
            wo_sl = [E2(nc.sbuf_tensor(f"wo_sl{i}", [128, 16 * 512], BF16)) for i in range(2)]
            wo_st = [E2(nc.sbuf_tensor(f"wo_st{i}", [128, 512], F32)) for i in range(2)]
            p_s2 = [E2(nc.psum_tensor(f"p_s2_{i}", [128, 1024], F32)) for i in range(2)]
            p_o = [E2(nc.psum_tensor(f"p_o{i}", [128, 512], F32)) for i in range(2)]
            p_d = [E2(nc.psum_tensor(f"p_d{i}", [128, 512], F32)) for i in range(2)]

            s_kv = k.sem("s_kv")
            s_q = [k.sem(f"s_q{i}") for i in range(2)]
            s_wo = [k.sem(f"s_wo{i}") for i in range(2)]
            wop_w = [k.sem(f"wop_w{i}") for i in range(2)]

            k.w("sync", "stqk0"); k.w("sync", "stqk1")
            k.w("sync", "v_w0"); k.w("sync", "v_w1")
            for g in range(KVL):
                k.dma("sync", K_res[:, g * T:(g + 1) * T], k_sp[g * 128:(g + 1) * 128, :], s_kv)
            _cdma(k, "sync", V_res[:, :], v_sp, 32, s_kv)
            k.w("tensor", s_kv)

            att_last = [0, 0, 0]
            ps_last = [0, 0]
            po_free = [0, 0]
            pd_free = [0, 0]
            qt_free = [0, 0]
            osb_dv = {}
            wost_free = [0, 0]

            s_q_mark = {}

            def load_group(j, g, gidx):
                qsl = gidx % 2
                k.w("sync", pe, qt_free[qsl])
                _cdma(k, "sync", q_t[qsl][:, :],
                      q_sp[g * 512:(g + 1) * 512, j * 512:(j + 1) * 512], 4, s_q[qsl])
                s_q_mark[gidx] = k.val(s_q[qsl])

            class APipe:
                def __init__(self):
                    self.pendq = []
                    self.exp_inc = []
                    self.tail_ready = {}
                    self.tail_acv = 0
                    self.gidx = 0
                    self.loaded = 0
                    self.groups = []
                    self.pi = 0

                def ensure_loads(self, upto_idx):
                    while self.loaded <= upto_idx:
                        j2, g2 = self.groups[self.loaded]
                        load_group(j2, g2, self.loaded)
                        self.loaded += 1

            ap = APipe()

            def finalize_head(rec):
                h = rec["h"]; jj = rec["j"] % 4
                po = p_o[h % 2]; pd = p_d[h % 2]
                k.w("scalar", pe)
                k.run("scalar", lambda e, pd=pd: e.activation(
                    out=s_lnb[h % 2][:, :], in_=pd[:, :], func=AF.Ln))
                k.run("scalar", lambda e, h=h: e.activation(
                    out=s_rec[h % 2][:, :], in_=s_lnb[h % 2][:, :], func=AF.Exp, scale=-1.0))
                k.drain_inc("scalar", ac)
                k.w("vector", ac)
                slot = (jj * 16 + h) * 512
                k.run("vector", lambda e, po=po, h=h, slot=slot: e.tensor_tensor(
                    out=O_sb[:, slot:slot + 512], in0=po[:, :], in1=s_rec[h % 2][:, :],
                    op=Op.mult))
                k.drain_inc("vector", dv)
                po_free[h % 2] = k.val(dv)
                pd_free[h % 2] = k.val(dv)
                osb_dv[(jj, h)] = k.val(dv)

            def att_ready_of(rec):
                pi = rec["pi"]
                if pi in ap.tail_ready:
                    return ap.tail_ready[pi]
                return (ac, ap.exp_inc[pi + 1])

            def emit_masks(rec):
                si3 = rec["si3"]
                k.w("vector", ac, ap.exp_inc[rec["pi"] + 1] if rec["pi"] + 1 < len(ap.exp_inc)
                    else ap.tail_acv)
                for u in (0, 1):
                    mi = 2 * rec["gp"] + u - 4 * rec["j"]
                    k.run("vector", lambda e, si3=si3, u=u, mi=mi: e.tensor_tensor(
                        out=att2[si3][:, u * 512:(u + 1) * 512],
                        in0=att2[si3][:, u * 512:(u + 1) * 512],
                        in1=c_masks[:, mi * 512:(mi + 1) * 512], op=Op.mult))
                k.drain_inc("vector", dv)
                ap.tail_ready[rec["pi"]] = (dv, k.val(dv))

            def emit_pv_den(rec):
                si3 = rec["si3"]
                if rec["diag"] and rec["pi"] not in ap.tail_ready:
                    emit_masks(rec)
                sem2, v2 = att_ready_of(rec)
                k.w("tensor", sem2, v2)
                h = rec["h"]; g = rec["g"]; nkt = rec["nkt"]
                po = p_o[h % 2]; pd = p_d[h % 2]
                for u in (0, 1):
                    kt = 2 * rec["gp"] + u
                    if kt == 0:
                        k.w("tensor", dv, po_free[h % 2])
                        k.w("tensor", dv, pd_free[h % 2])
                    k.run("tensor", lambda e, kt=kt, u=u, si3=si3, po=po, g=g, nkt=nkt: e.matmul(
                        po[:, :],
                        V_res[:, kt * DKV + g * 128:kt * DKV + (g + 1) * 128],
                        att2[si3][:, u * 512:(u + 1) * 512],
                        start=(kt == 0), stop=(kt == nkt - 1)), inc=pe)
                    k.run("tensor", lambda e, kt=kt, u=u, si3=si3, pd=pd, nkt=nkt: e.matmul(
                        pd[:, :], c_ones[:, :],
                        att2[si3][:, u * 512:(u + 1) * 512],
                        start=(kt == 0), stop=(kt == nkt - 1)), inc=pe)
                att_last[si3] = k.val(pe)
                if rec["gp"] == rec["nkt"] // 2 - 1:
                    finalize_head(rec)

            def tail_drain(ap):
                k.drain_inc("scalar", ac)
                acv = k.val(ac)
                ap.tail_acv = acv
                for rec in ap.pendq:
                    if rec["pi"] not in ap.tail_ready and not rec["diag"]:
                        ap.tail_ready[rec["pi"]] = (ac, acv)

            def attention_run(ap, j_range, after_j=None):
                for j in j_range:
                    for g in range(KVL):
                        ap.groups.append((j, g))
                ap.ensure_loads(min(ap.gidx + 1, len(ap.groups) - 1))
                for j in j_range:
                    nkt = 4 * j + 4
                    npair = nkt // 2
                    for h in range(HL):
                        g = h // 4
                        gidx = ap.gidx + g
                        qsl = gidx % 2
                        qcols = (h % 4) * 512
                        if h % 4 == 0:
                            ap.ensure_loads(min(gidx + 1, len(ap.groups) - 1))
                            k.w("tensor", "s_q0" if qsl == 0 else "s_q1", s_q_mark[gidx])
                        for gp in range(npair):
                            si = ap.pi % 2
                            si3 = ap.pi % 3
                            rec = dict(j=j, h=h, g=g, gp=gp, nkt=nkt, si=si, si3=si3,
                                       gidx=gidx, diag=(gp >= 2 * j))
                            k.w("tensor", ac, ps_last[si])
                            for u in (0, 1):
                                kt = 2 * gp + u
                                k.run("tensor", lambda e, kt=kt, u=u, si=si, g=g, qsl=qsl,
                                      qcols=qcols: e.matmul(
                                    p_s2[si][:, u * 512:(u + 1) * 512],
                                    K_res[:, g * T + kt * 128:g * T + (kt + 1) * 128],
                                    q_t[qsl][:, qcols:qcols + 512], start=True, stop=True),
                                    inc=pe)
                            qk_done = k.val(pe)
                            while len(ap.pendq) >= 2:
                                emit_pv_den(ap.pendq.pop(0))
                            k.w("scalar", pe, qk_done)
                            k.w("scalar", pe, att_last[si3])
                            k.run("scalar", lambda e, si=si, si3=si3: e.activation(
                                out=att2[si3][:, :], in_=p_s2[si][:, :], func=AF.Exp,
                                scale=SCALE), inc=ac)
                            ap.exp_inc.append(k.val(ac))
                            ps_last[si] = ap.exp_inc[ap.pi]
                            rec["pi"] = ap.pi
                            for prev in ap.pendq:
                                if prev["diag"] and prev["pi"] == ap.pi - 1 \
                                        and prev["pi"] not in ap.tail_ready:
                                    emit_masks(prev)
                            ap.pendq.append(rec)
                            ap.pi += 1
                        if h % 4 == 3:
                            qt_free[gidx % 2] = k.val(pe)
                    ap.gidx += KVL
                    if after_j is not None:
                        after_j(j)
                tail_drain(ap)
                while ap.pendq:
                    emit_pv_den(ap.pendq.pop(0))

            wosl_free = [0, 0]
            wop_mark = {}

            def wo_half(half):
                for fo in range(8):
                    sl = fo % 2
                    if half == 0 and fo == 0:
                        k.w("sync", "woc_w")
                    k.w("sync", pe, wosl_free[sl])
                    _cdma(k, "sync", wo_sl[sl][:, :],
                          wo_bf[:, fo * 512:(fo + 1) * 512], 16, s_wo[sl])
                    k.w("tensor", s_wo[sl])
                    for tl in range(16):
                        jj = tl // 4
                        qq = (tl % 4) * 128
                        pwi = tl % 2
                        pp = p_o[pwi]
                        k.w("tensor", dv, osb_dv[(jj, HL - 1)])
                        k.w("tensor", dv, po_free[pwi])
                        k.w("tensor", dv, wost_free[pwi])
                        for a in range(16):
                            k.run("tensor", lambda e, a=a, jj=jj, qq=qq, pp=pp, sl=sl: e.matmul(
                                pp[:, :],
                                O_sb[:, (jj * 16 + a) * 512 + qq:(jj * 16 + a) * 512 + qq + 128],
                                wo_sl[sl][:, a * 512:(a + 1) * 512],
                                start=(a == 0), stop=(a == 15)), inc=pe if a == 15 else None)
                        k.w("vector", pe)
                        k.w("vector", wop_w[pwi])
                        k.run("vector", lambda e, pp=pp, pwi=pwi: e.tensor_copy(
                            wo_st[pwi][:, :], pp[:, :]))
                        k.drain_inc("vector", dv)
                        wost_free[pwi] = k.val(dv)
                        po_free[pwi] = k.val(dv)
                        k.w("scalar", dv)
                        k.dma("scalar",
                              wo_part[fo, tl // 8, half, (tl % 8) * 128:(tl % 8) * 128 + 128, :],
                              wo_st[pwi][:, :], wop_w[pwi])
                        if half == 1 and tl % 8 == 7:
                            qh = tl // 8
                            k.w("gpsimd", wop_w[0])
                            k.w("gpsimd", wop_w[1])
                            k.run("gpsimd", lambda e, fo=fo, qh=qh: e.collective_compute(
                                "ReduceScatter", Op.add,
                                replica_groups=[[0, 1], [2, 3], [4, 5], [6, 7]],
                                ins=[wo_part[fo, qh].opt()],
                                outs=[rs_out[fo, qh].opt()]), inc=cc_s)
                    wosl_free[sl] = k.val(pe)

            wfc_w = k.sem("wfc_w")
            woc_w = k.sem("woc_w")
            attention_run(ap, range(0, 4), after_j=lambda j: k.dma(
                "gpsimd", wo_bf[j * 512:(j + 1) * 512, :],
                wo[j * 512:(j + 1) * 512, :], woc_w))
            wo_half(0)
            for j4 in range(8):
                k.dma("gpsimd", wf_bf[j4 * 512:(j4 + 1) * 512, :],
                      wf[j4 * 512:(j4 + 1) * 512, :], wfc_w)
            attention_run(ap, range(4, 8))
            wo_half(1)
            if dump_list:
                k.w("sync", cc_s)
                dsem = k.sem("dump_w")
                for name, t, shape, dt in dump_list:
                    if name in ("wo_part", "rs_out"):
                        d = nc.dram_tensor(name + "_dump", shape, dt, kind="ExternalOutput").ap()
                        k.dma("sync", d, t, dsem)
            k.barrier("p2_end")

        if upto < 3:
            return zero_out_and_finish()

        with ExitStack() as p3:
            E3 = p3.enter_context
            rs_t = [E3(nc.sbuf_tensor(f"rs_t{i}", [128, D], F32)) for i in range(2)]
            xo_t = [E3(nc.sbuf_tensor(f"xo_t{i}", [128, D], F32)) for i in range(2)]
            xr_t = [E3(nc.sbuf_tensor(f"xr_t{i}", [128, D], F32)) for i in range(2)]
            h2_tm = [E3(nc.sbuf_tensor(f"h2_tm{i}", [128, D], BF16)) for i in range(2)]
            h2_st = E3(nc.sbuf_tensor("h2_st", [128, 32 * 512], BF16))
            stat2b = [E3(nc.sbuf_tensor(f"stat2b{i}", [128, 8], F32)) for i in range(2)]
            expd = E3(nc.sbuf_tensor("expd", [128, 64], F32))
            hits = E3(nc.sbuf_tensor("hits", [128, 64], BF16))
            cnt_sb = E3(nc.sbuf_tensor("cnt_sb", [1, 64], F32))
            p_x2 = E3(nc.psum_tensor("p_x2", [128, 512], BF16))
            p_dt = [E3(nc.psum_tensor(f"p_dt{i}", [128, 64], F32)) for i in range(2)]
            p_cnt = E3(nc.psum_tensor("p_cnt", [1, 64], F32))

            s_rs = [k.sem(f"s_rs{i}") for i in range(2)]
            s_xo = [k.sem(f"s_xo{i}") for i in range(2)]
            xres_w = [k.sem(f"xres_w{i}") for i in range(2)]
            h2_w = k.sem("h2_w")
            cntin_w = k.sem("cntin_w")

            xrt_free = [(0, 0), (0, 0)]
            pdt_free = [0, 0]
            hits_free = [0]
            htm2_free = [0, 0]

            k.w("sync", cc_s)
            for tl in range(16):
                sl = tl % 2
                dvw, acw = xrt_free[sl]
                k.w("sync", dv, dvw)
                k.w("sync", ac, acw)
                k.dma("sync", rs_t[sl][:, :].rearrange("p (c n) -> p c n", c=8),
                      rs_out[:, tl // 8, (tl % 8) * 128:(tl % 8) * 128 + 128, :]
                      .transpose([1, 0, 2]), s_rs[sl])
                k.dma("scalar", xo_t[sl][:, :], x_own[tl * 128:(tl + 1) * 128, :], s_xo[sl])
                k.w("vector", s_rs[sl])
                k.w("vector", s_xo[sl])
                k.w("vector", xres_w[sl])
                k.run("vector", lambda e, sl=sl: e.tensor_tensor(
                    out=xr_t[sl][:, :], in0=rs_t[sl][:, :], in1=xo_t[sl][:, :], op=Op.add))
                k.drain_inc("vector", dv)
                k.w("scalar", dv)
                k.dma("scalar", xres_sp[tl * 128:(tl + 1) * 128, :], xr_t[sl][:, :], xres_w[sl])
                k.w("scalar", pe, htm2_free[sl])
                k.run("vector", lambda e, sl=sl: e.reduce_sum(
                    out=stat2b[sl][:, 0:1], in_=xr_t[sl][:, :], axis=AX.X))
                k.run("scalar", lambda e, sl=sl: e.activation(
                    out=h2_tm[sl][:, :], in_=xr_t[sl][:, :], func=AF.Square,
                    accum_out=stat2b[sl][:, 2:3]))
                k.drain_inc("scalar", ac)
                k.w("vector", ac)
                k.drain("vector")
                k.run("vector", lambda e, sl=sl: e.tensor_scalar(
                    out=stat2b[sl][:, 1:2], in0=stat2b[sl][:, 0:1], scalar1=1.0 / D, scalar2=None,
                    op0=Op.mult, op1=Op.bypass))
                k.drain("vector")
                k.run("vector", lambda e, sl=sl: e.tensor_tensor(
                    out=stat2b[sl][:, 3:4], in0=stat2b[sl][:, 1:2], in1=stat2b[sl][:, 1:2], op=Op.mult))
                k.drain("vector")
                k.run("vector", lambda e, sl=sl: e.tensor_scalar(
                    out=stat2b[sl][:, 4:5], in0=stat2b[sl][:, 2:3], scalar1=1.0 / D, scalar2=stat2b[sl][:, 3:4],
                    op0=Op.mult, op1=Op.subtract))
                k.drain_inc("vector", dv)
                k.w("scalar", dv)
                k.run("scalar", lambda e, sl=sl: e.activation(
                    out=stat2b[sl][:, 5:6], in_=stat2b[sl][:, 4:5], func=AF.Ln, bias=c_eps[:, :]))
                k.run("scalar", lambda e, sl=sl: e.activation(
                    out=stat2b[sl][:, 6:7], in_=stat2b[sl][:, 5:6], func=AF.Exp, scale=-0.5))
                k.drain_inc("scalar", ac)
                k.w("vector", ac)
                k.run("vector", lambda e, sl=sl: e.tensor_scalar(
                    out=h2_tm[sl][:, :], in0=xr_t[sl][:, :],
                    scalar1=stat2b[sl][:, 1:2], scalar2=stat2b[sl][:, 6:7],
                    op0=Op.subtract, op1=Op.mult))
                k.drain_inc("vector", dv)
                xrt_free[sl] = (k.val(dv), k.val(ac))
                k.w("tensor", dv)
                if tl % 4 == 0:
                    k.w("tensor", h2_w)
                for grp in range(8):
                    for u in range(4):
                        ch = grp * 4 + u
                        k.run("tensor", lambda e, ch=ch, u=u, sl=sl: e.transpose(
                            p_x2[:, u * 128:(u + 1) * 128],
                            h2_tm[sl][:, ch * 128:(ch + 1) * 128], c_iden[:, :]),
                            inc=pe if u == 3 else None)
                    k.w("vector", pe)
                    for u in range(4):
                        ch = grp * 4 + u
                        k.run("vector", lambda e, ch=ch, u=u, tl=tl: e.tensor_copy(
                            h2_st[:, ch * 512 + (tl % 4) * 128:ch * 512 + (tl % 4) * 128 + 128],
                            p_x2[:, u * 128:(u + 1) * 128]))
                    k.drain_inc("vector", dv)
                    if grp < 7:
                        k.w("tensor", dv)
                htm2_free[sl] = k.val(pe)
                pdt = p_dt[tl % 2]
                k.w("tensor", dv)
                k.w("tensor", ac, pdt_free[tl % 2])
                for fi in range(32):
                    k.run("tensor", lambda e, fi=fi, tl=tl, pdt=pdt: e.matmul(
                        pdt[:, :],
                        h2_st[:, fi * 512 + (tl % 4) * 128:fi * 512 + (tl % 4) * 128 + 128],
                        c_cent[:, fi * 64:(fi + 1) * 64],
                        start=(fi == 0), stop=(fi == 31)), inc=pe if fi == 31 else None)
                k.w("vector", pe)
                k.w("vector", pe, hits_free[0])
                k.run("vector", lambda e, pdt=pdt, sl=sl: e.reduce_max(
                    out=stat2b[sl][:, 7:8], in_=pdt[:, :], axis=AX.X, negate=True))
                k.drain("vector")
                k.run("vector", lambda e, sl=sl: e.tensor_scalar(
                    out=stat2b[sl][:, 7:8], in0=stat2b[sl][:, 7:8], scalar1=TEMP_INV, scalar2=None,
                    op0=Op.mult, op1=Op.bypass))
                k.drain_inc("vector", dv)
                k.w("scalar", dv)
                k.run("scalar", lambda e, pdt=pdt, sl=sl: e.activation(
                    out=expd[:, :], in_=pdt[:, :], func=AF.Exp, scale=TEMP_INV,
                    bias=stat2b[sl][:, 7:8], accum_out=stat2b[sl][:, 0:1]))
                k.drain_inc("scalar", ac)
                pdt_free[tl % 2] = k.val(ac)
                k.w("vector", ac)
                k.run("vector", lambda e, sl=sl: e.tensor_scalar(
                    out=stat2b[sl][:, 1:2], in0=stat2b[sl][:, 0:1], scalar1=THRESH, scalar2=None,
                    op0=Op.mult, op1=Op.bypass))
                k.drain("vector")
                k.run("vector", lambda e, sl=sl: e.tensor_scalar(
                    out=hits[:, :], in0=expd[:, :], scalar1=stat2b[sl][:, 1:2], scalar2=None,
                    op0=Op.is_gt, op1=Op.bypass))
                k.drain("vector")
                k.run("vector", lambda e, tl=tl: e.reduce_max(
                    out=c_qsel[:, tl:tl + 1], in_=hits[:, :], axis=AX.X))
                k.drain_inc("vector", dv)
                k.w("tensor", dv)
                k.run("tensor", lambda e, tl=tl: e.matmul(
                    p_cnt[:, :], c_onescol[:, :], hits[:, :],
                    start=(tl == 0), stop=(tl == 15)), inc=pe)
                hits_free[0] = k.val(pe)
                if tl % 4 == 3:
                    k.w("scalar", dv)
                    _cdma_out(k, "scalar",
                          h2_sp[:, (tl - 3) * 128:(tl + 1) * 128], h2_st[:, :], 32, h2_w)

            k.w("vector", pe)
            k.run("vector", lambda e: e.tensor_copy(cnt_sb[:, :], p_cnt[:, :]))
            k.drain_inc("vector", dv)
            cnt_dv = k.val(dv)
            k.barrier("p3_end")
        k.w("gpsimd", dv, cnt_dv)
        k.dma("gpsimd", cnt_in, cnt_sb[:, :], k.sem("cntin_w2"))
        k.w("gpsimd", "cntin_w2")
        k.run("gpsimd", lambda e: e.collective_compute(
            "AllReduce", Op.add, replica_groups=[list(range(8))],
            ins=[cnt_in.opt()], outs=[cnt_out.opt()]), inc=cc_s)
        if dump_list:
            k.w("sync", cc_s)
            dsem2 = k.sem("dump_w2")
            for name, t, shape, dt in dump_list:
                if name == "cnt_out":
                    d = nc.dram_tensor(name + "_dump", shape, dt, kind="ExternalOutput").ap()
                    k.dma("sync", d, t, dsem2)

        c_rsel = E(nc.sbuf_tensor("c_rsel", [128, D], BF16))
        c_brs = E(nc.sbuf_tensor("c_brs", [128, D], F32))
        with ExitStack() as p4:
            E4 = p4.enter_context
            cnt_col = E4(nc.sbuf_tensor("cnt_col", [64, 1], F32))
            csel_col = E4(nc.sbuf_tensor("csel_col", [64, 1], BF16))
            a_i32 = E4(nc.sbuf_tensor("a_i32", [64, 512], I32))
            a_f32 = E4(nc.sbuf_tensor("a_f32", [64, 512], F32))
            ohc = E4(nc.sbuf_tensor("ohc", [64, 512], BF16))
            rrow = E4(nc.sbuf_tensor("rrow", [1, 512], BF16))
            brep = E4(nc.sbuf_tensor("brep", [128, D], F32))
            p_p4 = E4(nc.psum_tensor("p_p4", [128, 512], F32))
            h2_half = E4(nc.sbuf_tensor("h2_half", [128, 32 * 1024], BF16))
            wf_sl = [E4(nc.sbuf_tensor(f"wf_sl{i}", [128, 32 * 512], BF16)) for i in range(2)]
            xr5 = [E4(nc.sbuf_tensor(f"xr5_{i}", [128, 512], F32)) for i in range(4)]
            f1b = [E4(nc.sbuf_tensor(f"f1b_{i}", [128, 512], F32)) for i in range(2)]
            f2b = [E4(nc.sbuf_tensor(f"f2b_{i}", [128, 512], F32)) for i in range(2)]
            ob5 = [E4(nc.sbuf_tensor(f"ob5_{i}", [128, 512], F32)) for i in range(2)]
            p_f = [E4(nc.psum_tensor(f"p_f{i}", [128, 512], F32)) for i in range(2)]
            s_h2 = k.sem("s_h2")
            s_wf = [k.sem(f"s_wf{i}") for i in range(2)]
            s_x5 = [k.sem(f"s_x5_{i}") for i in range(4)]
            out_w = [k.sem(f"out_w{i}") for i in range(2)]

            k.w("sync", h2_w)
            _cdma(k, "sync", h2_half[:, :], h2_sp[:, 0:1024], 32, s_h2)
            k.w("sync", "wfc_w")
            _cdma(k, "sync", wf_sl[0][:, :], wf_bf[:, 0:512], 32, s_wf[0])
            _cdma(k, "sync", wf_sl[1][:, :], wf_bf[:, 512:1024], 32, s_wf[1])

            s_cnt2 = k.sem("s_cnt2")
            s_as = k.sem("s_as")
            s_br = k.sem("s_br")
            k.w("sync", cc_s)
            k.dma("sync", cnt_col[:, :], cnt_out.transpose([1, 0]), s_cnt2, slow=True)
            k.dma("sync", brep[:, :], bias_rep, s_br)
            k.w("vector", s_cnt2)
            k.run("vector", lambda e: e.tensor_scalar(
                out=csel_col[:, :], in0=cnt_col[:, :], scalar1=0.5, scalar2=None,
                op0=Op.is_gt, op1=Op.bypass))
            k.drain_inc("vector", dv)
            for fc in range(8):
                k.w("sync", dv)
                k.w("sync", pe)
                k.dma("sync", a_i32[:, :],
                      assign[:, fc * 512:(fc + 1) * 512].partition_broadcast(64).squeeze(1),
                      s_as)
                k.w("vector", s_as)
                k.run("vector", lambda e: e.tensor_copy(a_f32[:, :], a_i32[:, :]))
                k.drain("vector")
                k.run("vector", lambda e: e.tensor_scalar(
                    out=ohc[:, :], in0=a_f32[:, :], scalar1=c_iota[:, :], scalar2=None,
                    op0=Op.is_equal, op1=Op.bypass))
                k.drain_inc("vector", dv)
                k.w("tensor", dv)
                k.run("tensor", lambda e: e.matmul(
                    p_p4[0:1, :], csel_col[:, :], ohc[:, :], start=True, stop=True), inc=pe)
                k.w("scalar", pe)
                k.run("scalar", lambda e: e.copy(out=rrow[:, :], in_=p_p4[0:1, :]))
                k.drain_inc("scalar", ac)
                k.w("tensor", ac)
                k.run("tensor", lambda e: e.matmul(
                    p_p4[:, :], c_onesk1[:, :], rrow[:, :], start=True, stop=True), inc=pe)
                k.w("vector", pe)
                k.w("vector", s_br)
                k.run("vector", lambda e, fc=fc: e.tensor_copy(
                    c_rsel[:, fc * 512:(fc + 1) * 512], p_p4[:, :]))
                k.drain("vector")
                k.run("vector", lambda e, fc=fc: e.tensor_tensor(
                    out=c_brs[:, fc * 512:(fc + 1) * 512],
                    in0=brep[:, fc * 512:(fc + 1) * 512],
                    in1=c_rsel[:, fc * 512:(fc + 1) * 512], op=Op.mult))
                k.drain_inc("vector", dv)
            rsel_dv = k.val(dv)

            xr5_free = [0, 0, 0, 0]
            pf_free = [0, 0]
            wf_free = [0, 0]
            k.w("vector", dv, rsel_dv)

            for half in range(2):
                if half == 1:
                    k.w("sync", pe)
                    k.w("sync", h2_w)
                    _cdma(k, "sync", h2_half[:, :],
                          h2_sp[:, half * 1024:(half + 1) * 1024], 32, s_h2)
                k.w("tensor", s_h2)
                for fo in range(8):
                    sl = fo % 2
                    if half > 0 or fo > 1:
                        k.w("sync", pe, wf_free[sl])
                        _cdma(k, "sync", wf_sl[sl][:, :],
                              wf_bf[:, fo * 512:(fo + 1) * 512], 32, s_wf[sl])
                    k.w("tensor", s_wf[sl])
                    for tl in range(8):
                        tg = half * 8 + tl
                        xsl = (fo * 8 + tl) % 4
                        pfi = tl % 2
                        pp = p_f[pfi]
                        k.w("sync", dv, xr5_free[xsl])
                        k.dma("sync", xr5[xsl][:, :],
                              xres_sp[tg * 128:(tg + 1) * 128, fo * 512:(fo + 1) * 512],
                              s_x5[xsl])
                        k.w("tensor", dv, pf_free[pfi])
                        for fi in range(32):
                            k.run("tensor", lambda e, fi=fi, tl=tl, pp=pp, sl=sl: e.matmul(
                                pp[:, :],
                                h2_half[:, fi * 1024 + tl * 128:fi * 1024 + (tl + 1) * 128],
                                wf_sl[sl][:, fi * 512:(fi + 1) * 512],
                                start=(fi == 0), stop=(fi == 31)),
                                inc=pe if fi == 31 else None)
                        k.w("vector", pe)
                        k.w("vector", s_x5[xsl])
                        k.w("vector", out_w[tl % 2])
                        k.run("vector", lambda e, pp=pp, tg=tg, fo=fo, tl=tl: e.scalar_tensor_tensor(
                            out=f1b[tl % 2][:, :], in0=pp[:, :], scalar=c_qsel[:, tg:tg + 1],
                            in1=c_rsel[:, fo * 512:(fo + 1) * 512], op0=Op.mult, op1=Op.mult))
                        k.run("vector", lambda e, tg=tg, fo=fo, tl=tl, xsl=xsl: e.scalar_tensor_tensor(
                            out=f2b[tl % 2][:, :], in0=c_brs[:, fo * 512:(fo + 1) * 512],
                            scalar=c_qsel[:, tg:tg + 1], in1=xr5[xsl][:, :],
                            op0=Op.mult, op1=Op.add))
                        k.drain("vector")
                        k.run("vector", lambda e, tl=tl: e.tensor_tensor(
                            out=ob5[tl % 2][:, :], in0=f1b[tl % 2][:, :], in1=f2b[tl % 2][:, :],
                            op=Op.add))
                        k.drain_inc("vector", dv)
                        xr5_free[xsl] = k.val(dv)
                        pf_free[pfi] = k.val(dv)
                        k.w("scalar", dv)
                        k.dma("scalar",
                              out[tg * 128:(tg + 1) * 128, fo * 512:(fo + 1) * 512],
                              ob5[tl % 2][:, :], out_w[tl % 2])
                    wf_free[sl] = k.val(pe)
            k.barrier("end")
            k.w("sync", out_w[0])
            k.w("sync", out_w[1])

        block = E(nc.Block())
        k.finalize(block)
    return nc


def kernel(L0_attn_x, L0_attn_Wq, L0_attn_Wk, L0_attn_Wv, L0_attn_Wo,
           L0_ffn_weight, L0_ffn_bias, L0_ffn_centroids, L0_ffn_assignments):
    x = np.ascontiguousarray(np.asarray(L0_attn_x, dtype=np.float32))
    Wq = np.asarray(L0_attn_Wq, dtype=np.float32)
    Wk = np.asarray(L0_attn_Wk, dtype=np.float32)
    Wv = np.asarray(L0_attn_Wv, dtype=np.float32)
    Wo = np.asarray(L0_attn_Wo, dtype=np.float32)
    Wf = np.asarray(L0_ffn_weight, dtype=np.float32)
    bias = np.asarray(L0_ffn_bias, dtype=np.float32)
    centroids = np.asarray(L0_ffn_centroids, dtype=np.float32)
    assignments = np.asarray(L0_ffn_assignments, dtype=np.int32)

    cosT, sinT = _rope_tables()
    idenM = np.eye(128, dtype=ml_dtypes.bfloat16)
    onesM = np.ones((128, 128), dtype=ml_dtypes.bfloat16)
    onescolM = np.ones((128, 1), dtype=ml_dtypes.bfloat16)
    onesk1M = np.ones((1, 128), dtype=ml_dtypes.bfloat16)
    iotaM = np.arange(64, dtype=np.float32).reshape(64, 1)
    kp = np.arange(128)[:, None]
    qf = np.arange(512)[None, :]
    masksM = np.ascontiguousarray(
        np.concatenate([(kp + 128 * i <= qf) for i in range(4)], axis=1).astype(ml_dtypes.bfloat16))
    bias_repM = np.ascontiguousarray(np.broadcast_to(bias[None, :], (128, D)))
    centM = np.ascontiguousarray(centroids.T)
    assignM = np.ascontiguousarray(assignments.reshape(1, D))

    nc = build_nc(dumps=tuple(DEBUG_DUMPS), upto=UPTO)
    in_maps = []
    for c in range(8):
        b, r = divmod(c, 2)
        in_maps.append(dict(
            x_seq=x[b],
            x_own=np.ascontiguousarray(x[b, r * TOWN:(r + 1) * TOWN, :]),
            wq=np.ascontiguousarray(Wq[r * DQ:(r + 1) * DQ, :].T),
            wk=np.ascontiguousarray(Wk[r * DKV:(r + 1) * DKV, :].T),
            wv=np.ascontiguousarray(Wv[r * DKV:(r + 1) * DKV, :].T),
            wo=np.ascontiguousarray(Wo[:, r * DQ:(r + 1) * DQ].T),
            wf=np.ascontiguousarray(Wf.T),
            bias_rep=bias_repM, cent=centM, assign=assignM,
            cosT=cosT, sinT=sinT, iden=idenM, ones=onesM, onescol=onescolM,
            onesk1=onesk1M, iota64=iotaM, masks=masksM,
            epsv=np.full((128, 1), EPS, dtype=np.float32),
        ))
    res = run_bass_kernel_spmd(nc, in_maps, core_ids=list(range(8)))
    globals()["_LAST_RESULTS"] = res
    outp = np.empty((4, T, D), dtype=np.float32)
    for c in range(8):
        b, r = divmod(c, 2)
        outp[b, r * TOWN:(r + 1) * TOWN, :] = res.results[c]["out"]
    return outp


# version 11
# speedup vs baseline: 1.0609x; 1.0609x over previous
import math
from contextlib import ExitStack

import numpy as np
import ml_dtypes

import concourse.bass as bass
import concourse.mybir as mybir
from concourse.bass_utils import run_bass_kernel_spmd
from concourse.alu_op_type import AluOpType

F32 = mybir.dt.float32
BF16 = mybir.dt.bfloat16
I32 = mybir.dt.int32
AF = mybir.ActivationFunctionType
AX = mybir.AxisListType
Op = AluOpType

D = 4096
T = 4096
HL = 16
KVL = 4
HD = 128
DQ = HL * HD
DKV = KVL * HD
TC = 1024
NBLK = T // TC
NJ = T // 512
SCALE = 1.0 / math.sqrt(HD)
EPS = 1e-5
TEMP_INV = 10.0
THRESH = 0.01
TOWN = 2048

DEBUG_DUMPS = []
UPTO = 99


class KB:
    ENGS = ("tensor", "vector", "scalar", "gpsimd", "sync")

    def __init__(self, nc, stack):
        self.nc = nc
        self.stack = stack
        self.streams = {e: [] for e in self.ENGS}
        self.sems = {}

    def sem(self, name):
        if name not in self.sems:
            h = self.stack.enter_context(self.nc.semaphore(name))
            self.sems[name] = [h, 0]
        return name

    def val(self, name):
        return self.sems[name][1]

    def em(self, eng, fn):
        self.streams[eng].append(fn)

    def w(self, eng, sname, v=None):
        if v is None:
            v = self.sems[sname][1]
        if v <= 0:
            return
        h = self.sems[sname][0]
        self.em(eng, lambda e, h=h, v=v: e.wait_ge(h, v))

    def inc(self, sname, n=1):
        s = self.sems[sname]
        s[1] += n
        return s[1]

    def run(self, eng, fn, inc=None, inc_n=1):
        if inc is not None:
            h = self.sems[inc][0]
            self.inc(inc, inc_n)
            self.em(eng, lambda e, fn=fn, h=h, n=inc_n: fn(e).then_inc(h, n))
        else:
            self.em(eng, lambda e, fn=fn: fn(e))

    def drain_inc(self, eng, sname, n=1):
        h = self.sems[sname][0]
        self.inc(sname, n)
        self.em(eng, lambda e, h=h, n=n: e.drain().then_inc(h, n))

    def drain(self, eng):
        self.em(eng, lambda e: e.drain())

    def dma(self, qeng, out_ap, in_ap, sname, slow=False):
        h = self.sems[sname][0]
        self.inc(sname, 16)
        if slow:
            self.em(qeng, lambda e, o=out_ap, i=in_ap, h=h: e.dma_start(
                out=o, in_=i, allow_slow_non_contiguous=True).then_inc(h, 16))
        else:
            self.em(qeng, lambda e, o=out_ap, i=in_ap, h=h: e.dma_start(
                out=o, in_=i).then_inc(h, 16))

    def barrier(self, tag):
        for eng in ("tensor", "vector", "scalar", "gpsimd"):
            self.drain_inc(eng, self.sem("bar_" + eng))
        self.run("sync", lambda e: e.nop(), inc=self.sem("bar_sync"))
        snapshot = [(n, s[1]) for n, s in self.sems.items()]
        for eng in self.ENGS:
            for n, v in snapshot:
                if v > 0:
                    h = self.sems[n][0]
                    self.em(eng, lambda e, h=h, v=v: e.wait_ge(h, v))

    def finalize(self, block):
        for eng in self.ENGS:
            fns = self.streams[eng]
            dec = getattr(block, eng)

            def section(e, fns=fns):
                for f in fns:
                    f(e)
            dec(section)


def _rope_tables():
    base = 500000.0
    inv_freq = 1.0 / base ** (np.arange(0, HD, 2, dtype=np.float32) / HD)
    factor, low_f, high_f, old_ctx = 32.0, 1.0, 4.0, 8192
    low_wl, high_wl = old_ctx / low_f, old_ctx / high_f
    wavelen = 2 * math.pi / inv_freq
    inv_llama = np.where(wavelen > low_wl, inv_freq / factor, inv_freq)
    smooth = (old_ctx / wavelen - low_f) / (high_f - low_f)
    smoothed = (1 - smooth) * inv_llama / factor + smooth * inv_llama
    is_med = ~(wavelen < high_wl) & ~(wavelen > low_wl)
    inv = np.where(is_med, smoothed, inv_llama).astype(np.float32)
    pos = np.arange(T, dtype=np.float32)
    freqs = pos[:, None] * inv[None, :]
    emb = np.concatenate([freqs, freqs], axis=-1)
    cosT = np.ascontiguousarray(np.cos(emb).astype(np.float32).T)
    sinT = np.ascontiguousarray(np.sin(emb).astype(np.float32).T)
    return cosT, sinT



def _cdma(k, qeng, sbuf_ap, dram2d, C, sem):
    in3 = dram2d.rearrange("(c p) n -> p c n", p=128)
    n = in3.shape[2]
    out3 = sbuf_ap.rearrange("p (c n) -> p c n", c=C)
    k.dma(qeng, out3, in3, sem)


def _cdma_out(k, qeng, dram2d, sbuf_ap, C, sem):
    out3 = dram2d.rearrange("(c p) n -> p c n", p=128)
    in3 = sbuf_ap.rearrange("p (c n) -> p c n", c=C)
    k.dma(qeng, out3, in3, sem)


def build_nc(dumps=(), upto=99):
    nc = bass.Bass("TRN2", target_bir_lowering=False, debug=False)

    def din(name, shape, dt):
        return nc.dram_tensor(name, shape, dt, kind="ExternalInput").ap()

    dump_list = []

    def dint(name, shape, dt):
        if name in ("wo_part", "rs_out", "cnt_in", "cnt_out"):
            t = nc.dram_tensor(name, shape, dt, kind="Internal").ap()
            if name in dumps:
                dump_list.append((name, t, shape, dt))
            return t
        kind = "ExternalOutput" if name in dumps else "Internal"
        return nc.dram_tensor(name, shape, dt, kind=kind).ap()

    x_seq = din("x_seq", [T, D], F32)
    x_own = din("x_own", [TOWN, D], F32)
    wq = din("wq", [D, DQ], F32)
    wk = din("wk", [D, DKV], F32)
    wv = din("wv", [D, DKV], F32)
    wo = din("wo", [DQ, D], F32)
    wf = din("wf", [D, D], F32)
    bias_rep = din("bias_rep", [128, D], F32)
    cent = din("cent", [D, 64], F32)
    assign = din("assign", [1, D], I32)
    cosT = din("cosT", [128, T], F32)
    sinT = din("sinT", [128, T], F32)
    iden = din("iden", [128, 128], BF16)
    ones = din("ones", [128, 128], BF16)
    onescol = din("onescol", [128, 1], BF16)
    onesk1 = din("onesk1", [1, 128], BF16)
    iota64 = din("iota64", [64, 1], F32)
    masks = din("masks", [128, 4 * 512], BF16)
    epsv = din("epsv", [128, 1], F32)

    cc_names = ("wo_part", "rs_out", "cnt_in", "cnt_out")
    q_sp = dint("q_sp", [DQ, T], BF16)
    k_sp = dint("k_sp", [DKV, T], BF16)
    v_sp = dint("v_sp", [T, DKV], BF16)
    wo_part = dint("wo_part", [8, 2, 2, 1024, 512], F32)
    rs_out = dint("rs_out", [8, 2, 1024, 512], F32)
    wf_bf = dint("wf_bf", [D, D], BF16)
    wo_bf = dint("wo_bf", [DQ, D], BF16)
    xres_sp = dint("xres_sp", [TOWN, D], F32)
    h2_sp = dint("h2_sp", [D, TOWN], BF16)
    cnt_in = dint("cnt_in", [1, 64], F32)
    cnt_out = dint("cnt_out", [1, 64], F32)
    out = nc.dram_tensor("out", [TOWN, D], F32, kind="ExternalOutput").ap()

    with ExitStack() as stack:
        E = stack.enter_context
        k = KB(nc, stack)
        c_iden = E(nc.sbuf_tensor("c_iden", [128, 128], BF16))
        c_ones = E(nc.sbuf_tensor("c_ones", [128, 128], BF16))
        c_onescol = E(nc.sbuf_tensor("c_onescol", [128, 1], BF16))
        c_onesk1 = E(nc.sbuf_tensor("c_onesk1", [1, 128], BF16))
        c_iota = E(nc.sbuf_tensor("c_iota", [64, 1], F32))
        c_masks = E(nc.sbuf_tensor("c_masks", [128, 4 * 512], BF16))
        c_cent = E(nc.sbuf_tensor("c_cent", [128, 32 * 64], BF16))
        c_qsel = E(nc.sbuf_tensor("c_qsel", [128, 16], F32))
        c_eps = E(nc.sbuf_tensor("c_eps", [128, 1], F32))

        ld_init = k.sem("ld_init")
        k.dma("sync", c_iden[:, :], iden, ld_init)
        k.dma("sync", c_ones[:, :], ones, ld_init)
        k.dma("sync", c_onescol[:, :], onescol, ld_init)
        k.dma("sync", c_onesk1[:, :], onesk1, ld_init)
        k.dma("sync", c_iota[:, :], iota64, ld_init)
        k.dma("sync", c_masks[:, :], masks, ld_init)
        k.dma("sync", c_eps[:, :], epsv, ld_init)
        _cdma(k, "gpsimd", c_cent[:, :], cent, 32, ld_init)
        for e in ("tensor", "vector", "scalar"):
            k.w(e, ld_init)

        dv = k.sem("dv")
        ac = k.sem("ac")
        pe = k.sem("pe")
        cc_s = k.sem("cc_s")

        def zero_out_and_finish():
            zbuf = E(nc.sbuf_tensor("zbuf", [128, D], F32))
            zs = k.sem("z_w")
            k.run("vector", lambda e: e.memset(zbuf[:, :], 0.0))
            k.drain_inc("vector", dv)
            k.w("sync", dv)
            for ti in range(16):
                k.dma("sync", out[ti * 128:(ti + 1) * 128, :], zbuf[:, :], zs)
            k.barrier("zend")
            block = E(nc.Block())
            k.finalize(block)
            return nc

        with ExitStack() as p1:
            E1 = p1.enter_context
            h_fm = E1(nc.sbuf_tensor("h_fm", [128, 32 * TC], BF16))
            x_in = [E1(nc.sbuf_tensor(f"x_in{i}", [128, D], F32)) for i in range(2)]
            h_tm = [E1(nc.sbuf_tensor(f"h_tm{i}", [128, D], BF16)) for i in range(2)]
            wq_sl = [E1(nc.sbuf_tensor(f"wq_sl{i}", [128, 32 * 256], BF16)) for i in range(2)]
            wv_sl = E1(nc.sbuf_tensor("wv_sl", [128, 32 * 512], BF16))
            cs_sl = [E1(nc.sbuf_tensor(f"cs_sl{i}", [128, 2 * 512], F32)) for i in range(2)]
            r1 = E1(nc.sbuf_tensor("r1", [128, 512], F32))
            rAB = E1(nc.sbuf_tensor("rAB", [128, 512], F32))
            qko = [E1(nc.sbuf_tensor(f"qko{i}", [128, 512], BF16)) for i in range(2)]
            vst = [E1(nc.sbuf_tensor(f"vst{i}", [128, 512], BF16)) for i in range(2)]
            statb = [E1(nc.sbuf_tensor(f"statb{i}", [128, 8], F32)) for i in range(2)]
            p_a = E1(nc.psum_tensor("p_a", [128, 512], F32))
            p_b = E1(nc.psum_tensor("p_b", [128, 512], F32))
            p_x = E1(nc.psum_tensor("p_x", [128, 512], BF16))
            p_mm = [p_a, p_b]

            s_x = [k.sem(f"s_x{i}") for i in range(2)]
            s_wq = [k.sem(f"s_wq{i}") for i in range(2)]
            s_wv = k.sem("s_wv")
            s_cs = [k.sem(f"s_cs{i}") for i in range(2)]
            stqk = [k.sem(f"stqk{i}") for i in range(2)]
            v_wsem = [k.sem(f"v_w{i}") for i in range(2)]

            xin_free = [(0, 0), (0, 0)]
            htm_free = [0, 0]
            pmm_free = [0, 0]
            wq_free = [0, 0]
            evac_cnt = [0]

            def ln_tile(blk, ti):
                t = blk * 8 + ti
                sl = t % 2
                dvw, acw = xin_free[sl]
                k.w("sync", dv, dvw)
                k.w("sync", ac, acw)
                k.dma("sync", x_in[sl][:, :], x_seq[t * 128:(t + 1) * 128, :], s_x[sl])
                k.w("vector", s_x[sl])
                k.w("scalar", s_x[sl])
                k.w("scalar", pe, htm_free[sl])
                k.run("vector", lambda e, sl=sl: e.reduce_sum(
                    out=statb[sl][:, 0:1], in_=x_in[sl][:, :], axis=AX.X))
                k.run("scalar", lambda e, sl=sl: e.activation(
                    out=h_tm[sl][:, :], in_=x_in[sl][:, :], func=AF.Square,
                    accum_out=statb[sl][:, 2:3]))
                k.drain_inc("scalar", ac)
                k.w("vector", ac)
                k.drain("vector")
                k.run("vector", lambda e: e.tensor_scalar(
                    out=statb[sl][:, 1:2], in0=statb[sl][:, 0:1], scalar1=1.0 / D, scalar2=None,
                    op0=Op.mult, op1=Op.bypass))
                k.drain("vector")
                k.run("vector", lambda e: e.tensor_tensor(
                    out=statb[sl][:, 3:4], in0=statb[sl][:, 1:2], in1=statb[sl][:, 1:2], op=Op.mult))
                k.drain("vector")
                k.run("vector", lambda e: e.tensor_scalar(
                    out=statb[sl][:, 4:5], in0=statb[sl][:, 2:3], scalar1=1.0 / D, scalar2=statb[sl][:, 3:4],
                    op0=Op.mult, op1=Op.subtract))
                k.drain_inc("vector", dv)
                k.w("scalar", dv)
                k.run("scalar", lambda e: e.activation(
                    out=statb[sl][:, 5:6], in_=statb[sl][:, 4:5], func=AF.Ln, bias=c_eps[:, :]))
                k.run("scalar", lambda e: e.activation(
                    out=statb[sl][:, 6:7], in_=statb[sl][:, 5:6], func=AF.Exp, scale=-0.5))
                k.drain_inc("scalar", ac)
                k.w("vector", ac)
                k.run("vector", lambda e, sl=sl: e.tensor_scalar(
                    out=h_tm[sl][:, :], in0=x_in[sl][:, :],
                    scalar1=statb[sl][:, 1:2], scalar2=statb[sl][:, 6:7],
                    op0=Op.subtract, op1=Op.mult))
                k.drain_inc("vector", dv)
                xin_free[sl] = (k.val(dv), k.val(ac))
                k.w("tensor", dv)
                for grp in range(8):
                    for u in range(4):
                        ch = grp * 4 + u
                        k.run("tensor", lambda e, ch=ch, u=u, sl=sl: e.transpose(
                            p_x[:, u * 128:(u + 1) * 128],
                            h_tm[sl][:, ch * 128:(ch + 1) * 128], c_iden[:, :]),
                            inc=pe if u == 3 else None)
                    k.w("vector", pe)
                    for u in range(4):
                        ch = grp * 4 + u
                        k.run("vector", lambda e, ch=ch, u=u, ti=ti: e.tensor_copy(
                            h_fm[:, ch * TC + ti * 128:ch * TC + (ti + 1) * 128],
                            p_x[:, u * 128:(u + 1) * 128]))
                    k.drain_inc("vector", dv)
                    if grp < 7:
                        k.w("tensor", dv)
                htm_free[sl] = k.val(pe)

            def proj_and_rope(slab, slab_cols, fout_off, t512_abs, pp_i, dst, row_off, st_sl):
                pp = p_mm[pp_i]
                blk_t512 = t512_abs % 2
                k.w("tensor", dv, pmm_free[pp_i])
                for fi in range(32):
                    k.run("tensor", lambda e, fi=fi, pp=pp, slab=slab, slab_cols=slab_cols,
                          fout_off=fout_off, blk_t512=blk_t512: e.matmul(
                        pp[:, :],
                        slab[:, fi * slab_cols + fout_off:fi * slab_cols + fout_off + 128],
                        h_fm[:, fi * TC + blk_t512 * 512:fi * TC + blk_t512 * 512 + 512],
                        start=(fi == 0), stop=(fi == 31)),
                        inc=pe if fi == 31 else None)
                csl = cs_sl[t512_abs % 2]
                ob = qko[st_sl]
                k.w("vector", pe)
                k.w("vector", s_cs[t512_abs % 2])
                k.w("vector", stqk[st_sl])
                k.run("vector", lambda e, pp=pp, csl=csl: e.tensor_tensor(
                    out=r1[:, :], in0=pp[:, :], in1=csl[:, 0:512], op=Op.mult))
                k.run("vector", lambda e, pp=pp, csl=csl: e.tensor_tensor(
                    out=rAB[0:64, :], in0=pp[64:128, :], in1=csl[0:64, 512:1024], op=Op.mult))
                k.run("vector", lambda e, pp=pp, csl=csl: e.tensor_tensor(
                    out=rAB[64:128, :], in0=pp[0:64, :], in1=csl[64:128, 512:1024], op=Op.mult))
                k.drain("vector")
                k.run("vector", lambda e, ob=ob: e.tensor_tensor(
                    out=ob[0:64, :], in0=r1[0:64, :], in1=rAB[0:64, :], op=Op.subtract))
                k.run("vector", lambda e, ob=ob: e.tensor_tensor(
                    out=ob[64:128, :], in0=r1[64:128, :], in1=rAB[64:128, :], op=Op.add))
                k.drain_inc("vector", dv)
                pmm_free[pp_i] = k.val(dv)
                k.w("scalar", dv)
                k.dma("scalar", dst[row_off:row_off + 128,
                                    t512_abs * 512:(t512_abs + 1) * 512], ob[:, :], stqk[st_sl])

            if upto >= 1:
                _cdma(k, "gpsimd", wv_sl[:, :], wv, 32, s_wv)
                for blk in range(NBLK):
                    for ti in range(8):
                        ln_tile(blk, ti)
                    for t5 in range(2):
                        t512_abs = blk * 2 + t5
                        sl = t512_abs % 2
                        if t512_abs >= 2:
                            k.w("sync", dv)
                        k.dma("sync", cs_sl[sl][:, 0:512],
                              cosT[:, t512_abs * 512:(t512_abs + 1) * 512], s_cs[sl])
                        k.dma("sync", cs_sl[sl][:, 512:1024],
                              sinT[:, t512_abs * 512:(t512_abs + 1) * 512], s_cs[sl])
                    for sq in range(10):
                        sl = sq % 2
                        wsrc = wq if sq < 8 else wk
                        woff = sq * 256 if sq < 8 else (sq - 8) * 256
                        k.w("gpsimd", pe, wq_free[sl])
                        _cdma(k, "gpsimd", wq_sl[sl][:, :],
                              wsrc[:, woff:woff + 256], 32, s_wq[sl])
                        k.w("tensor", s_wq[sl])
                        for hh in range(2):
                            if sq < 8:
                                dst, row = q_sp, (sq * 2 + hh) * 128
                            else:
                                dst, row = k_sp, ((sq - 8) * 2 + hh) * 128
                            for t5 in range(2):
                                t512_abs = blk * 2 + t5
                                i = evac_cnt[0]
                                proj_and_rope(wq_sl[sl], 256, hh * 128, t512_abs,
                                              i % 2, dst, row, i % 2)
                                evac_cnt[0] += 1
                        wq_free[sl] = k.val(pe)
                    k.w("tensor", s_wv)
                    for ti in range(8):
                        t = blk * 8 + ti
                        ppi = ti % 2
                        pp = p_mm[ppi]
                        k.w("tensor", dv, pmm_free[ppi])
                        for fi in range(32):
                            k.run("tensor", lambda e, fi=fi, ti=ti, pp=pp: e.matmul(
                                pp[:, :],
                                h_fm[:, fi * TC + ti * 128:fi * TC + (ti + 1) * 128],
                                wv_sl[:, fi * 512:(fi + 1) * 512],
                                start=(fi == 0), stop=(fi == 31)),
                                inc=pe if fi == 31 else None)
                        k.w("vector", pe)
                        k.w("vector", v_wsem[ti % 2])
                        k.run("vector", lambda e, ti=ti, pp=pp: e.tensor_copy(
                            vst[ti % 2][:, :], pp[:, :]))
                        k.drain_inc("vector", dv)
                        pmm_free[ppi] = k.val(dv)
                        k.w("scalar", dv)
                        k.dma("scalar", v_sp[t * 128:(t + 1) * 128, :], vst[ti % 2][:, :],
                              v_wsem[ti % 2])
            k.barrier("p1_end")

        if upto < 2:
            return zero_out_and_finish()

        with ExitStack() as p2:
            E2 = p2.enter_context
            K_res = E2(nc.sbuf_tensor("K_res", [128, 4 * T], BF16))
            V_res = E2(nc.sbuf_tensor("V_res", [128, 32 * DKV], BF16))
            O_sb = E2(nc.sbuf_tensor("O_sb", [128, 64 * 512], BF16))
            q_t = [E2(nc.sbuf_tensor(f"q_t{i}", [128, 4 * 512], BF16)) for i in range(2)]
            att2 = [E2(nc.sbuf_tensor(f"att2_{i}", [128, 1024], BF16)) for i in range(3)]
            s_lnb = [E2(nc.sbuf_tensor(f"s_lnb{i}", [128, 512], F32)) for i in range(2)]
            s_rec = [E2(nc.sbuf_tensor(f"s_rec{i}", [128, 512], F32)) for i in range(2)]
            wo_sl = [E2(nc.sbuf_tensor(f"wo_sl{i}", [128, 16 * 512], BF16)) for i in range(2)]
            wo_st = [E2(nc.sbuf_tensor(f"wo_st{i}", [128, 512], F32)) for i in range(2)]
            p_s2 = [E2(nc.psum_tensor(f"p_s2_{i}", [128, 1024], F32)) for i in range(2)]
            p_o = [E2(nc.psum_tensor(f"p_o{i}", [128, 512], F32)) for i in range(2)]
            p_d = [E2(nc.psum_tensor(f"p_d{i}", [128, 512], F32)) for i in range(2)]

            s_kv = k.sem("s_kv")
            s_q = [k.sem(f"s_q{i}") for i in range(2)]
            s_wo = [k.sem(f"s_wo{i}") for i in range(2)]
            wop_w = [k.sem(f"wop_w{i}") for i in range(2)]

            k.w("sync", "stqk0"); k.w("sync", "stqk1")
            k.w("sync", "v_w0"); k.w("sync", "v_w1")
            for g in range(KVL):
                k.dma("sync", K_res[:, g * T:(g + 1) * T], k_sp[g * 128:(g + 1) * 128, :], s_kv)
            _cdma(k, "sync", V_res[:, :], v_sp, 32, s_kv)
            k.w("tensor", s_kv)

            att_last = [0, 0, 0]
            ps_last = [0, 0]
            po_free = [0, 0]
            pd_free = [0, 0]
            qt_free = [0, 0]
            osb_dv = {}
            wost_free = [0, 0]

            s_q_mark = {}

            def load_group(j, g, gidx):
                qsl = gidx % 2
                k.w("sync", pe, qt_free[qsl])
                _cdma(k, "sync", q_t[qsl][:, :],
                      q_sp[g * 512:(g + 1) * 512, j * 512:(j + 1) * 512], 4, s_q[qsl])
                s_q_mark[gidx] = k.val(s_q[qsl])

            class APipe:
                def __init__(self):
                    self.pendq = []
                    self.exp_inc = []
                    self.tail_ready = {}
                    self.tail_acv = 0
                    self.gidx = 0
                    self.loaded = 0
                    self.groups = []
                    self.pi = 0

                def ensure_loads(self, upto_idx):
                    while self.loaded <= upto_idx:
                        j2, g2 = self.groups[self.loaded]
                        load_group(j2, g2, self.loaded)
                        self.loaded += 1

            ap = APipe()

            def finalize_head(rec):
                h = rec["h"]; jj = rec["j"] % 4
                po = p_o[h % 2]; pd = p_d[h % 2]
                k.w("scalar", pe)
                k.run("scalar", lambda e, pd=pd: e.activation(
                    out=s_lnb[h % 2][:, :], in_=pd[:, :], func=AF.Ln))
                k.run("scalar", lambda e, h=h: e.activation(
                    out=s_rec[h % 2][:, :], in_=s_lnb[h % 2][:, :], func=AF.Exp, scale=-1.0))
                k.drain_inc("scalar", ac)
                k.w("vector", ac)
                slot = (jj * 16 + h) * 512
                k.run("vector", lambda e, po=po, h=h, slot=slot: e.tensor_tensor(
                    out=O_sb[:, slot:slot + 512], in0=po[:, :], in1=s_rec[h % 2][:, :],
                    op=Op.mult))
                k.drain_inc("vector", dv)
                po_free[h % 2] = k.val(dv)
                pd_free[h % 2] = k.val(dv)
                osb_dv[(jj, h)] = k.val(dv)

            def att_ready_of(rec):
                pi = rec["pi"]
                if pi in ap.tail_ready:
                    return ap.tail_ready[pi]
                return (ac, ap.exp_inc[pi + 1])

            def emit_masks(rec):
                si3 = rec["si3"]
                k.w("vector", ac, ap.exp_inc[rec["pi"] + 1] if rec["pi"] + 1 < len(ap.exp_inc)
                    else ap.tail_acv)
                for u in (0, 1):
                    mi = 2 * rec["gp"] + u - 4 * rec["j"]
                    k.run("vector", lambda e, si3=si3, u=u, mi=mi: e.tensor_tensor(
                        out=att2[si3][:, u * 512:(u + 1) * 512],
                        in0=att2[si3][:, u * 512:(u + 1) * 512],
                        in1=c_masks[:, mi * 512:(mi + 1) * 512], op=Op.mult))
                k.drain_inc("vector", dv)
                ap.tail_ready[rec["pi"]] = (dv, k.val(dv))

            def emit_pv_den(rec):
                si3 = rec["si3"]
                if rec["diag"] and rec["pi"] not in ap.tail_ready:
                    emit_masks(rec)
                sem2, v2 = att_ready_of(rec)
                k.w("tensor", sem2, v2)
                h = rec["h"]; g = rec["g"]; nkt = rec["nkt"]
                po = p_o[h % 2]; pd = p_d[h % 2]
                for u in (0, 1):
                    kt = 2 * rec["gp"] + u
                    if kt == 0:
                        k.w("tensor", dv, po_free[h % 2])
                        k.w("tensor", dv, pd_free[h % 2])
                    k.run("tensor", lambda e, kt=kt, u=u, si3=si3, po=po, g=g, nkt=nkt: e.matmul(
                        po[:, :],
                        V_res[:, kt * DKV + g * 128:kt * DKV + (g + 1) * 128],
                        att2[si3][:, u * 512:(u + 1) * 512],
                        start=(kt == 0), stop=(kt == nkt - 1)), inc=pe)
                    k.run("tensor", lambda e, kt=kt, u=u, si3=si3, pd=pd, nkt=nkt: e.matmul(
                        pd[:, :], c_ones[:, :],
                        att2[si3][:, u * 512:(u + 1) * 512],
                        start=(kt == 0), stop=(kt == nkt - 1)), inc=pe)
                att_last[si3] = k.val(pe)
                if rec["gp"] == rec["nkt"] // 2 - 1:
                    finalize_head(rec)

            def tail_drain(ap):
                k.drain_inc("scalar", ac)
                acv = k.val(ac)
                ap.tail_acv = acv
                for rec in ap.pendq:
                    if rec["pi"] not in ap.tail_ready and not rec["diag"]:
                        ap.tail_ready[rec["pi"]] = (ac, acv)

            def attention_run(ap, j_range, after_j=None):
                for j in j_range:
                    for g in range(KVL):
                        ap.groups.append((j, g))
                ap.ensure_loads(min(ap.gidx + 1, len(ap.groups) - 1))
                for j in j_range:
                    nkt = 4 * j + 4
                    npair = nkt // 2
                    for h in range(HL):
                        g = h // 4
                        gidx = ap.gidx + g
                        qsl = gidx % 2
                        qcols = (h % 4) * 512
                        if h % 4 == 0:
                            ap.ensure_loads(min(gidx + 1, len(ap.groups) - 1))
                            k.w("tensor", "s_q0" if qsl == 0 else "s_q1", s_q_mark[gidx])
                        for gp in range(npair):
                            si = ap.pi % 2
                            si3 = ap.pi % 3
                            rec = dict(j=j, h=h, g=g, gp=gp, nkt=nkt, si=si, si3=si3,
                                       gidx=gidx, diag=(gp >= 2 * j))
                            k.w("tensor", ac, ps_last[si])
                            for u in (0, 1):
                                kt = 2 * gp + u
                                k.run("tensor", lambda e, kt=kt, u=u, si=si, g=g, qsl=qsl,
                                      qcols=qcols: e.matmul(
                                    p_s2[si][:, u * 512:(u + 1) * 512],
                                    K_res[:, g * T + kt * 128:g * T + (kt + 1) * 128],
                                    q_t[qsl][:, qcols:qcols + 512], start=True, stop=True),
                                    inc=pe)
                            qk_done = k.val(pe)
                            while len(ap.pendq) >= 2:
                                emit_pv_den(ap.pendq.pop(0))
                            k.w("scalar", pe, qk_done)
                            k.w("scalar", pe, att_last[si3])
                            k.run("scalar", lambda e, si=si, si3=si3: e.activation(
                                out=att2[si3][:, :], in_=p_s2[si][:, :], func=AF.Exp,
                                scale=SCALE), inc=ac)
                            ap.exp_inc.append(k.val(ac))
                            ps_last[si] = ap.exp_inc[ap.pi]
                            rec["pi"] = ap.pi
                            for prev in ap.pendq:
                                if prev["diag"] and prev["pi"] == ap.pi - 1 \
                                        and prev["pi"] not in ap.tail_ready:
                                    emit_masks(prev)
                            ap.pendq.append(rec)
                            ap.pi += 1
                        if h % 4 == 3:
                            qt_free[gidx % 2] = k.val(pe)
                    ap.gidx += KVL
                    if after_j is not None:
                        after_j(j)
                tail_drain(ap)
                while ap.pendq:
                    emit_pv_den(ap.pendq.pop(0))

            wosl_free = [0, 0]
            rs_mark = {}

            def wo_half(half):
                for fo in range(8):
                    sl = fo % 2
                    if half == 0 and fo == 0:
                        k.w("sync", "woc_w")
                    k.w("sync", pe, wosl_free[sl])
                    _cdma(k, "sync", wo_sl[sl][:, :],
                          wo_bf[:, fo * 512:(fo + 1) * 512], 16, s_wo[sl])
                    k.w("tensor", s_wo[sl])
                    for tl in range(16):
                        jj = tl // 4
                        qq = (tl % 4) * 128
                        pwi = tl % 2
                        pp = p_o[pwi]
                        k.w("tensor", dv, osb_dv[(jj, HL - 1)])
                        k.w("tensor", dv, po_free[pwi])
                        k.w("tensor", dv, wost_free[pwi])
                        for a in range(16):
                            k.run("tensor", lambda e, a=a, jj=jj, qq=qq, pp=pp, sl=sl: e.matmul(
                                pp[:, :],
                                O_sb[:, (jj * 16 + a) * 512 + qq:(jj * 16 + a) * 512 + qq + 128],
                                wo_sl[sl][:, a * 512:(a + 1) * 512],
                                start=(a == 0), stop=(a == 15)), inc=pe if a == 15 else None)
                        k.w("vector", pe)
                        k.w("vector", wop_w[pwi])
                        k.run("vector", lambda e, pp=pp, pwi=pwi: e.tensor_copy(
                            wo_st[pwi][:, :], pp[:, :]))
                        k.drain_inc("vector", dv)
                        wost_free[pwi] = k.val(dv)
                        po_free[pwi] = k.val(dv)
                        k.w("scalar", dv)
                        k.dma("scalar",
                              wo_part[fo, tl // 8, half, (tl % 8) * 128:(tl % 8) * 128 + 128, :],
                              wo_st[pwi][:, :], wop_w[pwi])
                        if half == 1 and tl % 8 == 7:
                            qh = tl // 8
                            k.w("gpsimd", wop_w[0])
                            k.w("gpsimd", wop_w[1])
                            k.run("gpsimd", lambda e, fo=fo, qh=qh: e.collective_compute(
                                "ReduceScatter", Op.add,
                                replica_groups=[[0, 1], [2, 3], [4, 5], [6, 7]],
                                ins=[wo_part[fo, qh].opt()],
                                outs=[rs_out[fo, qh].opt()]), inc=cc_s)
                            rs_mark[(fo, qh)] = k.val(cc_s)
                    wosl_free[sl] = k.val(pe)

            wfc_w = k.sem("wfc_w")
            woc_w = k.sem("woc_w")
            attention_run(ap, range(0, 4), after_j=lambda j: k.dma(
                "gpsimd", wo_bf[j * 512:(j + 1) * 512, :],
                wo[j * 512:(j + 1) * 512, :], woc_w))
            wo_half(0)
            for j4 in range(8):
                k.dma("gpsimd", wf_bf[j4 * 512:(j4 + 1) * 512, :],
                      wf[j4 * 512:(j4 + 1) * 512, :], wfc_w)
            attention_run(ap, range(4, 8))
            wo_half(1)
            if dump_list:
                k.w("sync", cc_s)
                dsem = k.sem("dump_w")
                for name, t, shape, dt in dump_list:
                    if name in ("wo_part", "rs_out"):
                        d = nc.dram_tensor(name + "_dump", shape, dt, kind="ExternalOutput").ap()
                        k.dma("sync", d, t, dsem)
            for eng in ("tensor", "vector", "scalar"):
                k.drain_inc(eng, k.sem("f2_" + eng))
            p2_fence = [(pe, k.val(pe)), (dv, k.val(dv)), (ac, k.val(ac)),
                        ("f2_tensor", k.val("f2_tensor")), ("f2_vector", k.val("f2_vector")),
                        ("f2_scalar", k.val("f2_scalar")),
                        ("wop_w0", k.val("wop_w0")), ("wop_w1", k.val("wop_w1")),
                        ("s_wo0", k.val("s_wo0")), ("s_wo1", k.val("s_wo1")),
                        ("s_kv", k.val("s_kv")), ("s_q0", k.val("s_q0")),
                        ("s_q1", k.val("s_q1")), ("wfc_w", k.val("wfc_w")),
                        ("woc_w", k.val("woc_w"))]
            for eng in KB.ENGS:
                for sname, v in p2_fence:
                    k.w(eng, sname, v)

        if upto < 3:
            return zero_out_and_finish()

        with ExitStack() as p3:
            E3 = p3.enter_context
            rs_t = [E3(nc.sbuf_tensor(f"rs_t{i}", [128, D], F32)) for i in range(2)]
            xo_t = [E3(nc.sbuf_tensor(f"xo_t{i}", [128, D], F32)) for i in range(2)]
            xr_t = [E3(nc.sbuf_tensor(f"xr_t{i}", [128, D], F32)) for i in range(2)]
            h2_tm = [E3(nc.sbuf_tensor(f"h2_tm{i}", [128, D], BF16)) for i in range(2)]
            h2_st = E3(nc.sbuf_tensor("h2_st", [128, 32 * 512], BF16))
            stat2b = [E3(nc.sbuf_tensor(f"stat2b{i}", [128, 8], F32)) for i in range(2)]
            expd = E3(nc.sbuf_tensor("expd", [128, 64], F32))
            hits = E3(nc.sbuf_tensor("hits", [128, 64], BF16))
            cnt_sb = E3(nc.sbuf_tensor("cnt_sb", [1, 64], F32))
            p_x2 = E3(nc.psum_tensor("p_x2", [128, 512], BF16))
            p_dt = [E3(nc.psum_tensor(f"p_dt{i}", [128, 64], F32)) for i in range(2)]
            p_cnt = E3(nc.psum_tensor("p_cnt", [1, 64], F32))

            s_rs = [k.sem(f"s_rs{i}") for i in range(2)]
            s_xo = [k.sem(f"s_xo{i}") for i in range(2)]
            xres_w = [k.sem(f"xres_w{i}") for i in range(2)]
            h2_w = k.sem("h2_w")
            cntin_w = k.sem("cntin_w")

            xrt_free = [(0, 0), (0, 0)]
            pdt_free = [0, 0]
            hits_free = [0]
            htm2_free = [0, 0]

            for tl in range(16):
                sl = tl % 2
                dvw, acw = xrt_free[sl]
                k.w("sync", cc_s, rs_mark[(7, tl // 8)])
                k.w("sync", dv, dvw)
                k.w("sync", ac, acw)
                k.dma("sync", rs_t[sl][:, :].rearrange("p (c n) -> p c n", c=8),
                      rs_out[:, tl // 8, (tl % 8) * 128:(tl % 8) * 128 + 128, :]
                      .transpose([1, 0, 2]), s_rs[sl])
                k.dma("scalar", xo_t[sl][:, :], x_own[tl * 128:(tl + 1) * 128, :], s_xo[sl])
                k.w("vector", s_rs[sl])
                k.w("vector", s_xo[sl])
                k.w("vector", xres_w[sl])
                k.run("vector", lambda e, sl=sl: e.tensor_tensor(
                    out=xr_t[sl][:, :], in0=rs_t[sl][:, :], in1=xo_t[sl][:, :], op=Op.add))
                k.drain_inc("vector", dv)
                k.w("scalar", dv)
                k.dma("scalar", xres_sp[tl * 128:(tl + 1) * 128, :], xr_t[sl][:, :], xres_w[sl])
                k.w("scalar", pe, htm2_free[sl])
                k.run("vector", lambda e, sl=sl: e.reduce_sum(
                    out=stat2b[sl][:, 0:1], in_=xr_t[sl][:, :], axis=AX.X))
                k.run("scalar", lambda e, sl=sl: e.activation(
                    out=h2_tm[sl][:, :], in_=xr_t[sl][:, :], func=AF.Square,
                    accum_out=stat2b[sl][:, 2:3]))
                k.drain_inc("scalar", ac)
                k.w("vector", ac)
                k.drain("vector")
                k.run("vector", lambda e, sl=sl: e.tensor_scalar(
                    out=stat2b[sl][:, 1:2], in0=stat2b[sl][:, 0:1], scalar1=1.0 / D, scalar2=None,
                    op0=Op.mult, op1=Op.bypass))
                k.drain("vector")
                k.run("vector", lambda e, sl=sl: e.tensor_tensor(
                    out=stat2b[sl][:, 3:4], in0=stat2b[sl][:, 1:2], in1=stat2b[sl][:, 1:2], op=Op.mult))
                k.drain("vector")
                k.run("vector", lambda e, sl=sl: e.tensor_scalar(
                    out=stat2b[sl][:, 4:5], in0=stat2b[sl][:, 2:3], scalar1=1.0 / D, scalar2=stat2b[sl][:, 3:4],
                    op0=Op.mult, op1=Op.subtract))
                k.drain_inc("vector", dv)
                k.w("scalar", dv)
                k.run("scalar", lambda e, sl=sl: e.activation(
                    out=stat2b[sl][:, 5:6], in_=stat2b[sl][:, 4:5], func=AF.Ln, bias=c_eps[:, :]))
                k.run("scalar", lambda e, sl=sl: e.activation(
                    out=stat2b[sl][:, 6:7], in_=stat2b[sl][:, 5:6], func=AF.Exp, scale=-0.5))
                k.drain_inc("scalar", ac)
                k.w("vector", ac)
                k.run("vector", lambda e, sl=sl: e.tensor_scalar(
                    out=h2_tm[sl][:, :], in0=xr_t[sl][:, :],
                    scalar1=stat2b[sl][:, 1:2], scalar2=stat2b[sl][:, 6:7],
                    op0=Op.subtract, op1=Op.mult))
                k.drain_inc("vector", dv)
                xrt_free[sl] = (k.val(dv), k.val(ac))
                k.w("tensor", dv)
                if tl % 4 == 0:
                    k.w("tensor", h2_w)
                for grp in range(8):
                    for u in range(4):
                        ch = grp * 4 + u
                        k.run("tensor", lambda e, ch=ch, u=u, sl=sl: e.transpose(
                            p_x2[:, u * 128:(u + 1) * 128],
                            h2_tm[sl][:, ch * 128:(ch + 1) * 128], c_iden[:, :]),
                            inc=pe if u == 3 else None)
                    k.w("vector", pe)
                    for u in range(4):
                        ch = grp * 4 + u
                        k.run("vector", lambda e, ch=ch, u=u, tl=tl: e.tensor_copy(
                            h2_st[:, ch * 512 + (tl % 4) * 128:ch * 512 + (tl % 4) * 128 + 128],
                            p_x2[:, u * 128:(u + 1) * 128]))
                    k.drain_inc("vector", dv)
                    if grp < 7:
                        k.w("tensor", dv)
                htm2_free[sl] = k.val(pe)
                pdt = p_dt[tl % 2]
                k.w("tensor", dv)
                k.w("tensor", ac, pdt_free[tl % 2])
                for fi in range(32):
                    k.run("tensor", lambda e, fi=fi, tl=tl, pdt=pdt: e.matmul(
                        pdt[:, :],
                        h2_st[:, fi * 512 + (tl % 4) * 128:fi * 512 + (tl % 4) * 128 + 128],
                        c_cent[:, fi * 64:(fi + 1) * 64],
                        start=(fi == 0), stop=(fi == 31)), inc=pe if fi == 31 else None)
                k.w("vector", pe)
                k.w("vector", pe, hits_free[0])
                k.run("vector", lambda e, pdt=pdt, sl=sl: e.reduce_max(
                    out=stat2b[sl][:, 7:8], in_=pdt[:, :], axis=AX.X, negate=True))
                k.drain("vector")
                k.run("vector", lambda e, sl=sl: e.tensor_scalar(
                    out=stat2b[sl][:, 7:8], in0=stat2b[sl][:, 7:8], scalar1=TEMP_INV, scalar2=None,
                    op0=Op.mult, op1=Op.bypass))
                k.drain_inc("vector", dv)
                k.w("scalar", dv)
                k.run("scalar", lambda e, pdt=pdt, sl=sl: e.activation(
                    out=expd[:, :], in_=pdt[:, :], func=AF.Exp, scale=TEMP_INV,
                    bias=stat2b[sl][:, 7:8], accum_out=stat2b[sl][:, 0:1]))
                k.drain_inc("scalar", ac)
                pdt_free[tl % 2] = k.val(ac)
                k.w("vector", ac)
                k.run("vector", lambda e, sl=sl: e.tensor_scalar(
                    out=stat2b[sl][:, 1:2], in0=stat2b[sl][:, 0:1], scalar1=THRESH, scalar2=None,
                    op0=Op.mult, op1=Op.bypass))
                k.drain("vector")
                k.run("vector", lambda e, sl=sl: e.tensor_scalar(
                    out=hits[:, :], in0=expd[:, :], scalar1=stat2b[sl][:, 1:2], scalar2=None,
                    op0=Op.is_gt, op1=Op.bypass))
                k.drain("vector")
                k.run("vector", lambda e, tl=tl: e.reduce_max(
                    out=c_qsel[:, tl:tl + 1], in_=hits[:, :], axis=AX.X))
                k.drain_inc("vector", dv)
                k.w("tensor", dv)
                k.run("tensor", lambda e, tl=tl: e.matmul(
                    p_cnt[:, :], c_onescol[:, :], hits[:, :],
                    start=(tl == 0), stop=(tl == 15)), inc=pe)
                hits_free[0] = k.val(pe)
                if tl % 4 == 3:
                    k.w("scalar", dv)
                    _cdma_out(k, "scalar",
                          h2_sp[:, (tl - 3) * 128:(tl + 1) * 128], h2_st[:, :], 32, h2_w)

            k.w("vector", pe)
            k.run("vector", lambda e: e.tensor_copy(cnt_sb[:, :], p_cnt[:, :]))
            k.drain_inc("vector", dv)
            cnt_dv = k.val(dv)
            k.barrier("p3_end")
        k.w("gpsimd", dv, cnt_dv)
        k.dma("gpsimd", cnt_in, cnt_sb[:, :], k.sem("cntin_w2"))
        k.w("gpsimd", "cntin_w2")
        k.run("gpsimd", lambda e: e.collective_compute(
            "AllReduce", Op.add, replica_groups=[list(range(8))],
            ins=[cnt_in.opt()], outs=[cnt_out.opt()]), inc=cc_s)
        if dump_list:
            k.w("sync", cc_s)
            dsem2 = k.sem("dump_w2")
            for name, t, shape, dt in dump_list:
                if name == "cnt_out":
                    d = nc.dram_tensor(name + "_dump", shape, dt, kind="ExternalOutput").ap()
                    k.dma("sync", d, t, dsem2)

        c_rsel = E(nc.sbuf_tensor("c_rsel", [128, D], BF16))
        c_brs = E(nc.sbuf_tensor("c_brs", [128, D], F32))
        with ExitStack() as p4:
            E4 = p4.enter_context
            cnt_col = E4(nc.sbuf_tensor("cnt_col", [64, 1], F32))
            csel_col = E4(nc.sbuf_tensor("csel_col", [64, 1], BF16))
            a_i32 = E4(nc.sbuf_tensor("a_i32", [64, 512], I32))
            a_f32 = E4(nc.sbuf_tensor("a_f32", [64, 512], F32))
            ohc = E4(nc.sbuf_tensor("ohc", [64, 512], BF16))
            rrow = E4(nc.sbuf_tensor("rrow", [1, 512], BF16))
            brep = E4(nc.sbuf_tensor("brep", [128, D], F32))
            p_p4 = E4(nc.psum_tensor("p_p4", [128, 512], F32))
            h2_half = E4(nc.sbuf_tensor("h2_half", [128, 32 * 1024], BF16))
            wf_sl = [E4(nc.sbuf_tensor(f"wf_sl{i}", [128, 32 * 512], BF16)) for i in range(2)]
            xr5 = [E4(nc.sbuf_tensor(f"xr5_{i}", [128, 512], F32)) for i in range(4)]
            f1b = [E4(nc.sbuf_tensor(f"f1b_{i}", [128, 512], F32)) for i in range(2)]
            f2b = [E4(nc.sbuf_tensor(f"f2b_{i}", [128, 512], F32)) for i in range(2)]
            ob5 = [E4(nc.sbuf_tensor(f"ob5_{i}", [128, 512], F32)) for i in range(2)]
            p_f = [E4(nc.psum_tensor(f"p_f{i}", [128, 512], F32)) for i in range(2)]
            s_h2 = k.sem("s_h2")
            s_wf = [k.sem(f"s_wf{i}") for i in range(2)]
            s_x5 = [k.sem(f"s_x5_{i}") for i in range(4)]
            out_w = [k.sem(f"out_w{i}") for i in range(2)]

            k.w("sync", h2_w)
            _cdma(k, "sync", h2_half[:, :], h2_sp[:, 0:1024], 32, s_h2)
            k.w("sync", "wfc_w")
            _cdma(k, "sync", wf_sl[0][:, :], wf_bf[:, 0:512], 32, s_wf[0])
            _cdma(k, "sync", wf_sl[1][:, :], wf_bf[:, 512:1024], 32, s_wf[1])

            s_cnt2 = k.sem("s_cnt2")
            s_as = k.sem("s_as")
            s_br = k.sem("s_br")
            k.w("sync", cc_s)
            k.dma("sync", cnt_col[:, :], cnt_out.transpose([1, 0]), s_cnt2, slow=True)
            k.dma("sync", brep[:, :], bias_rep, s_br)
            k.w("vector", s_cnt2)
            k.run("vector", lambda e: e.tensor_scalar(
                out=csel_col[:, :], in0=cnt_col[:, :], scalar1=0.5, scalar2=None,
                op0=Op.is_gt, op1=Op.bypass))
            k.drain_inc("vector", dv)
            for fc in range(8):
                k.w("sync", dv)
                k.w("sync", pe)
                k.dma("sync", a_i32[:, :],
                      assign[:, fc * 512:(fc + 1) * 512].partition_broadcast(64).squeeze(1),
                      s_as)
                k.w("vector", s_as)
                k.run("vector", lambda e: e.tensor_copy(a_f32[:, :], a_i32[:, :]))
                k.drain("vector")
                k.run("vector", lambda e: e.tensor_scalar(
                    out=ohc[:, :], in0=a_f32[:, :], scalar1=c_iota[:, :], scalar2=None,
                    op0=Op.is_equal, op1=Op.bypass))
                k.drain_inc("vector", dv)
                k.w("tensor", dv)
                k.run("tensor", lambda e: e.matmul(
                    p_p4[0:1, :], csel_col[:, :], ohc[:, :], start=True, stop=True), inc=pe)
                k.w("scalar", pe)
                k.run("scalar", lambda e: e.copy(out=rrow[:, :], in_=p_p4[0:1, :]))
                k.drain_inc("scalar", ac)
                k.w("tensor", ac)
                k.run("tensor", lambda e: e.matmul(
                    p_p4[:, :], c_onesk1[:, :], rrow[:, :], start=True, stop=True), inc=pe)
                k.w("vector", pe)
                k.w("vector", s_br)
                k.run("vector", lambda e, fc=fc: e.tensor_copy(
                    c_rsel[:, fc * 512:(fc + 1) * 512], p_p4[:, :]))
                k.drain("vector")
                k.run("vector", lambda e, fc=fc: e.tensor_tensor(
                    out=c_brs[:, fc * 512:(fc + 1) * 512],
                    in0=brep[:, fc * 512:(fc + 1) * 512],
                    in1=c_rsel[:, fc * 512:(fc + 1) * 512], op=Op.mult))
                k.drain_inc("vector", dv)
            rsel_dv = k.val(dv)

            xr5_free = [0, 0, 0, 0]
            pf_free = [0, 0]
            wf_free = [0, 0]
            k.w("vector", dv, rsel_dv)

            for half in range(2):
                if half == 1:
                    k.w("sync", pe)
                    k.w("sync", h2_w)
                    _cdma(k, "sync", h2_half[:, :],
                          h2_sp[:, half * 1024:(half + 1) * 1024], 32, s_h2)
                k.w("tensor", s_h2)
                for fo in range(8):
                    sl = fo % 2
                    if half > 0 or fo > 1:
                        k.w("sync", pe, wf_free[sl])
                        _cdma(k, "sync", wf_sl[sl][:, :],
                              wf_bf[:, fo * 512:(fo + 1) * 512], 32, s_wf[sl])
                    k.w("tensor", s_wf[sl])
                    for tl in range(8):
                        tg = half * 8 + tl
                        xsl = (fo * 8 + tl) % 4
                        pfi = tl % 2
                        pp = p_f[pfi]
                        k.w("sync", dv, xr5_free[xsl])
                        k.dma("sync", xr5[xsl][:, :],
                              xres_sp[tg * 128:(tg + 1) * 128, fo * 512:(fo + 1) * 512],
                              s_x5[xsl])
                        k.w("tensor", dv, pf_free[pfi])
                        for fi in range(32):
                            k.run("tensor", lambda e, fi=fi, tl=tl, pp=pp, sl=sl: e.matmul(
                                pp[:, :],
                                h2_half[:, fi * 1024 + tl * 128:fi * 1024 + (tl + 1) * 128],
                                wf_sl[sl][:, fi * 512:(fi + 1) * 512],
                                start=(fi == 0), stop=(fi == 31)),
                                inc=pe if fi == 31 else None)
                        k.w("vector", pe)
                        k.w("vector", s_x5[xsl])
                        k.w("vector", out_w[tl % 2])
                        k.run("vector", lambda e, pp=pp, tg=tg, fo=fo, tl=tl: e.scalar_tensor_tensor(
                            out=f1b[tl % 2][:, :], in0=pp[:, :], scalar=c_qsel[:, tg:tg + 1],
                            in1=c_rsel[:, fo * 512:(fo + 1) * 512], op0=Op.mult, op1=Op.mult))
                        k.run("vector", lambda e, tg=tg, fo=fo, tl=tl, xsl=xsl: e.scalar_tensor_tensor(
                            out=f2b[tl % 2][:, :], in0=c_brs[:, fo * 512:(fo + 1) * 512],
                            scalar=c_qsel[:, tg:tg + 1], in1=xr5[xsl][:, :],
                            op0=Op.mult, op1=Op.add))
                        k.drain("vector")
                        k.run("vector", lambda e, tl=tl: e.tensor_tensor(
                            out=ob5[tl % 2][:, :], in0=f1b[tl % 2][:, :], in1=f2b[tl % 2][:, :],
                            op=Op.add))
                        k.drain_inc("vector", dv)
                        xr5_free[xsl] = k.val(dv)
                        pf_free[pfi] = k.val(dv)
                        k.w("scalar", dv)
                        k.dma("scalar",
                              out[tg * 128:(tg + 1) * 128, fo * 512:(fo + 1) * 512],
                              ob5[tl % 2][:, :], out_w[tl % 2])
                    wf_free[sl] = k.val(pe)
            k.barrier("end")
            k.w("sync", out_w[0])
            k.w("sync", out_w[1])

        block = E(nc.Block())
        k.finalize(block)
    return nc


def kernel(L0_attn_x, L0_attn_Wq, L0_attn_Wk, L0_attn_Wv, L0_attn_Wo,
           L0_ffn_weight, L0_ffn_bias, L0_ffn_centroids, L0_ffn_assignments):
    x = np.ascontiguousarray(np.asarray(L0_attn_x, dtype=np.float32))
    Wq = np.asarray(L0_attn_Wq, dtype=np.float32)
    Wk = np.asarray(L0_attn_Wk, dtype=np.float32)
    Wv = np.asarray(L0_attn_Wv, dtype=np.float32)
    Wo = np.asarray(L0_attn_Wo, dtype=np.float32)
    Wf = np.asarray(L0_ffn_weight, dtype=np.float32)
    bias = np.asarray(L0_ffn_bias, dtype=np.float32)
    centroids = np.asarray(L0_ffn_centroids, dtype=np.float32)
    assignments = np.asarray(L0_ffn_assignments, dtype=np.int32)

    cosT, sinT = _rope_tables()
    idenM = np.eye(128, dtype=ml_dtypes.bfloat16)
    onesM = np.ones((128, 128), dtype=ml_dtypes.bfloat16)
    onescolM = np.ones((128, 1), dtype=ml_dtypes.bfloat16)
    onesk1M = np.ones((1, 128), dtype=ml_dtypes.bfloat16)
    iotaM = np.arange(64, dtype=np.float32).reshape(64, 1)
    kp = np.arange(128)[:, None]
    qf = np.arange(512)[None, :]
    masksM = np.ascontiguousarray(
        np.concatenate([(kp + 128 * i <= qf) for i in range(4)], axis=1).astype(ml_dtypes.bfloat16))
    bias_repM = np.ascontiguousarray(np.broadcast_to(bias[None, :], (128, D)))
    centM = np.ascontiguousarray(centroids.T)
    assignM = np.ascontiguousarray(assignments.reshape(1, D))

    nc = build_nc(dumps=tuple(DEBUG_DUMPS), upto=UPTO)
    in_maps = []
    for c in range(8):
        b, r = divmod(c, 2)
        in_maps.append(dict(
            x_seq=x[b],
            x_own=np.ascontiguousarray(x[b, r * TOWN:(r + 1) * TOWN, :]),
            wq=np.ascontiguousarray(Wq[r * DQ:(r + 1) * DQ, :].T),
            wk=np.ascontiguousarray(Wk[r * DKV:(r + 1) * DKV, :].T),
            wv=np.ascontiguousarray(Wv[r * DKV:(r + 1) * DKV, :].T),
            wo=np.ascontiguousarray(Wo[:, r * DQ:(r + 1) * DQ].T),
            wf=np.ascontiguousarray(Wf.T),
            bias_rep=bias_repM, cent=centM, assign=assignM,
            cosT=cosT, sinT=sinT, iden=idenM, ones=onesM, onescol=onescolM,
            onesk1=onesk1M, iota64=iotaM, masks=masksM,
            epsv=np.full((128, 1), EPS, dtype=np.float32),
        ))
    res = run_bass_kernel_spmd(nc, in_maps, core_ids=list(range(8)))
    globals()["_LAST_RESULTS"] = res
    outp = np.empty((4, T, D), dtype=np.float32)
    for c in range(8):
        b, r = divmod(c, 2)
        outp[b, r * TOWN:(r + 1) * TOWN, :] = res.results[c]["out"]
    return outp


# version 12
# speedup vs baseline: 1.1164x; 1.0523x over previous
import math
from contextlib import ExitStack

import numpy as np
import ml_dtypes

import concourse.bass as bass
import concourse.mybir as mybir
from concourse.bass_utils import run_bass_kernel_spmd
from concourse.alu_op_type import AluOpType

F32 = mybir.dt.float32
BF16 = mybir.dt.bfloat16
I32 = mybir.dt.int32
AF = mybir.ActivationFunctionType
AX = mybir.AxisListType
Op = AluOpType

D = 4096
T = 4096
HL = 16
KVL = 4
HD = 128
DQ = HL * HD
DKV = KVL * HD
TC = 1024
NBLK = T // TC
NJ = T // 512
SCALE = 1.0 / math.sqrt(HD)
EPS = 1e-5
TEMP_INV = 10.0
THRESH = 0.01
TOWN = 2048

DEBUG_DUMPS = []
UPTO = 99


class KB:
    ENGS = ("tensor", "vector", "scalar", "gpsimd", "sync")

    def __init__(self, nc, stack):
        self.nc = nc
        self.stack = stack
        self.streams = {e: [] for e in self.ENGS}
        self.sems = {}

    def sem(self, name):
        if name not in self.sems:
            h = self.stack.enter_context(self.nc.semaphore(name))
            self.sems[name] = [h, 0]
        return name

    def val(self, name):
        return self.sems[name][1]

    def em(self, eng, fn):
        self.streams[eng].append(fn)

    def w(self, eng, sname, v=None):
        if v is None:
            v = self.sems[sname][1]
        if v <= 0:
            return
        h = self.sems[sname][0]
        self.em(eng, lambda e, h=h, v=v: e.wait_ge(h, v))

    def inc(self, sname, n=1):
        s = self.sems[sname]
        s[1] += n
        return s[1]

    def run(self, eng, fn, inc=None, inc_n=1):
        if inc is not None:
            h = self.sems[inc][0]
            self.inc(inc, inc_n)
            self.em(eng, lambda e, fn=fn, h=h, n=inc_n: fn(e).then_inc(h, n))
        else:
            self.em(eng, lambda e, fn=fn: fn(e))

    def drain_inc(self, eng, sname, n=1):
        h = self.sems[sname][0]
        self.inc(sname, n)
        self.em(eng, lambda e, h=h, n=n: e.drain().then_inc(h, n))

    def drain(self, eng):
        self.em(eng, lambda e: e.drain())

    def dma(self, qeng, out_ap, in_ap, sname, slow=False):
        h = self.sems[sname][0]
        self.inc(sname, 16)
        if slow:
            self.em(qeng, lambda e, o=out_ap, i=in_ap, h=h: e.dma_start(
                out=o, in_=i, allow_slow_non_contiguous=True).then_inc(h, 16))
        else:
            self.em(qeng, lambda e, o=out_ap, i=in_ap, h=h: e.dma_start(
                out=o, in_=i).then_inc(h, 16))

    def barrier(self, tag):
        for eng in ("tensor", "vector", "scalar", "gpsimd"):
            self.drain_inc(eng, self.sem("bar_" + eng))
        self.run("sync", lambda e: e.nop(), inc=self.sem("bar_sync"))
        snapshot = [(n, s[1]) for n, s in self.sems.items()]
        for eng in self.ENGS:
            for n, v in snapshot:
                if v > 0:
                    h = self.sems[n][0]
                    self.em(eng, lambda e, h=h, v=v: e.wait_ge(h, v))

    def finalize(self, block):
        for eng in self.ENGS:
            fns = self.streams[eng]
            dec = getattr(block, eng)

            def section(e, fns=fns):
                for f in fns:
                    f(e)
            dec(section)


def _rope_tables():
    base = 500000.0
    inv_freq = 1.0 / base ** (np.arange(0, HD, 2, dtype=np.float32) / HD)
    factor, low_f, high_f, old_ctx = 32.0, 1.0, 4.0, 8192
    low_wl, high_wl = old_ctx / low_f, old_ctx / high_f
    wavelen = 2 * math.pi / inv_freq
    inv_llama = np.where(wavelen > low_wl, inv_freq / factor, inv_freq)
    smooth = (old_ctx / wavelen - low_f) / (high_f - low_f)
    smoothed = (1 - smooth) * inv_llama / factor + smooth * inv_llama
    is_med = ~(wavelen < high_wl) & ~(wavelen > low_wl)
    inv = np.where(is_med, smoothed, inv_llama).astype(np.float32)
    pos = np.arange(T, dtype=np.float32)
    freqs = pos[:, None] * inv[None, :]
    emb = np.concatenate([freqs, freqs], axis=-1)
    cosT = np.ascontiguousarray(np.cos(emb).astype(np.float32).T)
    sinT = np.ascontiguousarray(np.sin(emb).astype(np.float32).T)
    return cosT, sinT



def _cdma(k, qeng, sbuf_ap, dram2d, C, sem):
    in3 = dram2d.rearrange("(c p) n -> p c n", p=128)
    n = in3.shape[2]
    out3 = sbuf_ap.rearrange("p (c n) -> p c n", c=C)
    k.dma(qeng, out3, in3, sem)


def _cdma_out(k, qeng, dram2d, sbuf_ap, C, sem):
    out3 = dram2d.rearrange("(c p) n -> p c n", p=128)
    in3 = sbuf_ap.rearrange("p (c n) -> p c n", c=C)
    k.dma(qeng, out3, in3, sem)


def build_nc(dumps=(), upto=99):
    nc = bass.Bass("TRN2", target_bir_lowering=False, debug=False)

    def din(name, shape, dt):
        return nc.dram_tensor(name, shape, dt, kind="ExternalInput").ap()

    dump_list = []

    def dint(name, shape, dt):
        if name in ("wo_part", "rs_out", "cnt_in", "cnt_out"):
            t = nc.dram_tensor(name, shape, dt, kind="Internal").ap()
            if name in dumps:
                dump_list.append((name, t, shape, dt))
            return t
        kind = "ExternalOutput" if name in dumps else "Internal"
        return nc.dram_tensor(name, shape, dt, kind=kind).ap()

    x_seq = din("x_seq", [T, D], F32)
    x_own = din("x_own", [TOWN, D], F32)
    wq = din("wq", [D, DQ], F32)
    wk = din("wk", [D, DKV], F32)
    wv = din("wv", [D, DKV], F32)
    wo = din("wo", [DQ, D], F32)
    wf = din("wf", [D, D], F32)
    bias_rep = din("bias_rep", [128, D], F32)
    cent = din("cent", [D, 64], F32)
    assign = din("assign", [1, D], I32)
    cosT = din("cosT", [128, T], F32)
    sinT = din("sinT", [128, T], F32)
    iden = din("iden", [128, 128], BF16)
    ones = din("ones", [128, 128], BF16)
    onescol = din("onescol", [128, 1], BF16)
    onesk1 = din("onesk1", [1, 128], BF16)
    iota64 = din("iota64", [64, 1], F32)
    masks = din("masks", [128, 4 * 512], BF16)
    epsv = din("epsv", [128, 1], F32)

    cc_names = ("wo_part", "rs_out", "cnt_in", "cnt_out")
    q_sp = dint("q_sp", [DQ, T], BF16)
    k_sp = dint("k_sp", [DKV, T], BF16)
    v_sp = dint("v_sp", [T, DKV], BF16)
    wo_part = dint("wo_part", [8, 2 * TOWN, 512], BF16)
    rs_out = dint("rs_out", [8, TOWN, 512], BF16)
    xres_sp = dint("xres_sp", [TOWN, D], F32)
    h2_sp = dint("h2_sp", [D, TOWN], BF16)
    cnt_in = dint("cnt_in", [1, 64], F32)
    cnt_out = dint("cnt_out", [1, 64], F32)
    out = nc.dram_tensor("out", [TOWN, D], F32, kind="ExternalOutput").ap()

    with ExitStack() as stack:
        E = stack.enter_context
        k = KB(nc, stack)
        c_iden = E(nc.sbuf_tensor("c_iden", [128, 128], BF16))
        c_ones = E(nc.sbuf_tensor("c_ones", [128, 128], BF16))
        c_onescol = E(nc.sbuf_tensor("c_onescol", [128, 1], BF16))
        c_onesk1 = E(nc.sbuf_tensor("c_onesk1", [1, 128], BF16))
        c_iota = E(nc.sbuf_tensor("c_iota", [64, 1], F32))
        c_masks = E(nc.sbuf_tensor("c_masks", [128, 4 * 512], BF16))
        c_cent = E(nc.sbuf_tensor("c_cent", [128, 32 * 64], BF16))
        c_qsel = E(nc.sbuf_tensor("c_qsel", [128, 16], F32))
        c_eps = E(nc.sbuf_tensor("c_eps", [128, 1], F32))

        ld_init = k.sem("ld_init")
        k.dma("sync", c_iden[:, :], iden, ld_init)
        k.dma("sync", c_ones[:, :], ones, ld_init)
        k.dma("sync", c_onescol[:, :], onescol, ld_init)
        k.dma("sync", c_onesk1[:, :], onesk1, ld_init)
        k.dma("sync", c_iota[:, :], iota64, ld_init)
        k.dma("sync", c_masks[:, :], masks, ld_init)
        k.dma("sync", c_eps[:, :], epsv, ld_init)
        _cdma(k, "gpsimd", c_cent[:, :], cent, 32, ld_init)
        for e in ("tensor", "vector", "scalar"):
            k.w(e, ld_init)

        dv = k.sem("dv")
        ac = k.sem("ac")
        pe = k.sem("pe")
        cc_s = k.sem("cc_s")

        def zero_out_and_finish():
            zbuf = E(nc.sbuf_tensor("zbuf", [128, D], F32))
            zs = k.sem("z_w")
            k.run("vector", lambda e: e.memset(zbuf[:, :], 0.0))
            k.drain_inc("vector", dv)
            k.w("sync", dv)
            for ti in range(16):
                k.dma("sync", out[ti * 128:(ti + 1) * 128, :], zbuf[:, :], zs)
            k.barrier("zend")
            block = E(nc.Block())
            k.finalize(block)
            return nc

        with ExitStack() as p1:
            E1 = p1.enter_context
            h_fm = E1(nc.sbuf_tensor("h_fm", [128, 32 * TC], BF16))
            x_in = [E1(nc.sbuf_tensor(f"x_in{i}", [128, D], F32)) for i in range(2)]
            h_tm = [E1(nc.sbuf_tensor(f"h_tm{i}", [128, D], BF16)) for i in range(2)]
            wq_sl = [E1(nc.sbuf_tensor(f"wq_sl{i}", [128, 32 * 256], BF16)) for i in range(2)]
            wv_sl = E1(nc.sbuf_tensor("wv_sl", [128, 32 * 512], BF16))
            cs_sl = [E1(nc.sbuf_tensor(f"cs_sl{i}", [128, 2 * 512], F32)) for i in range(2)]
            r1 = E1(nc.sbuf_tensor("r1", [128, 512], F32))
            rAB = E1(nc.sbuf_tensor("rAB", [128, 512], F32))
            qko = [E1(nc.sbuf_tensor(f"qko{i}", [128, 512], BF16)) for i in range(2)]
            vst = [E1(nc.sbuf_tensor(f"vst{i}", [128, 512], BF16)) for i in range(2)]
            statb = [E1(nc.sbuf_tensor(f"statb{i}", [128, 8], F32)) for i in range(2)]
            p_a = E1(nc.psum_tensor("p_a", [128, 512], F32))
            p_b = E1(nc.psum_tensor("p_b", [128, 512], F32))
            p_x = E1(nc.psum_tensor("p_x", [128, 512], BF16))
            p_mm = [p_a, p_b]

            s_x = [k.sem(f"s_x{i}") for i in range(2)]
            s_wq = [k.sem(f"s_wq{i}") for i in range(2)]
            s_wv = k.sem("s_wv")
            s_cs = [k.sem(f"s_cs{i}") for i in range(2)]
            stqk = [k.sem(f"stqk{i}") for i in range(2)]
            v_wsem = [k.sem(f"v_w{i}") for i in range(2)]

            xin_free = [(0, 0), (0, 0)]
            htm_free = [0, 0]
            pmm_free = [0, 0]
            wq_free = [0, 0]
            evac_cnt = [0]

            def ln_tile(blk, ti):
                t = blk * 8 + ti
                sl = t % 2
                dvw, acw = xin_free[sl]
                k.w("sync", dv, dvw)
                k.w("sync", ac, acw)
                k.dma("sync", x_in[sl][:, :], x_seq[t * 128:(t + 1) * 128, :], s_x[sl])
                k.w("vector", s_x[sl])
                k.w("scalar", s_x[sl])
                k.w("scalar", pe, htm_free[sl])
                k.run("vector", lambda e, sl=sl: e.reduce_sum(
                    out=statb[sl][:, 0:1], in_=x_in[sl][:, :], axis=AX.X))
                k.run("scalar", lambda e, sl=sl: e.activation(
                    out=h_tm[sl][:, :], in_=x_in[sl][:, :], func=AF.Square,
                    accum_out=statb[sl][:, 2:3]))
                k.drain_inc("scalar", ac)
                k.w("vector", ac)
                k.drain("vector")
                k.run("vector", lambda e: e.tensor_scalar(
                    out=statb[sl][:, 1:2], in0=statb[sl][:, 0:1], scalar1=1.0 / D, scalar2=None,
                    op0=Op.mult, op1=Op.bypass))
                k.drain("vector")
                k.run("vector", lambda e: e.tensor_tensor(
                    out=statb[sl][:, 3:4], in0=statb[sl][:, 1:2], in1=statb[sl][:, 1:2], op=Op.mult))
                k.drain("vector")
                k.run("vector", lambda e: e.tensor_scalar(
                    out=statb[sl][:, 4:5], in0=statb[sl][:, 2:3], scalar1=1.0 / D, scalar2=statb[sl][:, 3:4],
                    op0=Op.mult, op1=Op.subtract))
                k.drain_inc("vector", dv)
                k.w("scalar", dv)
                k.run("scalar", lambda e: e.activation(
                    out=statb[sl][:, 5:6], in_=statb[sl][:, 4:5], func=AF.Ln, bias=c_eps[:, :]))
                k.run("scalar", lambda e: e.activation(
                    out=statb[sl][:, 6:7], in_=statb[sl][:, 5:6], func=AF.Exp, scale=-0.5))
                k.drain_inc("scalar", ac)
                k.w("vector", ac)
                k.run("vector", lambda e, sl=sl: e.tensor_scalar(
                    out=h_tm[sl][:, :], in0=x_in[sl][:, :],
                    scalar1=statb[sl][:, 1:2], scalar2=statb[sl][:, 6:7],
                    op0=Op.subtract, op1=Op.mult))
                k.drain_inc("vector", dv)
                xin_free[sl] = (k.val(dv), k.val(ac))
                k.w("tensor", dv)
                for grp in range(8):
                    for u in range(4):
                        ch = grp * 4 + u
                        k.run("tensor", lambda e, ch=ch, u=u, sl=sl: e.transpose(
                            p_x[:, u * 128:(u + 1) * 128],
                            h_tm[sl][:, ch * 128:(ch + 1) * 128], c_iden[:, :]),
                            inc=pe if u == 3 else None)
                    k.w("vector", pe)
                    for u in range(4):
                        ch = grp * 4 + u
                        k.run("vector", lambda e, ch=ch, u=u, ti=ti: e.tensor_copy(
                            h_fm[:, ch * TC + ti * 128:ch * TC + (ti + 1) * 128],
                            p_x[:, u * 128:(u + 1) * 128]))
                    k.drain_inc("vector", dv)
                    if grp < 7:
                        k.w("tensor", dv)
                htm_free[sl] = k.val(pe)

            def proj_and_rope(slab, slab_cols, fout_off, t512_abs, pp_i, dst, row_off, st_sl):
                pp = p_mm[pp_i]
                blk_t512 = t512_abs % 2
                k.w("tensor", dv, pmm_free[pp_i])
                for fi in range(32):
                    k.run("tensor", lambda e, fi=fi, pp=pp, slab=slab, slab_cols=slab_cols,
                          fout_off=fout_off, blk_t512=blk_t512: e.matmul(
                        pp[:, :],
                        slab[:, fi * slab_cols + fout_off:fi * slab_cols + fout_off + 128],
                        h_fm[:, fi * TC + blk_t512 * 512:fi * TC + blk_t512 * 512 + 512],
                        start=(fi == 0), stop=(fi == 31)),
                        inc=pe if fi == 31 else None)
                csl = cs_sl[t512_abs % 2]
                ob = qko[st_sl]
                k.w("vector", pe)
                k.w("vector", s_cs[t512_abs % 2])
                k.w("vector", stqk[st_sl])
                k.run("vector", lambda e, pp=pp, csl=csl: e.tensor_tensor(
                    out=r1[:, :], in0=pp[:, :], in1=csl[:, 0:512], op=Op.mult))
                k.run("vector", lambda e, pp=pp, csl=csl: e.tensor_tensor(
                    out=rAB[0:64, :], in0=pp[64:128, :], in1=csl[0:64, 512:1024], op=Op.mult))
                k.run("vector", lambda e, pp=pp, csl=csl: e.tensor_tensor(
                    out=rAB[64:128, :], in0=pp[0:64, :], in1=csl[64:128, 512:1024], op=Op.mult))
                k.drain("vector")
                k.run("vector", lambda e, ob=ob: e.tensor_tensor(
                    out=ob[0:64, :], in0=r1[0:64, :], in1=rAB[0:64, :], op=Op.subtract))
                k.run("vector", lambda e, ob=ob: e.tensor_tensor(
                    out=ob[64:128, :], in0=r1[64:128, :], in1=rAB[64:128, :], op=Op.add))
                k.drain_inc("vector", dv)
                pmm_free[pp_i] = k.val(dv)
                k.w("scalar", dv)
                k.dma("scalar", dst[row_off:row_off + 128,
                                    t512_abs * 512:(t512_abs + 1) * 512], ob[:, :], stqk[st_sl])

            if upto >= 1:
                _cdma(k, "gpsimd", wv_sl[:, :], wv, 32, s_wv)
                for blk in range(NBLK):
                    for ti in range(8):
                        ln_tile(blk, ti)
                    for t5 in range(2):
                        t512_abs = blk * 2 + t5
                        sl = t512_abs % 2
                        if t512_abs >= 2:
                            k.w("sync", dv)
                        k.dma("sync", cs_sl[sl][:, 0:512],
                              cosT[:, t512_abs * 512:(t512_abs + 1) * 512], s_cs[sl])
                        k.dma("sync", cs_sl[sl][:, 512:1024],
                              sinT[:, t512_abs * 512:(t512_abs + 1) * 512], s_cs[sl])
                    for sq in range(10):
                        sl = sq % 2
                        wsrc = wq if sq < 8 else wk
                        woff = sq * 256 if sq < 8 else (sq - 8) * 256
                        k.w("gpsimd", pe, wq_free[sl])
                        _cdma(k, "gpsimd", wq_sl[sl][:, :],
                              wsrc[:, woff:woff + 256], 32, s_wq[sl])
                        k.w("tensor", s_wq[sl])
                        for hh in range(2):
                            if sq < 8:
                                dst, row = q_sp, (sq * 2 + hh) * 128
                            else:
                                dst, row = k_sp, ((sq - 8) * 2 + hh) * 128
                            for t5 in range(2):
                                t512_abs = blk * 2 + t5
                                i = evac_cnt[0]
                                proj_and_rope(wq_sl[sl], 256, hh * 128, t512_abs,
                                              i % 2, dst, row, i % 2)
                                evac_cnt[0] += 1
                        wq_free[sl] = k.val(pe)
                    k.w("tensor", s_wv)
                    for ti in range(8):
                        t = blk * 8 + ti
                        ppi = ti % 2
                        pp = p_mm[ppi]
                        k.w("tensor", dv, pmm_free[ppi])
                        for fi in range(32):
                            k.run("tensor", lambda e, fi=fi, ti=ti, pp=pp: e.matmul(
                                pp[:, :],
                                h_fm[:, fi * TC + ti * 128:fi * TC + (ti + 1) * 128],
                                wv_sl[:, fi * 512:(fi + 1) * 512],
                                start=(fi == 0), stop=(fi == 31)),
                                inc=pe if fi == 31 else None)
                        k.w("vector", pe)
                        k.w("vector", v_wsem[ti % 2])
                        k.run("vector", lambda e, ti=ti, pp=pp: e.tensor_copy(
                            vst[ti % 2][:, :], pp[:, :]))
                        k.drain_inc("vector", dv)
                        pmm_free[ppi] = k.val(dv)
                        k.w("scalar", dv)
                        k.dma("scalar", v_sp[t * 128:(t + 1) * 128, :], vst[ti % 2][:, :],
                              v_wsem[ti % 2])
            k.barrier("p1_end")

        if upto < 2:
            return zero_out_and_finish()

        with ExitStack() as p2:
            E2 = p2.enter_context
            K_res = E2(nc.sbuf_tensor("K_res", [128, 4 * T], BF16))
            V_res = E2(nc.sbuf_tensor("V_res", [128, 32 * DKV], BF16))
            O_sb = E2(nc.sbuf_tensor("O_sb", [128, 64 * 512], BF16))
            q_t = [E2(nc.sbuf_tensor(f"q_t{i}", [128, 4 * 512], BF16)) for i in range(2)]
            att2 = [E2(nc.sbuf_tensor(f"att2_{i}", [128, 1024], BF16)) for i in range(2)]
            s_lnb = [E2(nc.sbuf_tensor(f"s_lnb{i}", [128, 512], F32)) for i in range(2)]
            s_rec = [E2(nc.sbuf_tensor(f"s_rec{i}", [128, 512], F32)) for i in range(2)]
            wo_sl = [E2(nc.sbuf_tensor(f"wo_sl{i}", [128, 16 * 512], BF16)) for i in range(2)]
            wo_st = [E2(nc.sbuf_tensor(f"wo_st{i}", [128, 512], BF16)) for i in range(2)]
            p_s2 = [E2(nc.psum_tensor(f"p_s2_{i}", [128, 1024], F32)) for i in range(2)]
            p_o = [E2(nc.psum_tensor(f"p_o{i}", [128, 512], F32)) for i in range(2)]
            p_d = [E2(nc.psum_tensor(f"p_d{i}", [128, 512], F32)) for i in range(2)]

            s_kv = k.sem("s_kv")
            s_q = [k.sem(f"s_q{i}") for i in range(2)]
            s_wo = [k.sem(f"s_wo{i}") for i in range(2)]
            wop_w = [k.sem(f"wop_w{i}") for i in range(2)]

            k.w("sync", "stqk0"); k.w("sync", "stqk1")
            k.w("sync", "v_w0"); k.w("sync", "v_w1")
            for g in range(KVL):
                k.dma("sync", K_res[:, g * T:(g + 1) * T], k_sp[g * 128:(g + 1) * 128, :], s_kv)
            _cdma(k, "sync", V_res[:, :], v_sp, 32, s_kv)
            k.w("tensor", s_kv)

            att_last = [0, 0]
            att_ready = [("ac", 0), ("ac", 0)]
            ps_last = [0, 0]
            po_free = [0, 0]
            pd_free = [0, 0]
            qt_free = [0, 0]
            osb_dv = {}
            wost_free = [0, 0]

            def attention_j(j):
                jj = j % 4
                nkt = 4 * j + 4
                npair = nkt // 2

                def load_group(g):
                    qsl = g % 2
                    k.w("sync", pe, qt_free[qsl])
                    _cdma(k, "sync", q_t[qsl][:, :],
                          q_sp[g * 512:(g + 1) * 512, j * 512:(j + 1) * 512], 4, s_q[qsl])

                load_group(0)
                load_group(1)
                for h in range(HL):
                    g = h // 4
                    qsl = g % 2
                    qcols = (h % 4) * 512
                    po = p_o[h % 2]
                    pd = p_d[h % 2]
                    if h % 4 == 0:
                        k.w("tensor", s_q[qsl])

                    def emit_pv_den(gp2, si2, po=po, pd=pd, g=g, nkt=nkt, h=h):
                        sem2, v2 = att_ready[si2]
                        k.w("tensor", sem2, v2)
                        for u in (0, 1):
                            kt = 2 * gp2 + u
                            if kt == 0:
                                k.w("tensor", dv, po_free[h % 2])
                                k.w("tensor", ac, pd_free[h % 2])
                            k.run("tensor", lambda e, kt=kt, u=u, si2=si2, po=po, g=g, nkt=nkt: e.matmul(
                                po[:, :],
                                V_res[:, kt * DKV + g * 128:kt * DKV + (g + 1) * 128],
                                att2[si2][:, u * 512:(u + 1) * 512],
                                start=(kt == 0), stop=(kt == nkt - 1)), inc=pe)
                            k.run("tensor", lambda e, u=u, kt=kt, si2=si2, pd=pd, nkt=nkt: e.matmul(
                                pd[:, :], c_ones[:, :],
                                att2[si2][:, u * 512:(u + 1) * 512],
                                start=(kt == 0), stop=(kt == nkt - 1)), inc=pe)
                        att_last[si2] = k.val(pe)

                    pend = None
                    for gp in range(npair):
                        si = gp % 2
                        k.w("tensor", ac, ps_last[si])
                        for u in (0, 1):
                            kt = 2 * gp + u
                            k.run("tensor", lambda e, kt=kt, u=u, si=si, g=g, qsl=qsl, qcols=qcols: e.matmul(
                                p_s2[si][:, u * 512:(u + 1) * 512],
                                K_res[:, g * T + kt * 128:g * T + (kt + 1) * 128],
                                q_t[qsl][:, qcols:qcols + 512], start=True, stop=True), inc=pe)
                        k.w("scalar", pe)
                        k.w("scalar", pe, att_last[si])
                        k.run("scalar", lambda e, si=si: e.activation(
                            out=att2[si][:, :], in_=p_s2[si][:, :], func=AF.Exp, scale=SCALE))
                        k.drain_inc("scalar", ac)
                        ps_last[si] = k.val(ac)
                        if gp >= 2 * j:
                            k.w("vector", ac)
                            for u in (0, 1):
                                mi = 2 * gp + u - 4 * j
                                k.run("vector", lambda e, si=si, u=u, mi=mi: e.tensor_tensor(
                                    out=att2[si][:, u * 512:(u + 1) * 512],
                                    in0=att2[si][:, u * 512:(u + 1) * 512],
                                    in1=c_masks[:, mi * 512:(mi + 1) * 512], op=Op.mult))
                            k.drain_inc("vector", dv)
                            att_ready[si] = (dv, k.val(dv))
                        else:
                            att_ready[si] = (ac, k.val(ac))
                        if pend is not None:
                            emit_pv_den(*pend)
                        pend = (gp, si)
                    emit_pv_den(*pend)
                    if h % 4 == 3:
                        qt_free[qsl] = k.val(pe)
                        if g + 2 < KVL:
                            load_group(g + 2)
                    k.w("scalar", pe)
                    k.run("scalar", lambda e, pd=pd, h=h: e.activation(
                        out=s_lnb[h % 2][:, :], in_=pd[:, :], func=AF.Ln))
                    k.run("scalar", lambda e, h=h: e.activation(
                        out=s_rec[h % 2][:, :], in_=s_lnb[h % 2][:, :], func=AF.Exp, scale=-1.0))
                    k.drain_inc("scalar", ac)
                    pd_free[h % 2] = k.val(ac)
                    k.w("vector", ac)
                    slot = (jj * 16 + h) * 512
                    k.run("vector", lambda e, po=po, h=h, slot=slot: e.tensor_tensor(
                        out=O_sb[:, slot:slot + 512], in0=po[:, :], in1=s_rec[h % 2][:, :],
                        op=Op.mult))
                    k.drain_inc("vector", dv)
                    po_free[h % 2] = k.val(dv)
                    osb_dv[(jj, h)] = k.val(dv)

            wosl_free = [0, 0]
            wop_mark = {}

            def wo_half(half):
                for fo in range(8):
                    sl = fo % 2
                    k.w("gpsimd", pe, wosl_free[sl])
                    _cdma(k, "gpsimd", wo_sl[sl][:, :],
                          wo[:, fo * 512:(fo + 1) * 512], 16, s_wo[sl])
                    k.w("tensor", s_wo[sl])
                    for tl in range(16):
                        jj = tl // 4
                        qq = (tl % 4) * 128
                        pwi = tl % 2
                        pp = p_o[pwi]
                        k.w("tensor", dv, osb_dv[(jj, HL - 1)])
                        k.w("tensor", dv, po_free[pwi])
                        k.w("tensor", dv, wost_free[pwi])
                        for a in range(16):
                            k.run("tensor", lambda e, a=a, jj=jj, qq=qq, pp=pp, sl=sl: e.matmul(
                                pp[:, :],
                                O_sb[:, (jj * 16 + a) * 512 + qq:(jj * 16 + a) * 512 + qq + 128],
                                wo_sl[sl][:, a * 512:(a + 1) * 512],
                                start=(a == 0), stop=(a == 15)), inc=pe if a == 15 else None)
                        k.w("vector", pe)
                        k.w("vector", wop_w[pwi])
                        k.run("vector", lambda e, pp=pp, pwi=pwi: e.tensor_copy(
                            wo_st[pwi][:, :], pp[:, :]))
                        k.drain_inc("vector", dv)
                        wost_free[pwi] = k.val(dv)
                        po_free[pwi] = k.val(dv)
                        k.w("scalar", dv)
                        k.dma("scalar",
                              wo_part[fo, half * TOWN + tl * 128:half * TOWN + (tl + 1) * 128, :],
                              wo_st[pwi][:, :], wop_w[pwi])
                    wosl_free[sl] = k.val(pe)
                    if half == 1:
                        k.w("gpsimd", wop_w[0])
                        k.w("gpsimd", wop_w[1])
                        k.run("gpsimd", lambda e, fo=fo: e.collective_compute(
                            "ReduceScatter", Op.add,
                            replica_groups=[[0, 1], [2, 3], [4, 5], [6, 7]],
                            ins=[wo_part[fo].opt()],
                            outs=[rs_out[fo].opt()]), inc=cc_s)

            for j in range(4):
                attention_j(j)
            wo_half(0)
            for j in range(4, 8):
                attention_j(j)
            wo_half(1)
            if dump_list:
                k.w("sync", cc_s)
                dsem = k.sem("dump_w")
                for name, t, shape, dt in dump_list:
                    if name in ("wo_part", "rs_out"):
                        d = nc.dram_tensor(name + "_dump", shape, dt, kind="ExternalOutput").ap()
                        k.dma("sync", d, t, dsem)
            k.barrier("p2_end")

        if upto < 3:
            return zero_out_and_finish()

        with ExitStack() as p3:
            E3 = p3.enter_context
            rs_t = [E3(nc.sbuf_tensor(f"rs_t{i}", [128, D], BF16)) for i in range(2)]
            xo_t = [E3(nc.sbuf_tensor(f"xo_t{i}", [128, D], F32)) for i in range(2)]
            xr_t = [E3(nc.sbuf_tensor(f"xr_t{i}", [128, D], F32)) for i in range(2)]
            h2_tm = [E3(nc.sbuf_tensor(f"h2_tm{i}", [128, D], BF16)) for i in range(2)]
            h2_st = E3(nc.sbuf_tensor("h2_st", [128, 32 * 512], BF16))
            stat2b = [E3(nc.sbuf_tensor(f"stat2b{i}", [128, 8], F32)) for i in range(2)]
            expd = E3(nc.sbuf_tensor("expd", [128, 64], F32))
            hits = E3(nc.sbuf_tensor("hits", [128, 64], BF16))
            cnt_sb = E3(nc.sbuf_tensor("cnt_sb", [1, 64], F32))
            p_x2 = E3(nc.psum_tensor("p_x2", [128, 512], BF16))
            p_dt = [E3(nc.psum_tensor(f"p_dt{i}", [128, 64], F32)) for i in range(2)]
            p_cnt = E3(nc.psum_tensor("p_cnt", [1, 64], F32))

            s_rs = [k.sem(f"s_rs{i}") for i in range(2)]
            s_xo = [k.sem(f"s_xo{i}") for i in range(2)]
            xres_w = [k.sem(f"xres_w{i}") for i in range(2)]
            h2_w = k.sem("h2_w")
            cntin_w = k.sem("cntin_w")

            xrt_free = [(0, 0), (0, 0)]
            pdt_free = [0, 0]
            hits_free = [0]
            htm2_free = [0, 0]

            k.w("sync", cc_s)
            for tl in range(16):
                sl = tl % 2
                dvw, acw = xrt_free[sl]
                k.w("sync", dv, dvw)
                k.w("sync", ac, acw)
                k.dma("sync", rs_t[sl][:, :].rearrange("p (c n) -> p c n", c=8),
                      rs_out[:, tl * 128:(tl + 1) * 128, :].transpose([1, 0, 2]), s_rs[sl])
                k.dma("sync", xo_t[sl][:, :], x_own[tl * 128:(tl + 1) * 128, :], s_xo[sl])
                k.w("vector", s_rs[sl])
                k.w("vector", s_xo[sl])
                k.w("vector", xres_w[sl])
                k.run("vector", lambda e, sl=sl: e.tensor_tensor(
                    out=xr_t[sl][:, :], in0=rs_t[sl][:, :], in1=xo_t[sl][:, :], op=Op.add))
                k.drain_inc("vector", dv)
                k.w("scalar", dv)
                k.dma("scalar", xres_sp[tl * 128:(tl + 1) * 128, :], xr_t[sl][:, :], xres_w[sl])
                k.w("scalar", pe, htm2_free[sl])
                k.run("vector", lambda e, sl=sl: e.reduce_sum(
                    out=stat2b[sl][:, 0:1], in_=xr_t[sl][:, :], axis=AX.X))
                k.run("scalar", lambda e, sl=sl: e.activation(
                    out=h2_tm[sl][:, :], in_=xr_t[sl][:, :], func=AF.Square,
                    accum_out=stat2b[sl][:, 2:3]))
                k.drain_inc("scalar", ac)
                k.w("vector", ac)
                k.drain("vector")
                k.run("vector", lambda e, sl=sl: e.tensor_scalar(
                    out=stat2b[sl][:, 1:2], in0=stat2b[sl][:, 0:1], scalar1=1.0 / D, scalar2=None,
                    op0=Op.mult, op1=Op.bypass))
                k.drain("vector")
                k.run("vector", lambda e, sl=sl: e.tensor_tensor(
                    out=stat2b[sl][:, 3:4], in0=stat2b[sl][:, 1:2], in1=stat2b[sl][:, 1:2], op=Op.mult))
                k.drain("vector")
                k.run("vector", lambda e, sl=sl: e.tensor_scalar(
                    out=stat2b[sl][:, 4:5], in0=stat2b[sl][:, 2:3], scalar1=1.0 / D, scalar2=stat2b[sl][:, 3:4],
                    op0=Op.mult, op1=Op.subtract))
                k.drain_inc("vector", dv)
                k.w("scalar", dv)
                k.run("scalar", lambda e, sl=sl: e.activation(
                    out=stat2b[sl][:, 5:6], in_=stat2b[sl][:, 4:5], func=AF.Ln, bias=c_eps[:, :]))
                k.run("scalar", lambda e, sl=sl: e.activation(
                    out=stat2b[sl][:, 6:7], in_=stat2b[sl][:, 5:6], func=AF.Exp, scale=-0.5))
                k.drain_inc("scalar", ac)
                k.w("vector", ac)
                k.run("vector", lambda e, sl=sl: e.tensor_scalar(
                    out=h2_tm[sl][:, :], in0=xr_t[sl][:, :],
                    scalar1=stat2b[sl][:, 1:2], scalar2=stat2b[sl][:, 6:7],
                    op0=Op.subtract, op1=Op.mult))
                k.drain_inc("vector", dv)
                xrt_free[sl] = (k.val(dv), k.val(ac))
                k.w("tensor", dv)
                if tl % 4 == 0:
                    k.w("tensor", h2_w)
                for grp in range(8):
                    for u in range(4):
                        ch = grp * 4 + u
                        k.run("tensor", lambda e, ch=ch, u=u, sl=sl: e.transpose(
                            p_x2[:, u * 128:(u + 1) * 128],
                            h2_tm[sl][:, ch * 128:(ch + 1) * 128], c_iden[:, :]),
                            inc=pe if u == 3 else None)
                    k.w("vector", pe)
                    for u in range(4):
                        ch = grp * 4 + u
                        k.run("vector", lambda e, ch=ch, u=u, tl=tl: e.tensor_copy(
                            h2_st[:, ch * 512 + (tl % 4) * 128:ch * 512 + (tl % 4) * 128 + 128],
                            p_x2[:, u * 128:(u + 1) * 128]))
                    k.drain_inc("vector", dv)
                    if grp < 7:
                        k.w("tensor", dv)
                htm2_free[sl] = k.val(pe)
                pdt = p_dt[tl % 2]
                k.w("tensor", dv)
                k.w("tensor", ac, pdt_free[tl % 2])
                for fi in range(32):
                    k.run("tensor", lambda e, fi=fi, tl=tl, pdt=pdt: e.matmul(
                        pdt[:, :],
                        h2_st[:, fi * 512 + (tl % 4) * 128:fi * 512 + (tl % 4) * 128 + 128],
                        c_cent[:, fi * 64:(fi + 1) * 64],
                        start=(fi == 0), stop=(fi == 31)), inc=pe if fi == 31 else None)
                k.w("vector", pe)
                k.w("vector", pe, hits_free[0])
                k.run("vector", lambda e, pdt=pdt, sl=sl: e.reduce_max(
                    out=stat2b[sl][:, 7:8], in_=pdt[:, :], axis=AX.X, negate=True))
                k.drain("vector")
                k.run("vector", lambda e, sl=sl: e.tensor_scalar(
                    out=stat2b[sl][:, 7:8], in0=stat2b[sl][:, 7:8], scalar1=TEMP_INV, scalar2=None,
                    op0=Op.mult, op1=Op.bypass))
                k.drain_inc("vector", dv)
                k.w("scalar", dv)
                k.run("scalar", lambda e, pdt=pdt, sl=sl: e.activation(
                    out=expd[:, :], in_=pdt[:, :], func=AF.Exp, scale=TEMP_INV,
                    bias=stat2b[sl][:, 7:8], accum_out=stat2b[sl][:, 0:1]))
                k.drain_inc("scalar", ac)
                pdt_free[tl % 2] = k.val(ac)
                k.w("vector", ac)
                k.run("vector", lambda e, sl=sl: e.tensor_scalar(
                    out=stat2b[sl][:, 1:2], in0=stat2b[sl][:, 0:1], scalar1=THRESH, scalar2=None,
                    op0=Op.mult, op1=Op.bypass))
                k.drain("vector")
                k.run("vector", lambda e, sl=sl: e.tensor_scalar(
                    out=hits[:, :], in0=expd[:, :], scalar1=stat2b[sl][:, 1:2], scalar2=None,
                    op0=Op.is_gt, op1=Op.bypass))
                k.drain("vector")
                k.run("vector", lambda e, tl=tl: e.reduce_max(
                    out=c_qsel[:, tl:tl + 1], in_=hits[:, :], axis=AX.X))
                k.drain_inc("vector", dv)
                k.w("tensor", dv)
                k.run("tensor", lambda e, tl=tl: e.matmul(
                    p_cnt[:, :], c_onescol[:, :], hits[:, :],
                    start=(tl == 0), stop=(tl == 15)), inc=pe)
                hits_free[0] = k.val(pe)
                if tl % 4 == 3:
                    k.w("scalar", dv)
                    _cdma_out(k, "scalar",
                          h2_sp[:, (tl - 3) * 128:(tl + 1) * 128], h2_st[:, :], 32, h2_w)

            k.w("vector", pe)
            k.run("vector", lambda e: e.tensor_copy(cnt_sb[:, :], p_cnt[:, :]))
            k.drain_inc("vector", dv)
            k.w("gpsimd", dv)
            k.dma("gpsimd", cnt_in, cnt_sb[:, :], cntin_w)
            k.w("gpsimd", cntin_w)
            k.run("gpsimd", lambda e: e.collective_compute(
                "AllReduce", Op.add, replica_groups=[list(range(8))],
                ins=[cnt_in.opt()], outs=[cnt_out.opt()]), inc=cc_s)
            if dump_list:
                k.w("sync", cc_s)
                dsem2 = k.sem("dump_w2")
                for name, t, shape, dt in dump_list:
                    if name == "cnt_out":
                        d = nc.dram_tensor(name + "_dump", shape, dt, kind="ExternalOutput").ap()
                        k.dma("sync", d, t, dsem2)
            k.barrier("p3_end")

        c_rsel = E(nc.sbuf_tensor("c_rsel", [128, D], BF16))
        c_brs = E(nc.sbuf_tensor("c_brs", [128, D], F32))
        with ExitStack() as p4:
            E4 = p4.enter_context
            cnt_col = E4(nc.sbuf_tensor("cnt_col", [64, 1], F32))
            csel_col = E4(nc.sbuf_tensor("csel_col", [64, 1], BF16))
            a_i32 = E4(nc.sbuf_tensor("a_i32", [64, 512], I32))
            a_f32 = E4(nc.sbuf_tensor("a_f32", [64, 512], F32))
            ohc = E4(nc.sbuf_tensor("ohc", [64, 512], BF16))
            rrow = E4(nc.sbuf_tensor("rrow", [1, 512], BF16))
            brep = E4(nc.sbuf_tensor("brep", [128, D], F32))
            p_p4 = E4(nc.psum_tensor("p_p4", [128, 512], F32))

            s_cnt2 = k.sem("s_cnt2")
            s_as = k.sem("s_as")
            s_br = k.sem("s_br")
            k.w("sync", cc_s)
            k.dma("sync", cnt_col[:, :], cnt_out.transpose([1, 0]), s_cnt2, slow=True)
            k.dma("sync", brep[:, :], bias_rep, s_br)
            k.w("vector", s_cnt2)
            k.run("vector", lambda e: e.tensor_scalar(
                out=csel_col[:, :], in0=cnt_col[:, :], scalar1=0.5, scalar2=None,
                op0=Op.is_gt, op1=Op.bypass))
            k.drain_inc("vector", dv)
            for fc in range(8):
                k.w("sync", dv)
                k.w("sync", pe)
                k.dma("sync", a_i32[:, :],
                      assign[:, fc * 512:(fc + 1) * 512].partition_broadcast(64).squeeze(1),
                      s_as)
                k.w("vector", s_as)
                k.run("vector", lambda e: e.tensor_copy(a_f32[:, :], a_i32[:, :]))
                k.drain("vector")
                k.run("vector", lambda e: e.tensor_scalar(
                    out=ohc[:, :], in0=a_f32[:, :], scalar1=c_iota[:, :], scalar2=None,
                    op0=Op.is_equal, op1=Op.bypass))
                k.drain_inc("vector", dv)
                k.w("tensor", dv)
                k.run("tensor", lambda e: e.matmul(
                    p_p4[0:1, :], csel_col[:, :], ohc[:, :], start=True, stop=True), inc=pe)
                k.w("scalar", pe)
                k.run("scalar", lambda e: e.copy(out=rrow[:, :], in_=p_p4[0:1, :]))
                k.drain_inc("scalar", ac)
                k.w("tensor", ac)
                k.run("tensor", lambda e: e.matmul(
                    p_p4[:, :], c_onesk1[:, :], rrow[:, :], start=True, stop=True), inc=pe)
                k.w("vector", pe)
                k.w("vector", s_br)
                k.run("vector", lambda e, fc=fc: e.tensor_copy(
                    c_rsel[:, fc * 512:(fc + 1) * 512], p_p4[:, :]))
                k.drain("vector")
                k.run("vector", lambda e, fc=fc: e.tensor_tensor(
                    out=c_brs[:, fc * 512:(fc + 1) * 512],
                    in0=brep[:, fc * 512:(fc + 1) * 512],
                    in1=c_rsel[:, fc * 512:(fc + 1) * 512], op=Op.mult))
                k.drain_inc("vector", dv)
            k.barrier("p4_end")

        with ExitStack() as p5:
            E5 = p5.enter_context
            h2_half = E5(nc.sbuf_tensor("h2_half", [128, 32 * 1024], BF16))
            wf_sl = [E5(nc.sbuf_tensor(f"wf_sl{i}", [128, 32 * 512], BF16)) for i in range(2)]
            xr5 = [E5(nc.sbuf_tensor(f"xr5_{i}", [128, 512], F32)) for i in range(4)]
            f1b = [E5(nc.sbuf_tensor(f"f1b_{i}", [128, 512], F32)) for i in range(2)]
            f2b = [E5(nc.sbuf_tensor(f"f2b_{i}", [128, 512], F32)) for i in range(2)]
            ob5 = [E5(nc.sbuf_tensor(f"ob5_{i}", [128, 512], F32)) for i in range(2)]
            p_f = [E5(nc.psum_tensor(f"p_f{i}", [128, 512], F32)) for i in range(2)]

            s_h2 = k.sem("s_h2")
            s_wf = [k.sem(f"s_wf{i}") for i in range(2)]
            s_x5 = [k.sem(f"s_x5_{i}") for i in range(4)]
            out_w = [k.sem(f"out_w{i}") for i in range(2)]

            xr5_free = [0, 0, 0, 0]
            pf_free = [0, 0]
            wf_free = [0, 0]

            for half in range(2):
                if half == 1:
                    k.w("sync", pe)
                k.w("sync", h2_w)
                _cdma(k, "sync", h2_half[:, :],
                      h2_sp[:, half * 1024:(half + 1) * 1024], 32, s_h2)
                k.w("tensor", s_h2)
                for fo in range(8):
                    sl = fo % 2
                    k.w("gpsimd", pe, wf_free[sl])
                    _cdma(k, "gpsimd", wf_sl[sl][:, :],
                          wf[:, fo * 512:(fo + 1) * 512], 32, s_wf[sl])
                    k.w("tensor", s_wf[sl])
                    for tl in range(8):
                        tg = half * 8 + tl
                        xsl = (fo * 8 + tl) % 4
                        pfi = tl % 2
                        pp = p_f[pfi]
                        k.w("sync", dv, xr5_free[xsl])
                        k.dma("sync", xr5[xsl][:, :],
                              xres_sp[tg * 128:(tg + 1) * 128, fo * 512:(fo + 1) * 512],
                              s_x5[xsl])
                        k.w("tensor", dv, pf_free[pfi])
                        for fi in range(32):
                            k.run("tensor", lambda e, fi=fi, tl=tl, pp=pp, sl=sl: e.matmul(
                                pp[:, :],
                                h2_half[:, fi * 1024 + tl * 128:fi * 1024 + (tl + 1) * 128],
                                wf_sl[sl][:, fi * 512:(fi + 1) * 512],
                                start=(fi == 0), stop=(fi == 31)),
                                inc=pe if fi == 31 else None)
                        k.w("vector", pe)
                        k.w("vector", s_x5[xsl])
                        k.w("vector", out_w[tl % 2])
                        k.run("vector", lambda e, pp=pp, tg=tg, fo=fo, tl=tl: e.scalar_tensor_tensor(
                            out=f1b[tl % 2][:, :], in0=pp[:, :], scalar=c_qsel[:, tg:tg + 1],
                            in1=c_rsel[:, fo * 512:(fo + 1) * 512], op0=Op.mult, op1=Op.mult))
                        k.run("vector", lambda e, tg=tg, fo=fo, tl=tl, xsl=xsl: e.scalar_tensor_tensor(
                            out=f2b[tl % 2][:, :], in0=c_brs[:, fo * 512:(fo + 1) * 512],
                            scalar=c_qsel[:, tg:tg + 1], in1=xr5[xsl][:, :],
                            op0=Op.mult, op1=Op.add))
                        k.drain("vector")
                        k.run("vector", lambda e, tl=tl: e.tensor_tensor(
                            out=ob5[tl % 2][:, :], in0=f1b[tl % 2][:, :], in1=f2b[tl % 2][:, :],
                            op=Op.add))
                        k.drain_inc("vector", dv)
                        xr5_free[xsl] = k.val(dv)
                        pf_free[pfi] = k.val(dv)
                        k.w("scalar", dv)
                        k.dma("scalar",
                              out[tg * 128:(tg + 1) * 128, fo * 512:(fo + 1) * 512],
                              ob5[tl % 2][:, :], out_w[tl % 2])
                    wf_free[sl] = k.val(pe)
            k.barrier("end")
            k.w("sync", out_w[0])
            k.w("sync", out_w[1])

        block = E(nc.Block())
        k.finalize(block)
    return nc


def kernel(L0_attn_x, L0_attn_Wq, L0_attn_Wk, L0_attn_Wv, L0_attn_Wo,
           L0_ffn_weight, L0_ffn_bias, L0_ffn_centroids, L0_ffn_assignments):
    x = np.ascontiguousarray(np.asarray(L0_attn_x, dtype=np.float32))
    Wq = np.asarray(L0_attn_Wq, dtype=np.float32)
    Wk = np.asarray(L0_attn_Wk, dtype=np.float32)
    Wv = np.asarray(L0_attn_Wv, dtype=np.float32)
    Wo = np.asarray(L0_attn_Wo, dtype=np.float32)
    Wf = np.asarray(L0_ffn_weight, dtype=np.float32)
    bias = np.asarray(L0_ffn_bias, dtype=np.float32)
    centroids = np.asarray(L0_ffn_centroids, dtype=np.float32)
    assignments = np.asarray(L0_ffn_assignments, dtype=np.int32)

    cosT, sinT = _rope_tables()
    idenM = np.eye(128, dtype=ml_dtypes.bfloat16)
    onesM = np.ones((128, 128), dtype=ml_dtypes.bfloat16)
    onescolM = np.ones((128, 1), dtype=ml_dtypes.bfloat16)
    onesk1M = np.ones((1, 128), dtype=ml_dtypes.bfloat16)
    iotaM = np.arange(64, dtype=np.float32).reshape(64, 1)
    kp = np.arange(128)[:, None]
    qf = np.arange(512)[None, :]
    masksM = np.ascontiguousarray(
        np.concatenate([(kp + 128 * i <= qf) for i in range(4)], axis=1).astype(ml_dtypes.bfloat16))
    bias_repM = np.ascontiguousarray(np.broadcast_to(bias[None, :], (128, D)))
    centM = np.ascontiguousarray(centroids.T)
    assignM = np.ascontiguousarray(assignments.reshape(1, D))

    nc = build_nc(dumps=tuple(DEBUG_DUMPS), upto=UPTO)
    in_maps = []
    for c in range(8):
        b, r = divmod(c, 2)
        in_maps.append(dict(
            x_seq=x[b],
            x_own=np.ascontiguousarray(x[b, r * TOWN:(r + 1) * TOWN, :]),
            wq=np.ascontiguousarray(Wq[r * DQ:(r + 1) * DQ, :].T),
            wk=np.ascontiguousarray(Wk[r * DKV:(r + 1) * DKV, :].T),
            wv=np.ascontiguousarray(Wv[r * DKV:(r + 1) * DKV, :].T),
            wo=np.ascontiguousarray(Wo[:, r * DQ:(r + 1) * DQ].T),
            wf=np.ascontiguousarray(Wf.T),
            bias_rep=bias_repM, cent=centM, assign=assignM,
            cosT=cosT, sinT=sinT, iden=idenM, ones=onesM, onescol=onescolM,
            onesk1=onesk1M, iota64=iotaM, masks=masksM,
            epsv=np.full((128, 1), EPS, dtype=np.float32),
        ))
    res = run_bass_kernel_spmd(nc, in_maps, core_ids=list(range(8)))
    globals()["_LAST_RESULTS"] = res
    outp = np.empty((4, T, D), dtype=np.float32)
    for c in range(8):
        b, r = divmod(c, 2)
        outp[b, r * TOWN:(r + 1) * TOWN, :] = res.results[c]["out"]
    return outp


# version 13
# speedup vs baseline: 1.1334x; 1.0152x over previous
import math
from contextlib import ExitStack

import numpy as np
import ml_dtypes

import concourse.bass as bass
import concourse.mybir as mybir
from concourse.bass_utils import run_bass_kernel_spmd
from concourse.alu_op_type import AluOpType

F32 = mybir.dt.float32
BF16 = mybir.dt.bfloat16
I32 = mybir.dt.int32
AF = mybir.ActivationFunctionType
AX = mybir.AxisListType
Op = AluOpType

D = 4096
T = 4096
HL = 16
KVL = 4
HD = 128
DQ = HL * HD
DKV = KVL * HD
TC = 1024
NBLK = T // TC
NJ = T // 512
SCALE = 1.0 / math.sqrt(HD)
EPS = 1e-5
TEMP_INV = 10.0
THRESH = 0.01
TOWN = 2048

DEBUG_DUMPS = []
UPTO = 99


class KB:
    ENGS = ("tensor", "vector", "scalar", "gpsimd", "sync")

    def __init__(self, nc, stack):
        self.nc = nc
        self.stack = stack
        self.streams = {e: [] for e in self.ENGS}
        self.sems = {}

    def sem(self, name):
        if name not in self.sems:
            h = self.stack.enter_context(self.nc.semaphore(name))
            self.sems[name] = [h, 0]
        return name

    def val(self, name):
        return self.sems[name][1]

    def em(self, eng, fn):
        self.streams[eng].append(fn)

    def w(self, eng, sname, v=None):
        if v is None:
            v = self.sems[sname][1]
        if v <= 0:
            return
        h = self.sems[sname][0]
        self.em(eng, lambda e, h=h, v=v: e.wait_ge(h, v))

    def inc(self, sname, n=1):
        s = self.sems[sname]
        s[1] += n
        return s[1]

    def run(self, eng, fn, inc=None, inc_n=1):
        if inc is not None:
            h = self.sems[inc][0]
            self.inc(inc, inc_n)
            self.em(eng, lambda e, fn=fn, h=h, n=inc_n: fn(e).then_inc(h, n))
        else:
            self.em(eng, lambda e, fn=fn: fn(e))

    def drain_inc(self, eng, sname, n=1):
        h = self.sems[sname][0]
        self.inc(sname, n)
        self.em(eng, lambda e, h=h, n=n: e.drain().then_inc(h, n))

    def drain(self, eng):
        self.em(eng, lambda e: e.drain())

    def dma(self, qeng, out_ap, in_ap, sname, slow=False):
        h = self.sems[sname][0]
        self.inc(sname, 16)
        if slow:
            self.em(qeng, lambda e, o=out_ap, i=in_ap, h=h: e.dma_start(
                out=o, in_=i, allow_slow_non_contiguous=True).then_inc(h, 16))
        else:
            self.em(qeng, lambda e, o=out_ap, i=in_ap, h=h: e.dma_start(
                out=o, in_=i).then_inc(h, 16))

    def barrier(self, tag):
        for eng in ("tensor", "vector", "scalar", "gpsimd"):
            self.drain_inc(eng, self.sem("bar_" + eng))
        self.run("sync", lambda e: e.nop(), inc=self.sem("bar_sync"))
        snapshot = [(n, s[1]) for n, s in self.sems.items()]
        for eng in self.ENGS:
            for n, v in snapshot:
                if v > 0:
                    h = self.sems[n][0]
                    self.em(eng, lambda e, h=h, v=v: e.wait_ge(h, v))

    def finalize(self, block):
        for eng in self.ENGS:
            fns = self.streams[eng]
            dec = getattr(block, eng)

            def section(e, fns=fns):
                for f in fns:
                    f(e)
            dec(section)


def _rope_tables():
    base = 500000.0
    inv_freq = 1.0 / base ** (np.arange(0, HD, 2, dtype=np.float32) / HD)
    factor, low_f, high_f, old_ctx = 32.0, 1.0, 4.0, 8192
    low_wl, high_wl = old_ctx / low_f, old_ctx / high_f
    wavelen = 2 * math.pi / inv_freq
    inv_llama = np.where(wavelen > low_wl, inv_freq / factor, inv_freq)
    smooth = (old_ctx / wavelen - low_f) / (high_f - low_f)
    smoothed = (1 - smooth) * inv_llama / factor + smooth * inv_llama
    is_med = ~(wavelen < high_wl) & ~(wavelen > low_wl)
    inv = np.where(is_med, smoothed, inv_llama).astype(np.float32)
    pos = np.arange(T, dtype=np.float32)
    freqs = pos[:, None] * inv[None, :]
    emb = np.concatenate([freqs, freqs], axis=-1)
    cosT = np.ascontiguousarray(np.cos(emb).astype(np.float32).T)
    sinT = np.ascontiguousarray(np.sin(emb).astype(np.float32).T)
    return cosT, sinT



def _cdma(k, qeng, sbuf_ap, dram2d, C, sem):
    in3 = dram2d.rearrange("(c p) n -> p c n", p=128)
    n = in3.shape[2]
    out3 = sbuf_ap.rearrange("p (c n) -> p c n", c=C)
    k.dma(qeng, out3, in3, sem)


def _cdma_out(k, qeng, dram2d, sbuf_ap, C, sem):
    out3 = dram2d.rearrange("(c p) n -> p c n", p=128)
    in3 = sbuf_ap.rearrange("p (c n) -> p c n", c=C)
    k.dma(qeng, out3, in3, sem)


def build_nc(dumps=(), upto=99):
    nc = bass.Bass("TRN2", target_bir_lowering=False, debug=False)

    def din(name, shape, dt):
        return nc.dram_tensor(name, shape, dt, kind="ExternalInput").ap()

    dump_list = []

    def dint(name, shape, dt):
        if name in ("wo_part", "rs_out", "cnt_in", "cnt_out"):
            t = nc.dram_tensor(name, shape, dt, kind="Internal").ap()
            if name in dumps:
                dump_list.append((name, t, shape, dt))
            return t
        kind = "ExternalOutput" if name in dumps else "Internal"
        return nc.dram_tensor(name, shape, dt, kind=kind).ap()

    x_seq = din("x_seq", [T, D], F32)
    x_own = din("x_own", [TOWN, D], F32)
    wq = din("wq", [D, DQ], F32)
    wk = din("wk", [D, DKV], F32)
    wv = din("wv", [D, DKV], F32)
    wo = din("wo", [DQ, D], F32)
    wf = din("wf", [D, D], F32)
    bias_rep = din("bias_rep", [128, D], F32)
    cent = din("cent", [D, 64], F32)
    assign = din("assign", [1, D], I32)
    cosT = din("cosT", [128, T], F32)
    sinT = din("sinT", [128, T], F32)
    iden = din("iden", [128, 128], BF16)
    ones = din("ones", [128, 128], BF16)
    onescol = din("onescol", [128, 1], BF16)
    onesk1 = din("onesk1", [1, 128], BF16)
    iota64 = din("iota64", [64, 1], F32)
    masks = din("masks", [128, 4 * 512], BF16)
    epsv = din("epsv", [128, 1], F32)

    cc_names = ("wo_part", "rs_out", "cnt_in", "cnt_out")
    q_sp = dint("q_sp", [DQ, T], BF16)
    k_sp = dint("k_sp", [DKV, T], BF16)
    v_sp = dint("v_sp", [T, DKV], BF16)
    wo_bf = dint("wo_bf", [DQ, D], BF16)
    wo_part = dint("wo_part", [8, 2 * TOWN, 512], BF16)
    rs_out = dint("rs_out", [8, TOWN, 512], BF16)
    xres_sp = dint("xres_sp", [TOWN, D], F32)
    h2_sp = dint("h2_sp", [D, TOWN], BF16)
    cnt_in = dint("cnt_in", [1, 64], F32)
    cnt_out = dint("cnt_out", [1, 64], F32)
    out = nc.dram_tensor("out", [TOWN, D], F32, kind="ExternalOutput").ap()

    with ExitStack() as stack:
        E = stack.enter_context
        k = KB(nc, stack)
        c_iden = E(nc.sbuf_tensor("c_iden", [128, 128], BF16))
        c_ones = E(nc.sbuf_tensor("c_ones", [128, 128], BF16))
        c_onescol = E(nc.sbuf_tensor("c_onescol", [128, 1], BF16))
        c_onesk1 = E(nc.sbuf_tensor("c_onesk1", [1, 128], BF16))
        c_iota = E(nc.sbuf_tensor("c_iota", [64, 1], F32))
        c_masks = E(nc.sbuf_tensor("c_masks", [128, 4 * 512], BF16))
        c_cent = E(nc.sbuf_tensor("c_cent", [128, 32 * 64], BF16))
        c_qsel = E(nc.sbuf_tensor("c_qsel", [128, 16], F32))
        c_eps = E(nc.sbuf_tensor("c_eps", [128, 1], F32))

        ld_init = k.sem("ld_init")
        k.dma("sync", c_iden[:, :], iden, ld_init)
        k.dma("sync", c_ones[:, :], ones, ld_init)
        k.dma("sync", c_onescol[:, :], onescol, ld_init)
        k.dma("sync", c_onesk1[:, :], onesk1, ld_init)
        k.dma("sync", c_iota[:, :], iota64, ld_init)
        k.dma("sync", c_masks[:, :], masks, ld_init)
        k.dma("sync", c_eps[:, :], epsv, ld_init)
        _cdma(k, "gpsimd", c_cent[:, :], cent, 32, ld_init)
        for e in ("tensor", "vector", "scalar"):
            k.w(e, ld_init)

        dv = k.sem("dv")
        ac = k.sem("ac")
        pe = k.sem("pe")
        cc_s = k.sem("cc_s")

        def zero_out_and_finish():
            zbuf = E(nc.sbuf_tensor("zbuf", [128, D], F32))
            zs = k.sem("z_w")
            k.run("vector", lambda e: e.memset(zbuf[:, :], 0.0))
            k.drain_inc("vector", dv)
            k.w("sync", dv)
            for ti in range(16):
                k.dma("sync", out[ti * 128:(ti + 1) * 128, :], zbuf[:, :], zs)
            k.barrier("zend")
            block = E(nc.Block())
            k.finalize(block)
            return nc

        with ExitStack() as p1:
            E1 = p1.enter_context
            h_fm = E1(nc.sbuf_tensor("h_fm", [128, 32 * TC], BF16))
            x_in = [E1(nc.sbuf_tensor(f"x_in{i}", [128, D], F32)) for i in range(2)]
            h_tm = [E1(nc.sbuf_tensor(f"h_tm{i}", [128, D], BF16)) for i in range(2)]
            wq_sl = [E1(nc.sbuf_tensor(f"wq_sl{i}", [128, 32 * 256], BF16)) for i in range(2)]
            wv_sl = E1(nc.sbuf_tensor("wv_sl", [128, 32 * 512], BF16))
            cs_sl = [E1(nc.sbuf_tensor(f"cs_sl{i}", [128, 2 * 512], F32)) for i in range(2)]
            r1 = E1(nc.sbuf_tensor("r1", [128, 512], F32))
            rAB = E1(nc.sbuf_tensor("rAB", [128, 512], F32))
            qko = [E1(nc.sbuf_tensor(f"qko{i}", [128, 512], BF16)) for i in range(2)]
            vst = [E1(nc.sbuf_tensor(f"vst{i}", [128, 512], BF16)) for i in range(2)]
            statb = [E1(nc.sbuf_tensor(f"statb{i}", [128, 8], F32)) for i in range(2)]
            p_a = E1(nc.psum_tensor("p_a", [128, 512], F32))
            p_b = E1(nc.psum_tensor("p_b", [128, 512], F32))
            p_x = E1(nc.psum_tensor("p_x", [128, 512], BF16))
            p_mm = [p_a, p_b]

            s_x = [k.sem(f"s_x{i}") for i in range(2)]
            s_wq = [k.sem(f"s_wq{i}") for i in range(2)]
            s_wv = k.sem("s_wv")
            s_cs = [k.sem(f"s_cs{i}") for i in range(2)]
            stqk = [k.sem(f"stqk{i}") for i in range(2)]
            v_wsem = [k.sem(f"v_w{i}") for i in range(2)]

            xin_free = [(0, 0), (0, 0)]
            htm_free = [0, 0]
            pmm_free = [0, 0]
            wq_free = [0, 0]
            evac_cnt = [0]

            def ln_tile(blk, ti):
                t = blk * 8 + ti
                sl = t % 2
                dvw, acw = xin_free[sl]
                k.w("sync", dv, dvw)
                k.w("sync", ac, acw)
                k.dma("sync", x_in[sl][:, :], x_seq[t * 128:(t + 1) * 128, :], s_x[sl])
                k.w("vector", s_x[sl])
                k.w("scalar", s_x[sl])
                k.w("scalar", pe, htm_free[sl])
                k.run("vector", lambda e, sl=sl: e.reduce_sum(
                    out=statb[sl][:, 0:1], in_=x_in[sl][:, :], axis=AX.X))
                k.run("scalar", lambda e, sl=sl: e.activation(
                    out=h_tm[sl][:, :], in_=x_in[sl][:, :], func=AF.Square,
                    accum_out=statb[sl][:, 2:3]))
                k.drain_inc("scalar", ac)
                k.w("vector", ac)
                k.drain("vector")
                k.run("vector", lambda e: e.tensor_scalar(
                    out=statb[sl][:, 1:2], in0=statb[sl][:, 0:1], scalar1=1.0 / D, scalar2=None,
                    op0=Op.mult, op1=Op.bypass))
                k.drain("vector")
                k.run("vector", lambda e: e.tensor_tensor(
                    out=statb[sl][:, 3:4], in0=statb[sl][:, 1:2], in1=statb[sl][:, 1:2], op=Op.mult))
                k.drain("vector")
                k.run("vector", lambda e: e.tensor_scalar(
                    out=statb[sl][:, 4:5], in0=statb[sl][:, 2:3], scalar1=1.0 / D, scalar2=statb[sl][:, 3:4],
                    op0=Op.mult, op1=Op.subtract))
                k.drain_inc("vector", dv)
                k.w("scalar", dv)
                k.run("scalar", lambda e: e.activation(
                    out=statb[sl][:, 5:6], in_=statb[sl][:, 4:5], func=AF.Ln, bias=c_eps[:, :]))
                k.run("scalar", lambda e: e.activation(
                    out=statb[sl][:, 6:7], in_=statb[sl][:, 5:6], func=AF.Exp, scale=-0.5))
                k.drain_inc("scalar", ac)
                k.w("vector", ac)
                k.run("vector", lambda e, sl=sl: e.tensor_scalar(
                    out=h_tm[sl][:, :], in0=x_in[sl][:, :],
                    scalar1=statb[sl][:, 1:2], scalar2=statb[sl][:, 6:7],
                    op0=Op.subtract, op1=Op.mult))
                k.drain_inc("vector", dv)
                xin_free[sl] = (k.val(dv), k.val(ac))
                k.w("tensor", dv)
                for grp in range(8):
                    for u in range(4):
                        ch = grp * 4 + u
                        k.run("tensor", lambda e, ch=ch, u=u, sl=sl: e.transpose(
                            p_x[:, u * 128:(u + 1) * 128],
                            h_tm[sl][:, ch * 128:(ch + 1) * 128], c_iden[:, :]),
                            inc=pe if u == 3 else None)
                    k.w("vector", pe)
                    for u in range(4):
                        ch = grp * 4 + u
                        k.run("vector", lambda e, ch=ch, u=u, ti=ti: e.tensor_copy(
                            h_fm[:, ch * TC + ti * 128:ch * TC + (ti + 1) * 128],
                            p_x[:, u * 128:(u + 1) * 128]))
                    k.drain_inc("vector", dv)
                    if grp < 7:
                        k.w("tensor", dv)
                htm_free[sl] = k.val(pe)

            def proj_and_rope(slab, slab_cols, fout_off, t512_abs, pp_i, dst, row_off, st_sl):
                pp = p_mm[pp_i]
                blk_t512 = t512_abs % 2
                k.w("tensor", dv, pmm_free[pp_i])
                for fi in range(32):
                    k.run("tensor", lambda e, fi=fi, pp=pp, slab=slab, slab_cols=slab_cols,
                          fout_off=fout_off, blk_t512=blk_t512: e.matmul(
                        pp[:, :],
                        slab[:, fi * slab_cols + fout_off:fi * slab_cols + fout_off + 128],
                        h_fm[:, fi * TC + blk_t512 * 512:fi * TC + blk_t512 * 512 + 512],
                        start=(fi == 0), stop=(fi == 31)),
                        inc=pe if fi == 31 else None)
                csl = cs_sl[t512_abs % 2]
                ob = qko[st_sl]
                k.w("vector", pe)
                k.w("vector", s_cs[t512_abs % 2])
                k.w("vector", stqk[st_sl])
                k.run("vector", lambda e, pp=pp, csl=csl: e.tensor_tensor(
                    out=r1[:, :], in0=pp[:, :], in1=csl[:, 0:512], op=Op.mult))
                k.run("vector", lambda e, pp=pp, csl=csl: e.tensor_tensor(
                    out=rAB[0:64, :], in0=pp[64:128, :], in1=csl[0:64, 512:1024], op=Op.mult))
                k.run("vector", lambda e, pp=pp, csl=csl: e.tensor_tensor(
                    out=rAB[64:128, :], in0=pp[0:64, :], in1=csl[64:128, 512:1024], op=Op.mult))
                k.drain("vector")
                k.run("vector", lambda e, ob=ob: e.tensor_tensor(
                    out=ob[0:64, :], in0=r1[0:64, :], in1=rAB[0:64, :], op=Op.subtract))
                k.run("vector", lambda e, ob=ob: e.tensor_tensor(
                    out=ob[64:128, :], in0=r1[64:128, :], in1=rAB[64:128, :], op=Op.add))
                k.drain_inc("vector", dv)
                pmm_free[pp_i] = k.val(dv)
                k.w("scalar", dv)
                k.dma("scalar", dst[row_off:row_off + 128,
                                    t512_abs * 512:(t512_abs + 1) * 512], ob[:, :], stqk[st_sl])

            if upto >= 1:
                _cdma(k, "gpsimd", wv_sl[:, :], wv, 32, s_wv)
                for blk in range(NBLK):
                    for ti in range(8):
                        ln_tile(blk, ti)
                    for t5 in range(2):
                        t512_abs = blk * 2 + t5
                        sl = t512_abs % 2
                        if t512_abs >= 2:
                            k.w("sync", dv)
                        k.dma("sync", cs_sl[sl][:, 0:512],
                              cosT[:, t512_abs * 512:(t512_abs + 1) * 512], s_cs[sl])
                        k.dma("sync", cs_sl[sl][:, 512:1024],
                              sinT[:, t512_abs * 512:(t512_abs + 1) * 512], s_cs[sl])
                    for sq in range(10):
                        sl = sq % 2
                        wsrc = wq if sq < 8 else wk
                        woff = sq * 256 if sq < 8 else (sq - 8) * 256
                        k.w("gpsimd", pe, wq_free[sl])
                        _cdma(k, "gpsimd", wq_sl[sl][:, :],
                              wsrc[:, woff:woff + 256], 32, s_wq[sl])
                        k.w("tensor", s_wq[sl])
                        for hh in range(2):
                            if sq < 8:
                                dst, row = q_sp, (sq * 2 + hh) * 128
                            else:
                                dst, row = k_sp, ((sq - 8) * 2 + hh) * 128
                            for t5 in range(2):
                                t512_abs = blk * 2 + t5
                                i = evac_cnt[0]
                                proj_and_rope(wq_sl[sl], 256, hh * 128, t512_abs,
                                              i % 2, dst, row, i % 2)
                                evac_cnt[0] += 1
                        wq_free[sl] = k.val(pe)
                    k.w("tensor", s_wv)
                    for ti in range(8):
                        t = blk * 8 + ti
                        ppi = ti % 2
                        pp = p_mm[ppi]
                        k.w("tensor", dv, pmm_free[ppi])
                        for fi in range(32):
                            k.run("tensor", lambda e, fi=fi, ti=ti, pp=pp: e.matmul(
                                pp[:, :],
                                h_fm[:, fi * TC + ti * 128:fi * TC + (ti + 1) * 128],
                                wv_sl[:, fi * 512:(fi + 1) * 512],
                                start=(fi == 0), stop=(fi == 31)),
                                inc=pe if fi == 31 else None)
                        k.w("vector", pe)
                        k.w("vector", v_wsem[ti % 2])
                        k.run("vector", lambda e, ti=ti, pp=pp: e.tensor_copy(
                            vst[ti % 2][:, :], pp[:, :]))
                        k.drain_inc("vector", dv)
                        pmm_free[ppi] = k.val(dv)
                        k.w("scalar", dv)
                        k.dma("scalar", v_sp[t * 128:(t + 1) * 128, :], vst[ti % 2][:, :],
                              v_wsem[ti % 2])
            k.barrier("p1_end")

        if upto < 2:
            return zero_out_and_finish()

        with ExitStack() as p2:
            E2 = p2.enter_context
            K_res = E2(nc.sbuf_tensor("K_res", [128, 4 * T], BF16))
            V_res = E2(nc.sbuf_tensor("V_res", [128, 32 * DKV], BF16))
            O_sb = E2(nc.sbuf_tensor("O_sb", [128, 64 * 512], BF16))
            q_t = [E2(nc.sbuf_tensor(f"q_t{i}", [128, 4 * 512], BF16)) for i in range(2)]
            att2 = [E2(nc.sbuf_tensor(f"att2_{i}", [128, 1024], BF16)) for i in range(2)]
            s_lnb = [E2(nc.sbuf_tensor(f"s_lnb{i}", [128, 512], F32)) for i in range(2)]
            s_rec = [E2(nc.sbuf_tensor(f"s_rec{i}", [128, 512], F32)) for i in range(2)]
            wo_sl = [E2(nc.sbuf_tensor(f"wo_sl{i}", [128, 16 * 512], BF16)) for i in range(2)]
            wo_st = [E2(nc.sbuf_tensor(f"wo_st{i}", [128, 512], BF16)) for i in range(2)]
            p_s2 = [E2(nc.psum_tensor(f"p_s2_{i}", [128, 1024], F32)) for i in range(2)]
            p_o = [E2(nc.psum_tensor(f"p_o{i}", [128, 512], F32)) for i in range(2)]
            p_d = [E2(nc.psum_tensor(f"p_d{i}", [128, 512], F32)) for i in range(2)]

            s_kv = k.sem("s_kv")
            s_q = [k.sem(f"s_q{i}") for i in range(2)]
            s_wo = [k.sem(f"s_wo{i}") for i in range(2)]
            wop_w = [k.sem(f"wop_w{i}") for i in range(2)]

            k.w("sync", "stqk0"); k.w("sync", "stqk1")
            k.w("sync", "v_w0"); k.w("sync", "v_w1")
            for g in range(KVL):
                k.dma("sync", K_res[:, g * T:(g + 1) * T], k_sp[g * 128:(g + 1) * 128, :], s_kv)
            _cdma(k, "sync", V_res[:, :], v_sp, 32, s_kv)
            k.w("tensor", s_kv)

            att_last = [0, 0]
            att_ready = [("ac", 0), ("ac", 0)]
            ps_last = [0, 0]
            po_free = [0, 0]
            pd_free = [0, 0]
            qt_free = [0, 0]
            osb_dv = {}
            wost_free = [0, 0]

            def attention_j(j):
                jj = j % 4
                nkt = 4 * j + 4
                npair = nkt // 2

                def load_group(g):
                    qsl = g % 2
                    k.w("sync", pe, qt_free[qsl])
                    _cdma(k, "sync", q_t[qsl][:, :],
                          q_sp[g * 512:(g + 1) * 512, j * 512:(j + 1) * 512], 4, s_q[qsl])

                load_group(0)
                load_group(1)
                for h in range(HL):
                    g = h // 4
                    qsl = g % 2
                    qcols = (h % 4) * 512
                    po = p_o[h % 2]
                    pd = p_d[h % 2]
                    if h % 4 == 0:
                        k.w("tensor", s_q[qsl])

                    def emit_pv_den(gp2, si2, po=po, pd=pd, g=g, nkt=nkt, h=h):
                        sem2, v2 = att_ready[si2]
                        k.w("tensor", sem2, v2)
                        for u in (0, 1):
                            kt = 2 * gp2 + u
                            if kt == 0:
                                k.w("tensor", dv, po_free[h % 2])
                                k.w("tensor", ac, pd_free[h % 2])
                            k.run("tensor", lambda e, kt=kt, u=u, si2=si2, po=po, g=g, nkt=nkt: e.matmul(
                                po[:, :],
                                V_res[:, kt * DKV + g * 128:kt * DKV + (g + 1) * 128],
                                att2[si2][:, u * 512:(u + 1) * 512],
                                start=(kt == 0), stop=(kt == nkt - 1)), inc=pe)
                            k.run("tensor", lambda e, u=u, kt=kt, si2=si2, pd=pd, nkt=nkt: e.matmul(
                                pd[:, :], c_ones[:, :],
                                att2[si2][:, u * 512:(u + 1) * 512],
                                start=(kt == 0), stop=(kt == nkt - 1)), inc=pe)
                        att_last[si2] = k.val(pe)

                    pend = None
                    for gp in range(npair):
                        si = gp % 2
                        k.w("tensor", ac, ps_last[si])
                        for u in (0, 1):
                            kt = 2 * gp + u
                            k.run("tensor", lambda e, kt=kt, u=u, si=si, g=g, qsl=qsl, qcols=qcols: e.matmul(
                                p_s2[si][:, u * 512:(u + 1) * 512],
                                K_res[:, g * T + kt * 128:g * T + (kt + 1) * 128],
                                q_t[qsl][:, qcols:qcols + 512], start=True, stop=True), inc=pe)
                        k.w("scalar", pe)
                        k.w("scalar", pe, att_last[si])
                        k.run("scalar", lambda e, si=si: e.activation(
                            out=att2[si][:, :], in_=p_s2[si][:, :], func=AF.Exp, scale=SCALE))
                        k.drain_inc("scalar", ac)
                        ps_last[si] = k.val(ac)
                        if gp >= 2 * j:
                            k.w("vector", ac)
                            for u in (0, 1):
                                mi = 2 * gp + u - 4 * j
                                k.run("vector", lambda e, si=si, u=u, mi=mi: e.tensor_tensor(
                                    out=att2[si][:, u * 512:(u + 1) * 512],
                                    in0=att2[si][:, u * 512:(u + 1) * 512],
                                    in1=c_masks[:, mi * 512:(mi + 1) * 512], op=Op.mult))
                            k.drain_inc("vector", dv)
                            att_ready[si] = (dv, k.val(dv))
                        else:
                            att_ready[si] = (ac, k.val(ac))
                        if pend is not None:
                            emit_pv_den(*pend)
                        pend = (gp, si)
                    emit_pv_den(*pend)
                    if h % 4 == 3:
                        qt_free[qsl] = k.val(pe)
                        if g + 2 < KVL:
                            load_group(g + 2)
                    k.w("scalar", pe)
                    k.run("scalar", lambda e, pd=pd, h=h: e.activation(
                        out=s_lnb[h % 2][:, :], in_=pd[:, :], func=AF.Ln))
                    k.run("scalar", lambda e, h=h: e.activation(
                        out=s_rec[h % 2][:, :], in_=s_lnb[h % 2][:, :], func=AF.Exp, scale=-1.0))
                    k.drain_inc("scalar", ac)
                    pd_free[h % 2] = k.val(ac)
                    k.w("vector", ac)
                    slot = (jj * 16 + h) * 512
                    k.run("vector", lambda e, po=po, h=h, slot=slot: e.tensor_tensor(
                        out=O_sb[:, slot:slot + 512], in0=po[:, :], in1=s_rec[h % 2][:, :],
                        op=Op.mult))
                    k.drain_inc("vector", dv)
                    po_free[h % 2] = k.val(dv)
                    osb_dv[(jj, h)] = k.val(dv)

            wosl_free = [0, 0]
            wop_mark = {}

            def wo_half(half):
                for fo in range(8):
                    sl = fo % 2
                    if half == 0:
                        k.w("gpsimd", pe, wosl_free[sl])
                        _cdma(k, "gpsimd", wo_sl[sl][:, :],
                              wo[:, fo * 512:(fo + 1) * 512], 16, s_wo[sl])
                    else:
                        if fo == 0:
                            k.w("sync", "woc_w")
                        k.w("sync", pe, wosl_free[sl])
                        _cdma(k, "sync", wo_sl[sl][:, :],
                              wo_bf[:, fo * 512:(fo + 1) * 512], 16, s_wo[sl])
                    k.w("tensor", s_wo[sl])
                    for tl in range(16):
                        jj = tl // 4
                        qq = (tl % 4) * 128
                        pwi = tl % 2
                        pp = p_o[pwi]
                        k.w("tensor", dv, osb_dv[(jj, HL - 1)])
                        k.w("tensor", dv, po_free[pwi])
                        k.w("tensor", dv, wost_free[pwi])
                        for a in range(16):
                            k.run("tensor", lambda e, a=a, jj=jj, qq=qq, pp=pp, sl=sl: e.matmul(
                                pp[:, :],
                                O_sb[:, (jj * 16 + a) * 512 + qq:(jj * 16 + a) * 512 + qq + 128],
                                wo_sl[sl][:, a * 512:(a + 1) * 512],
                                start=(a == 0), stop=(a == 15)), inc=pe if a == 15 else None)
                        k.w("vector", pe)
                        k.w("vector", wop_w[pwi])
                        k.run("vector", lambda e, pp=pp, pwi=pwi: e.tensor_copy(
                            wo_st[pwi][:, :], pp[:, :]))
                        k.drain_inc("vector", dv)
                        wost_free[pwi] = k.val(dv)
                        po_free[pwi] = k.val(dv)
                        k.w("scalar", dv)
                        k.dma("scalar",
                              wo_part[fo, half * TOWN + tl * 128:half * TOWN + (tl + 1) * 128, :],
                              wo_st[pwi][:, :], wop_w[pwi])
                    wosl_free[sl] = k.val(pe)
                    if half == 1:
                        k.w("gpsimd", wop_w[0])
                        k.w("gpsimd", wop_w[1])
                        k.run("gpsimd", lambda e, fo=fo: e.collective_compute(
                            "ReduceScatter", Op.add,
                            replica_groups=[[0, 1], [2, 3], [4, 5], [6, 7]],
                            ins=[wo_part[fo].opt()],
                            outs=[rs_out[fo].opt()]), inc=cc_s)

            for j in range(4):
                attention_j(j)
            wo_half(0)
            woc_w = k.sem("woc_w")
            for c4 in range(4):
                k.dma("gpsimd", wo_bf[c4 * 512:(c4 + 1) * 512, :],
                      wo[c4 * 512:(c4 + 1) * 512, :], woc_w)
            for j in range(4, 8):
                attention_j(j)
            wo_half(1)
            if dump_list:
                k.w("sync", cc_s)
                dsem = k.sem("dump_w")
                for name, t, shape, dt in dump_list:
                    if name in ("wo_part", "rs_out"):
                        d = nc.dram_tensor(name + "_dump", shape, dt, kind="ExternalOutput").ap()
                        k.dma("sync", d, t, dsem)
            k.barrier("p2_end")

        if upto < 3:
            return zero_out_and_finish()

        with ExitStack() as p3:
            E3 = p3.enter_context
            rs_t = [E3(nc.sbuf_tensor(f"rs_t{i}", [128, D], BF16)) for i in range(2)]
            xo_t = [E3(nc.sbuf_tensor(f"xo_t{i}", [128, D], F32)) for i in range(2)]
            xr_t = [E3(nc.sbuf_tensor(f"xr_t{i}", [128, D], F32)) for i in range(2)]
            h2_tm = [E3(nc.sbuf_tensor(f"h2_tm{i}", [128, D], BF16)) for i in range(2)]
            h2_st = E3(nc.sbuf_tensor("h2_st", [128, 32 * 512], BF16))
            stat2b = [E3(nc.sbuf_tensor(f"stat2b{i}", [128, 8], F32)) for i in range(2)]
            expd = E3(nc.sbuf_tensor("expd", [128, 64], F32))
            hits = E3(nc.sbuf_tensor("hits", [128, 64], BF16))
            cnt_sb = E3(nc.sbuf_tensor("cnt_sb", [1, 64], F32))
            p_x2 = E3(nc.psum_tensor("p_x2", [128, 512], BF16))
            p_dt = [E3(nc.psum_tensor(f"p_dt{i}", [128, 64], F32)) for i in range(2)]
            p_cnt = E3(nc.psum_tensor("p_cnt", [1, 64], F32))

            s_rs = [k.sem(f"s_rs{i}") for i in range(2)]
            s_xo = [k.sem(f"s_xo{i}") for i in range(2)]
            xres_w = [k.sem(f"xres_w{i}") for i in range(2)]
            h2_w = k.sem("h2_w")
            cntin_w = k.sem("cntin_w")

            xrt_free = [(0, 0), (0, 0)]
            pdt_free = [0, 0]
            hits_free = [0]
            htm2_free = [0, 0]

            k.w("sync", cc_s)
            for tl in range(16):
                sl = tl % 2
                dvw, acw = xrt_free[sl]
                k.w("sync", dv, dvw)
                k.w("sync", ac, acw)
                k.dma("sync", rs_t[sl][:, :].rearrange("p (c n) -> p c n", c=8),
                      rs_out[:, tl * 128:(tl + 1) * 128, :].transpose([1, 0, 2]), s_rs[sl])
                k.dma("scalar", xo_t[sl][:, :], x_own[tl * 128:(tl + 1) * 128, :], s_xo[sl])
                k.w("vector", s_rs[sl])
                k.w("vector", s_xo[sl])
                k.w("vector", xres_w[sl])
                k.run("vector", lambda e, sl=sl: e.tensor_tensor(
                    out=xr_t[sl][:, :], in0=rs_t[sl][:, :], in1=xo_t[sl][:, :], op=Op.add))
                k.drain_inc("vector", dv)
                k.w("scalar", dv)
                k.dma("scalar", xres_sp[tl * 128:(tl + 1) * 128, :], xr_t[sl][:, :], xres_w[sl])
                k.w("scalar", pe, htm2_free[sl])
                k.run("vector", lambda e, sl=sl: e.reduce_sum(
                    out=stat2b[sl][:, 0:1], in_=xr_t[sl][:, :], axis=AX.X))
                k.run("scalar", lambda e, sl=sl: e.activation(
                    out=h2_tm[sl][:, :], in_=xr_t[sl][:, :], func=AF.Square,
                    accum_out=stat2b[sl][:, 2:3]))
                k.drain_inc("scalar", ac)
                k.w("vector", ac)
                k.drain("vector")
                k.run("vector", lambda e, sl=sl: e.tensor_scalar(
                    out=stat2b[sl][:, 1:2], in0=stat2b[sl][:, 0:1], scalar1=1.0 / D, scalar2=None,
                    op0=Op.mult, op1=Op.bypass))
                k.drain("vector")
                k.run("vector", lambda e, sl=sl: e.tensor_tensor(
                    out=stat2b[sl][:, 3:4], in0=stat2b[sl][:, 1:2], in1=stat2b[sl][:, 1:2], op=Op.mult))
                k.drain("vector")
                k.run("vector", lambda e, sl=sl: e.tensor_scalar(
                    out=stat2b[sl][:, 4:5], in0=stat2b[sl][:, 2:3], scalar1=1.0 / D, scalar2=stat2b[sl][:, 3:4],
                    op0=Op.mult, op1=Op.subtract))
                k.drain_inc("vector", dv)
                k.w("scalar", dv)
                k.run("scalar", lambda e, sl=sl: e.activation(
                    out=stat2b[sl][:, 5:6], in_=stat2b[sl][:, 4:5], func=AF.Ln, bias=c_eps[:, :]))
                k.run("scalar", lambda e, sl=sl: e.activation(
                    out=stat2b[sl][:, 6:7], in_=stat2b[sl][:, 5:6], func=AF.Exp, scale=-0.5))
                k.drain_inc("scalar", ac)
                k.w("vector", ac)
                k.run("vector", lambda e, sl=sl: e.tensor_scalar(
                    out=h2_tm[sl][:, :], in0=xr_t[sl][:, :],
                    scalar1=stat2b[sl][:, 1:2], scalar2=stat2b[sl][:, 6:7],
                    op0=Op.subtract, op1=Op.mult))
                k.drain_inc("vector", dv)
                xrt_free[sl] = (k.val(dv), k.val(ac))
                k.w("tensor", dv)
                if tl % 4 == 0:
                    k.w("tensor", h2_w)
                for grp in range(8):
                    for u in range(4):
                        ch = grp * 4 + u
                        k.run("tensor", lambda e, ch=ch, u=u, sl=sl: e.transpose(
                            p_x2[:, u * 128:(u + 1) * 128],
                            h2_tm[sl][:, ch * 128:(ch + 1) * 128], c_iden[:, :]),
                            inc=pe if u == 3 else None)
                    k.w("vector", pe)
                    for u in range(4):
                        ch = grp * 4 + u
                        k.run("vector", lambda e, ch=ch, u=u, tl=tl: e.tensor_copy(
                            h2_st[:, ch * 512 + (tl % 4) * 128:ch * 512 + (tl % 4) * 128 + 128],
                            p_x2[:, u * 128:(u + 1) * 128]))
                    k.drain_inc("vector", dv)
                    if grp < 7:
                        k.w("tensor", dv)
                htm2_free[sl] = k.val(pe)
                pdt = p_dt[tl % 2]
                k.w("tensor", dv)
                k.w("tensor", ac, pdt_free[tl % 2])
                for fi in range(32):
                    k.run("tensor", lambda e, fi=fi, tl=tl, pdt=pdt: e.matmul(
                        pdt[:, :],
                        h2_st[:, fi * 512 + (tl % 4) * 128:fi * 512 + (tl % 4) * 128 + 128],
                        c_cent[:, fi * 64:(fi + 1) * 64],
                        start=(fi == 0), stop=(fi == 31)), inc=pe if fi == 31 else None)
                k.w("vector", pe)
                k.w("vector", pe, hits_free[0])
                k.run("vector", lambda e, pdt=pdt, sl=sl: e.reduce_max(
                    out=stat2b[sl][:, 7:8], in_=pdt[:, :], axis=AX.X, negate=True))
                k.drain("vector")
                k.run("vector", lambda e, sl=sl: e.tensor_scalar(
                    out=stat2b[sl][:, 7:8], in0=stat2b[sl][:, 7:8], scalar1=TEMP_INV, scalar2=None,
                    op0=Op.mult, op1=Op.bypass))
                k.drain_inc("vector", dv)
                k.w("scalar", dv)
                k.run("scalar", lambda e, pdt=pdt, sl=sl: e.activation(
                    out=expd[:, :], in_=pdt[:, :], func=AF.Exp, scale=TEMP_INV,
                    bias=stat2b[sl][:, 7:8], accum_out=stat2b[sl][:, 0:1]))
                k.drain_inc("scalar", ac)
                pdt_free[tl % 2] = k.val(ac)
                k.w("vector", ac)
                k.run("vector", lambda e, sl=sl: e.tensor_scalar(
                    out=stat2b[sl][:, 1:2], in0=stat2b[sl][:, 0:1], scalar1=THRESH, scalar2=None,
                    op0=Op.mult, op1=Op.bypass))
                k.drain("vector")
                k.run("vector", lambda e, sl=sl: e.tensor_scalar(
                    out=hits[:, :], in0=expd[:, :], scalar1=stat2b[sl][:, 1:2], scalar2=None,
                    op0=Op.is_gt, op1=Op.bypass))
                k.drain("vector")
                k.run("vector", lambda e, tl=tl: e.reduce_max(
                    out=c_qsel[:, tl:tl + 1], in_=hits[:, :], axis=AX.X))
                k.drain_inc("vector", dv)
                k.w("tensor", dv)
                k.run("tensor", lambda e, tl=tl: e.matmul(
                    p_cnt[:, :], c_onescol[:, :], hits[:, :],
                    start=(tl == 0), stop=(tl == 15)), inc=pe)
                hits_free[0] = k.val(pe)
                if tl % 4 == 3:
                    k.w("scalar", dv)
                    _cdma_out(k, "scalar",
                          h2_sp[:, (tl - 3) * 128:(tl + 1) * 128], h2_st[:, :], 32, h2_w)

            k.w("vector", pe)
            k.run("vector", lambda e: e.tensor_copy(cnt_sb[:, :], p_cnt[:, :]))
            k.drain_inc("vector", dv)
            k.w("gpsimd", dv)
            k.dma("gpsimd", cnt_in, cnt_sb[:, :], cntin_w)
            k.w("gpsimd", cntin_w)
            k.run("gpsimd", lambda e: e.collective_compute(
                "AllReduce", Op.add, replica_groups=[list(range(8))],
                ins=[cnt_in.opt()], outs=[cnt_out.opt()]), inc=cc_s)
            if dump_list:
                k.w("sync", cc_s)
                dsem2 = k.sem("dump_w2")
                for name, t, shape, dt in dump_list:
                    if name == "cnt_out":
                        d = nc.dram_tensor(name + "_dump", shape, dt, kind="ExternalOutput").ap()
                        k.dma("sync", d, t, dsem2)
            k.barrier("p3_end")

        c_rsel = E(nc.sbuf_tensor("c_rsel", [128, D], BF16))
        c_brs = E(nc.sbuf_tensor("c_brs", [128, D], F32))
        with ExitStack() as p4:
            E4 = p4.enter_context
            cnt_col = E4(nc.sbuf_tensor("cnt_col", [64, 1], F32))
            csel_col = E4(nc.sbuf_tensor("csel_col", [64, 1], BF16))
            a_i32 = E4(nc.sbuf_tensor("a_i32", [64, 512], I32))
            a_f32 = E4(nc.sbuf_tensor("a_f32", [64, 512], F32))
            ohc = E4(nc.sbuf_tensor("ohc", [64, 512], BF16))
            rrow = E4(nc.sbuf_tensor("rrow", [1, 512], BF16))
            brep = E4(nc.sbuf_tensor("brep", [128, D], F32))
            p_p4 = E4(nc.psum_tensor("p_p4", [128, 512], F32))

            s_cnt2 = k.sem("s_cnt2")
            s_as = k.sem("s_as")
            s_br = k.sem("s_br")
            k.w("sync", cc_s)
            k.dma("sync", cnt_col[:, :], cnt_out.transpose([1, 0]), s_cnt2, slow=True)
            k.dma("sync", brep[:, :], bias_rep, s_br)
            k.w("vector", s_cnt2)
            k.run("vector", lambda e: e.tensor_scalar(
                out=csel_col[:, :], in0=cnt_col[:, :], scalar1=0.5, scalar2=None,
                op0=Op.is_gt, op1=Op.bypass))
            k.drain_inc("vector", dv)
            for fc in range(8):
                k.w("sync", dv)
                k.w("sync", pe)
                k.dma("sync", a_i32[:, :],
                      assign[:, fc * 512:(fc + 1) * 512].partition_broadcast(64).squeeze(1),
                      s_as)
                k.w("vector", s_as)
                k.run("vector", lambda e: e.tensor_copy(a_f32[:, :], a_i32[:, :]))
                k.drain("vector")
                k.run("vector", lambda e: e.tensor_scalar(
                    out=ohc[:, :], in0=a_f32[:, :], scalar1=c_iota[:, :], scalar2=None,
                    op0=Op.is_equal, op1=Op.bypass))
                k.drain_inc("vector", dv)
                k.w("tensor", dv)
                k.run("tensor", lambda e: e.matmul(
                    p_p4[0:1, :], csel_col[:, :], ohc[:, :], start=True, stop=True), inc=pe)
                k.w("scalar", pe)
                k.run("scalar", lambda e: e.copy(out=rrow[:, :], in_=p_p4[0:1, :]))
                k.drain_inc("scalar", ac)
                k.w("tensor", ac)
                k.run("tensor", lambda e: e.matmul(
                    p_p4[:, :], c_onesk1[:, :], rrow[:, :], start=True, stop=True), inc=pe)
                k.w("vector", pe)
                k.w("vector", s_br)
                k.run("vector", lambda e, fc=fc: e.tensor_copy(
                    c_rsel[:, fc * 512:(fc + 1) * 512], p_p4[:, :]))
                k.drain("vector")
                k.run("vector", lambda e, fc=fc: e.tensor_tensor(
                    out=c_brs[:, fc * 512:(fc + 1) * 512],
                    in0=brep[:, fc * 512:(fc + 1) * 512],
                    in1=c_rsel[:, fc * 512:(fc + 1) * 512], op=Op.mult))
                k.drain_inc("vector", dv)
            k.barrier("p4_end")

        with ExitStack() as p5:
            E5 = p5.enter_context
            h2_half = E5(nc.sbuf_tensor("h2_half", [128, 32 * 1024], BF16))
            wf_sl = [E5(nc.sbuf_tensor(f"wf_sl{i}", [128, 32 * 512], BF16)) for i in range(2)]
            xr5 = [E5(nc.sbuf_tensor(f"xr5_{i}", [128, 512], F32)) for i in range(4)]
            f1b = [E5(nc.sbuf_tensor(f"f1b_{i}", [128, 512], F32)) for i in range(2)]
            f2b = [E5(nc.sbuf_tensor(f"f2b_{i}", [128, 512], F32)) for i in range(2)]
            ob5 = [E5(nc.sbuf_tensor(f"ob5_{i}", [128, 512], F32)) for i in range(2)]
            p_f = [E5(nc.psum_tensor(f"p_f{i}", [128, 512], F32)) for i in range(2)]

            s_h2 = k.sem("s_h2")
            s_wf = [k.sem(f"s_wf{i}") for i in range(2)]
            s_x5 = [k.sem(f"s_x5_{i}") for i in range(4)]
            out_w = [k.sem(f"out_w{i}") for i in range(2)]

            xr5_free = [0, 0, 0, 0]
            pf_free = [0, 0]
            wf_free = [0, 0]

            for half in range(2):
                if half == 1:
                    k.w("sync", pe)
                k.w("sync", h2_w)
                _cdma(k, "sync", h2_half[:, :],
                      h2_sp[:, half * 1024:(half + 1) * 1024], 32, s_h2)
                k.w("tensor", s_h2)
                for fo in range(8):
                    sl = fo % 2
                    k.w("gpsimd", pe, wf_free[sl])
                    _cdma(k, "gpsimd", wf_sl[sl][:, :],
                          wf[:, fo * 512:(fo + 1) * 512], 32, s_wf[sl])
                    k.w("tensor", s_wf[sl])
                    for tl in range(8):
                        tg = half * 8 + tl
                        xsl = (fo * 8 + tl) % 4
                        pfi = tl % 2
                        pp = p_f[pfi]
                        k.w("sync", dv, xr5_free[xsl])
                        k.dma("sync", xr5[xsl][:, :],
                              xres_sp[tg * 128:(tg + 1) * 128, fo * 512:(fo + 1) * 512],
                              s_x5[xsl])
                        k.w("tensor", dv, pf_free[pfi])
                        for fi in range(32):
                            k.run("tensor", lambda e, fi=fi, tl=tl, pp=pp, sl=sl: e.matmul(
                                pp[:, :],
                                h2_half[:, fi * 1024 + tl * 128:fi * 1024 + (tl + 1) * 128],
                                wf_sl[sl][:, fi * 512:(fi + 1) * 512],
                                start=(fi == 0), stop=(fi == 31)),
                                inc=pe if fi == 31 else None)
                        k.w("vector", pe)
                        k.w("vector", s_x5[xsl])
                        k.w("vector", out_w[tl % 2])
                        k.run("vector", lambda e, pp=pp, tg=tg, fo=fo, tl=tl: e.scalar_tensor_tensor(
                            out=f1b[tl % 2][:, :], in0=pp[:, :], scalar=c_qsel[:, tg:tg + 1],
                            in1=c_rsel[:, fo * 512:(fo + 1) * 512], op0=Op.mult, op1=Op.mult))
                        k.run("vector", lambda e, tg=tg, fo=fo, tl=tl, xsl=xsl: e.scalar_tensor_tensor(
                            out=f2b[tl % 2][:, :], in0=c_brs[:, fo * 512:(fo + 1) * 512],
                            scalar=c_qsel[:, tg:tg + 1], in1=xr5[xsl][:, :],
                            op0=Op.mult, op1=Op.add))
                        k.drain("vector")
                        k.run("vector", lambda e, tl=tl: e.tensor_tensor(
                            out=ob5[tl % 2][:, :], in0=f1b[tl % 2][:, :], in1=f2b[tl % 2][:, :],
                            op=Op.add))
                        k.drain_inc("vector", dv)
                        xr5_free[xsl] = k.val(dv)
                        pf_free[pfi] = k.val(dv)
                        k.w("scalar", dv)
                        k.dma("scalar",
                              out[tg * 128:(tg + 1) * 128, fo * 512:(fo + 1) * 512],
                              ob5[tl % 2][:, :], out_w[tl % 2])
                    wf_free[sl] = k.val(pe)
            k.barrier("end")
            k.w("sync", out_w[0])
            k.w("sync", out_w[1])

        block = E(nc.Block())
        k.finalize(block)
    return nc


def kernel(L0_attn_x, L0_attn_Wq, L0_attn_Wk, L0_attn_Wv, L0_attn_Wo,
           L0_ffn_weight, L0_ffn_bias, L0_ffn_centroids, L0_ffn_assignments):
    x = np.ascontiguousarray(np.asarray(L0_attn_x, dtype=np.float32))
    Wq = np.asarray(L0_attn_Wq, dtype=np.float32)
    Wk = np.asarray(L0_attn_Wk, dtype=np.float32)
    Wv = np.asarray(L0_attn_Wv, dtype=np.float32)
    Wo = np.asarray(L0_attn_Wo, dtype=np.float32)
    Wf = np.asarray(L0_ffn_weight, dtype=np.float32)
    bias = np.asarray(L0_ffn_bias, dtype=np.float32)
    centroids = np.asarray(L0_ffn_centroids, dtype=np.float32)
    assignments = np.asarray(L0_ffn_assignments, dtype=np.int32)

    cosT, sinT = _rope_tables()
    idenM = np.eye(128, dtype=ml_dtypes.bfloat16)
    onesM = np.ones((128, 128), dtype=ml_dtypes.bfloat16)
    onescolM = np.ones((128, 1), dtype=ml_dtypes.bfloat16)
    onesk1M = np.ones((1, 128), dtype=ml_dtypes.bfloat16)
    iotaM = np.arange(64, dtype=np.float32).reshape(64, 1)
    kp = np.arange(128)[:, None]
    qf = np.arange(512)[None, :]
    masksM = np.ascontiguousarray(
        np.concatenate([(kp + 128 * i <= qf) for i in range(4)], axis=1).astype(ml_dtypes.bfloat16))
    bias_repM = np.ascontiguousarray(np.broadcast_to(bias[None, :], (128, D)))
    centM = np.ascontiguousarray(centroids.T)
    assignM = np.ascontiguousarray(assignments.reshape(1, D))

    nc = build_nc(dumps=tuple(DEBUG_DUMPS), upto=UPTO)
    in_maps = []
    for c in range(8):
        b, r = divmod(c, 2)
        in_maps.append(dict(
            x_seq=x[b],
            x_own=np.ascontiguousarray(x[b, r * TOWN:(r + 1) * TOWN, :]),
            wq=np.ascontiguousarray(Wq[r * DQ:(r + 1) * DQ, :].T),
            wk=np.ascontiguousarray(Wk[r * DKV:(r + 1) * DKV, :].T),
            wv=np.ascontiguousarray(Wv[r * DKV:(r + 1) * DKV, :].T),
            wo=np.ascontiguousarray(Wo[:, r * DQ:(r + 1) * DQ].T),
            wf=np.ascontiguousarray(Wf.T),
            bias_rep=bias_repM, cent=centM, assign=assignM,
            cosT=cosT, sinT=sinT, iden=idenM, ones=onesM, onescol=onescolM,
            onesk1=onesk1M, iota64=iotaM, masks=masksM,
            epsv=np.full((128, 1), EPS, dtype=np.float32),
        ))
    res = run_bass_kernel_spmd(nc, in_maps, core_ids=list(range(8)))
    globals()["_LAST_RESULTS"] = res
    outp = np.empty((4, T, D), dtype=np.float32)
    for c in range(8):
        b, r = divmod(c, 2)
        outp[b, r * TOWN:(r + 1) * TOWN, :] = res.results[c]["out"]
    return outp
